# version 07
# baseline (speedup 1.0000x reference)
import os
import sys
import numpy as np
import ml_dtypes

try:
    import concourse.bass as bass
except ImportError:
    sys.path.insert(0, "/opt/trn_rl_repo")
    import concourse.bass as bass
import concourse.mybir as mybir
from concourse.bass_utils import run_bass_kernel_spmd

F32 = mybir.dt.float32
BF16 = mybir.dt.bfloat16
I32 = mybir.dt.int32
AF = mybir.ActivationFunctionType
ALU = mybir.AluOpType
AX = mybir.AxisListType

B, T, D = 16, 2048, 2048
NH, NKV, HD = 16, 4, 128
II = 8192
TT = 512
DC = D // 128
ICN = II // 128
HCN = DC
NCORE = 8
SER = {"c4", "c6"}
NEG = -30000.0
EPS_QK = 1e-6
EPS_LN = 1e-5


class SemC:
    def __init__(self, nc, name):
        self.h = nc.alloc_semaphore(name)
        self.n = 0
        self.name = name


class Buf:
    def __init__(self):
        self.need = {}

    def consumed(self, sem, cnt):
        if sem is not None and (sem not in self.need or self.need[sem] < cnt):
            self.need[sem] = cnt

    def take(self):
        w = list(self.need.items())
        self.need = {}
        return w


class Prog:
    def __init__(self, nc, paranoid=False):
        self.nc = nc
        self.ops = []
        self.labels = []
        self.phase = ""
        self.label_of = {}
        if paranoid is True:
            paranoid = {"sync", "scalar", "vector", "tensor", "gpsimd"}
        self.paranoid = paranoid or set()
        if self.paranoid:
            self.pp = {e: SemC(nc, "pp_" + e) for e in self.paranoid}

    def rec(self, eng, fn, *args, waits=(), inc=None, **kwargs):
        ws = [(s, c) for (s, c) in waits if s is not None]
        if inc is not None:
            inc[0].n += inc[1]
        self.ops.append((eng, fn, args, kwargs, ws, inc, self.phase))
        self.labels.append(self.phase + "/" + fn)

    def replay(self):
        hw = {}
        dep = mybir.DependencyInfo(mybir.DependencyClass.NoSync,
                                   mybir.DependencyReason.DATA)

        def run_engine(engobj, engname):
            prev = None
            prev_done = None
            rn = {}
            for oi, (eng, fn, args, kwargs, ws, inc, phase) in enumerate(self.ops):
                if eng != engname:
                    continue
                lbl = self.labels[oi]
                par = engname in self.paranoid or (engname + ":" + phase) in self.paranoid
                if par and fn != "wait_ge" and prev_done is not None:
                    engobj.wait_ge(prev_done[0].h, prev_done[1])
                for (sem, cnt) in ws:
                    key = (engname, sem.name)
                    if hw.get(key, -1) >= cnt:
                        continue
                    hw[key] = cnt
                    engobj.wait_ge(sem.h, cnt)
                ins = getattr(engobj, fn)(*args, **kwargs)
                self.label_of[ins.ins.name] = lbl
                if inc is not None:
                    ins.then_inc(inc[0].h, inc[1])
                if par and fn != "wait_ge":
                    if inc is not None:
                        rn[inc[0].name] = rn.get(inc[0].name, 0) + inc[1]
                        prev_done = (inc[0], rn[inc[0].name])
                    else:
                        amt = 16 if fn == "dma_start" else 1
                        pp = self.pp[engname]
                        pp.n += amt
                        ins.then_inc(pp.h, amt)
                        prev_done = (pp, pp.n)
                if prev is not None:
                    try:
                        ins.ins.add_dependency(prev, dep)
                    except Exception:
                        pass
                prev = ins.ins.name

        with self.nc.Block() as block:
            block.sync(lambda e: run_engine(e, "sync"))
            block.scalar(lambda e: run_engine(e, "scalar"))
            block.vector(lambda e: run_engine(e, "vector"))
            block.tensor(lambda e: run_engine(e, "tensor"))
            block.gpsimd(lambda e: run_engine(e, "gpsimd"))


def build_core_program(n_seq=2, n_tiles=T // TT, paranoid=False):
    nc = bass.Bass("TRN2", target_bir_lowering=False)
    NTOK = n_tiles * TT

    xT = nc.dram_tensor("xT", [n_seq, D, NTOK], F32, kind="ExternalInput").ap()
    wq_t = nc.dram_tensor("wq_t", [NH, 128, D], F32, kind="ExternalInput").ap()
    wk_t = nc.dram_tensor("wk_t", [NKV, 128, D], F32, kind="ExternalInput").ap()
    wv_t = nc.dram_tensor("wv_t", [DC, 128, 512], F32, kind="ExternalInput").ap()
    wo_t = nc.dram_tensor("wo_t", [DC, 128, D], F32, kind="ExternalInput").ap()
    g_t = nc.dram_tensor("g_t", [ICN, 128, D], I32, kind="ExternalInput").ap()
    u_t = nc.dram_tensor("u_t", [ICN, 128, D], I32, kind="ExternalInput").ap()
    d_t = nc.dram_tensor("d_t", [HCN, 128, II], I32, kind="ExternalInput").ap()
    qs_d = nc.dram_tensor("qs", [128, 1], F32, kind="ExternalInput").ap()
    ks_d = nc.dram_tensor("ks", [128, 1], F32, kind="ExternalInput").ap()
    gs_d = nc.dram_tensor("gs", [128, ICN], F32, kind="ExternalInput").ap()
    us_d = nc.dram_tensor("us", [128, ICN], F32, kind="ExternalInput").ap()
    ds_d = nc.dram_tensor("ds", [128, HCN], F32, kind="ExternalInput").ap()
    mka_d = nc.dram_tensor("mka", [128, 4 * 512], BF16, kind="ExternalInput").ap()
    mkb_d = nc.dram_tensor("mkb", [128, 4 * 512], BF16, kind="ExternalInput").ap()
    idn_d = nc.dram_tensor("idn", [128, 128], F32, kind="ExternalInput").ap()
    outT = nc.dram_tensor("outT", [n_seq, D, NTOK], F32, kind="ExternalOutput").ap()
    gs_s = nc.dram_tensor("gs_s", [ICN, 128, D], BF16, kind="Internal").ap()
    us_s = nc.dram_tensor("us_s", [ICN, 128, D], BF16, kind="Internal").ap()
    ds_s = nc.dram_tensor("ds_s", [HCN, 128, II], BF16, kind="Internal").ap()
    wq_s = nc.dram_tensor("wq_s", [NH, 128, D], BF16, kind="Internal").ap()
    wk_s = nc.dram_tensor("wk_s", [NKV, 128, D], BF16, kind="Internal").ap()
    wv_s = nc.dram_tensor("wv_s", [DC, 128, 512], BF16, kind="Internal").ap()
    wo_s = nc.dram_tensor("wo_s", [DC, 128, D], BF16, kind="Internal").ap()
    xs_s = nc.dram_tensor("xs_s", [n_seq, D, NTOK], BF16, kind="Internal").ap()

    A = nc.alloc_sbuf_tensor
    identf = A("identf", [128, 128], F32)
    onesbf = A("onesbf", [128, 1], BF16)
    ones1f = A("ones1f", [1, 128], F32)
    ones1bf = A("ones1bf", [1, 128], BF16)
    qs = A("qs_s", [128, 1], F32)
    ks = A("ks_s", [128, 1], F32)
    gsc = A("gsc", [128, ICN], F32)
    usc = A("usc", [128, ICN], F32)
    dsc = A("dsc", [128, HCN], F32)
    mka = A("mka_s", [128, 4 * 512], BF16)
    mkb = A("mkb_s", [128, 4 * 512], BF16)
    khat_t = A("khat", [128, NKV * NTOK], BF16)
    vsb_t = A("vsb", [128, (NTOK // 128) * 512], BF16)
    arena_t = A("arena", [128, ICN * 512], BF16)
    khat = khat_t.ap()
    vsb = vsb_t.ap()
    arena = arena_t.ap()
    x0bf = arena[:, 0:DC * 512]
    qhat = arena[:, DC * 512:2 * DC * 512]
    ocat = arena[:, 2 * DC * 512:3 * DC * 512]
    r1v = arena[:, 3 * DC * 512:4 * DC * 512]
    hact = arena
    x1bf = A("x1bf", [128, DC * 512], BF16).ap()
    r2v = A("r2v", [128, DC * 512], BF16).ap()
    pwin = r2v[:, 4096:4096 + 2 * 2048]
    sqb = A("sqb", [128, 2, 512], BF16).ap()
    wqb = A("wqb", [128, 2, DC * 128], BF16).ap()
    wvb = A("wvb", [128, 2, 512], BF16).ap()
    wgb = A("wgb", [128, 2, HCN * 128], BF16).ap()
    wub = A("wub", [128, 2, HCN * 128], BF16).ap()
    wdb = A("wdb", [128, 2, 16 * 128], BF16).ap()
    scrA = A("scrA", [128, 512], F32).ap()
    scrB = A("scrB", [128, 2, 1024], F32).ap()
    dump = A("dump", [128, 512], BF16).ap()
    ptb = A("ptb", [128, 2, 1024], BF16).ap()
    stt_t = A("stt", [128, 2, 8], F32).ap()
    sttl_t = A("sttl", [128, 2, 8], F32).ap()
    sttw_t = A("sttw", [128, 2, 4], F32).ap()
    sttlf_t = A("sttlf", [128, 2, 4], F32).ap()
    stt2_t = A("stt2", [128, 2, 4], F32).ap()
    st4 = A("st4", [4, 128], F32).ap()
    strow = A("strow", [1, 512], F32).ap()
    rows = A("rows", [1, 2, 512], F32).ap()
    rstdr = A("rstdr", [1, 2, 512], BF16).ap()
    x0res = A("x0res", [128, 4, 512], F32).ap()
    constt = A("constt", [128, 8], F32).ap()
    if os.environ.get("KSBUF"):
        print("sbuf remaining bytes/partition:", nc.sbuf_bytes_remaining)
    assert nc.sbuf_bytes_remaining >= 0, nc.sbuf_bytes_remaining

    P01 = [nc.alloc_psum_tensor(f"P{i}", [128, 512], F32) for i in range(2)]
    P23_t = nc.alloc_psum_tensor("P23", [128, 1024], F32)
    P47 = [nc.alloc_psum_tensor(f"P{i}", [128, 512], F32) for i in range(4, 8)]
    P23w = P23_t.ap()
    Pap = [P01[0].ap(), P01[1].ap(), P23w[:, 0:512], P23w[:, 512:1024]] + [p.ap() for p in P47]

    pr = Prog(nc, paranoid=paranoid)
    sems = {}
    for name in ("s_const", "s_stat", "g_x0", "e_pe", "e_dve", "e_act", "e_pool"):
        sems[name] = SemC(nc, name)
    for base in ("wq", "wv", "g", "u", "d", "out", "pwi", "pwo",
                 "wqg", "wvg"):
        for j in range(2):
            sems[base + str(j)] = SemC(nc, base + str(j))
    for j in range(4):
        sems["res" + str(j)] = SemC(nc, "res" + str(j))
    sems["s_x0"] = SemC(nc, "s_x0")
    S = type("S", (), sems)
    RS = {b: (sems[b + "0"], sems[b + "1"]) for b in
          ("wq", "wv", "g", "u", "d", "out", "pwi", "pwo", "wqg", "wvg")}
    RS["res"] = tuple(sems["res" + str(j)] for j in range(4))

    bufs = {
        "wqb": [Buf(), Buf()], "wvb": [Buf(), Buf()], "wgb": [Buf(), Buf()],
        "wub": [Buf(), Buf()], "wdb": [Buf(), Buf()], "x0res": [Buf(), Buf(), Buf(), Buf()],
        "sqb": [Buf(), Buf()], "scrB": [Buf(), Buf()],
        "ptb": [Buf(), Buf()], "pwin": [Buf(), Buf()],
        "arena": Buf(), "strow": Buf(), "scrA": Buf(), "st4": Buf(),
        "stt2": [Buf(), Buf()],
        "P": [Buf() for _ in range(8)],
    }

    def dve(fn, *a, waits=(), **k):
        pr.rec("vector", fn, *a, waits=waits, inc=(S.e_dve, 1), **k)
        return S.e_dve.n

    def act(fn, *a, waits=(), **k):
        pr.rec("scalar", fn, *a, waits=waits, inc=(S.e_act, 1), **k)
        return S.e_act.n

    def pool(fn, *a, waits=(), **k):
        pr.rec("gpsimd", fn, *a, waits=waits, inc=(S.e_pool, 1), **k)
        return S.e_pool.n

    def pe(fn, *a, waits=(), mark=True, **k):
        pr.rec("tensor", fn, *a, waits=waits, inc=(S.e_pe, 1) if mark else None, **k)
        return S.e_pe.n if mark else None

    def dma(queue, out, in_, sem, waits=()):
        pr.rec(queue, "dma_start", waits=waits, inc=(sem, 16), out=out, in_=in_)
        return sem.n

    for dst, src in ((qs, qs_d), (ks, ks_d), (gsc, gs_d), (usc, us_d), (dsc, ds_d),
                     (mka, mka_d), (mkb, mkb_d), (identf, idn_d)):
        dma("sync", dst.ap()[:], src, S.s_const)
    CONSTW = (S.s_const, S.s_const.n)
    dve("memset", onesbf.ap()[:], 1.0)
    dve("memset", ones1f.ap()[:], 1.0)
    dve("memset", ones1bf.ap()[:], 1.0)
    cvals = [EPS_QK, EPS_LN, -0.5 * float(np.log(128.0)), 1.0 / 128.0, -0.5, -1.0]
    for j, v in enumerate(cvals):
        dve("memset", constt[:, j:j + 1], v)
        nc.const_aps.register(F32, v, constt[:, j:j + 1])
    MEMSET = S.e_dve.n

    X1_DONE = [0]
    x0_info = [None]
    wq_i = [0]
    wv_i = [0]
    x0_i = [0]
    PW_TOTAL = 2 * ICN + HCN * 4

    def pw_wait(c):
        return (RS["pwo"][c % 2], 16 * (c // 2 + 1))

    conv_info = {}
    PW_DONE = [0, 0]

    def load_wq(kind, idx, first_tile):
        slot = wq_i[0] % 2
        wq_i[0] += 1
        w = bufs["wqb"][slot].take()
        if first_tile:
            srcap = {"k": wk_t, "q": wq_t, "o": wo_t}[kind][idx]
            cnt = dma("gpsimd", wqb[:, slot, :], srcap, RS["wqg"][slot], waits=w)
            return slot, (RS["wqg"][slot], cnt)
        else:
            srcap = {"k": wk_s, "q": wq_s, "o": wo_s}[kind][idx]
            w = w + [conv_info[(kind, idx)]]
            cnt = dma("sync", wqb[:, slot, :], srcap, RS["wq"][slot], waits=w)
        return slot, (RS["wq"][slot], cnt)

    def load_wv(dc, first_tile):
        slot = wv_i[0] % 2
        wv_i[0] += 1
        w = bufs["wvb"][slot].take()
        if first_tile:
            cnt = dma("gpsimd", wvb[:, slot, :], wv_t[dc], RS["wvg"][slot], waits=w)
            return slot, (RS["wvg"][slot], cnt)
        else:
            w = w + [conv_info[("v", dc)]]
            cnt = dma("sync", wvb[:, slot, :], wv_s[dc], RS["wv"][slot], waits=w)
        return slot, (RS["wv"][slot], cnt)

    def emit_cast_x0(s, t, first_tile=False):
        t0 = t * TT
        w = bufs["arena"].take()
        if first_tile:
            src3 = xT[s].rearrange("(dc p) t -> p dc t", p=128)[:, :, t0:t0 + TT]
            cnt = dma("gpsimd", x0bf.rearrange("p (dc t) -> p dc t", dc=DC), src3,
                      S.g_x0, waits=w)
            x0_info[0] = (S.g_x0, cnt)
        else:
            src3 = xs_s[s].rearrange("(dc p) t -> p dc t", p=128)[:, :, t0:t0 + TT]
            w = w + [conv_info[("x", s, dc)] for dc in range(DC)]
            cnt = dma("sync", x0bf.rearrange("p (dc t) -> p dc t", dc=DC), src3,
                      S.s_x0, waits=w)
            x0_info[0] = (S.s_x0, cnt)
        x0_i[0] += 1

    def emit_phasew():
        chunks = []
        for ic in range(ICN):
            chunks.append((g_t[ic], gs_s[ic], True, None))
            chunks.append((u_t[ic], us_s[ic], True, None))
        for hc in range(HCN):
            for qtr in range(4):
                chunks.append((d_t[hc][:, qtr * 2048:(qtr + 1) * 2048],
                               ds_s[hc][:, qtr * 2048:(qtr + 1) * 2048], True, None))
        xchunks = [[(xT[s][dc * 128:(dc + 1) * 128, :],
                     xs_s[s][dc * 128:(dc + 1) * 128, :], False, ("x", s, dc))
                    for dc in range(DC)] for s in range(n_seq)]
        d_at = len(chunks) - HCN * 4
        chunks[d_at:d_at] = xchunks[0]
        for oc in range(NKV):
            chunks.append((wk_t[oc], wk_s[oc], False, ("k", oc)))
        for oc in range(NH):
            chunks.append((wq_t[oc], wq_s[oc], False, ("q", oc)))
        for mc in range(DC):
            chunks.append((wo_t[mc], wo_s[mc], False, ("o", mc)))
        for dc in range(DC):
            chunks.append((wv_t[dc], wv_s[dc], False, ("v", dc)))
        for s in range(1, n_seq):
            chunks.extend(xchunks[s])
        for k, (csrc, dst, sub, key) in enumerate(chunks):
            slot = k % 2
            view = pwin[:, slot * 2048:(slot + 1) * 2048]
            nfree = csrc.shape[-1] if len(csrc.shape) == 2 else None
            v = view if (nfree is None or nfree == 2048) else view[:, 0:nfree]
            n_in = dma("gpsimd", v, csrc, RS["pwi"][slot], waits=bufs["pwin"][slot].take())
            if sub:
                c = dve("tensor_scalar_add", out=v, in0=v, scalar1=-128.0,
                        waits=[(RS["pwi"][slot], n_in)])
                w_out = [(S.e_dve, c)]
            else:
                w_out = [(RS["pwi"][slot], n_in)]
            n_out = dma("gpsimd", dst, v, RS["pwo"][slot], waits=w_out)
            bufs["pwin"][slot].consumed(RS["pwo"][slot], n_out)
            if key is not None:
                conv_info[key] = (RS["pwo"][slot], n_out)
        PW_DONE[0] = RS["pwo"][0].n
        PW_DONE[1] = RS["pwo"][1].n

    def qknorm(bi, out_slice, scale_vec, is_q, rst_slot, mm_n):
        c_copy = dve("tensor_copy", out=scrA, in_=Pap[bi],
                     waits=[(S.e_pe, mm_n)] + bufs["scrA"].take())
        bufs["P"][bi].consumed(S.e_dve, c_copy)
        c_sq = dve("tensor_mul", out=sqb[:, 0, :], in0=scrA, in1=scrA,
                   waits=bufs["sqb"][0].take())
        n = pe("matmul", out=Pap[6][0:1, :], lhsT=onesbf.ap()[:], rhs=sqb[:, 0, :],
               start=True, stop=True, waits=[(S.e_dve, c_sq)] + bufs["P"][6].take())
        bufs["sqb"][0].consumed(S.e_pe, n)
        c1 = act("activation", out=rows[0:1, 1, :], in_=Pap[6][0:1, :], func=AF.Ln,
                 scale=1.0 / 128.0, bias=EPS_QK, waits=[(S.e_pe, n)])
        bufs["P"][6].consumed(S.e_act, c1)
        bias2 = -0.5 * float(np.log(128.0)) if is_q else 0.0
        c2 = act("activation", out=rstdr[0:1, rst_slot, :], in_=rows[0:1, 1, :],
                 func=AF.Exp, scale=-0.5, bias=bias2)
        n2 = pe("matmul", out=Pap[7], lhsT=ones1bf.ap()[:],
                rhs=rstdr[0:1, rst_slot, :], start=True, stop=True,
                waits=[(S.e_act, c2)] + bufs["P"][7].take())
        c3 = dve("scalar_tensor_tensor", out=out_slice, in0=scrA, scalar=scale_vec,
                 in1=Pap[7], op0=ALU.mult, op1=ALU.mult,
                 waits=[(S.e_pe, n2), CONSTW])
        bufs["P"][7].consumed(S.e_dve, c3)
        bufs["scrA"].consumed(S.e_dve, c3)
        return c3

    def ln_apply(bi_sum, bi_sq, rsrc, dst_bf, out_spec, pe_n):
        c1 = dve("tensor_scalar_mul", out=rows[0:1, 0, :], in0=Pap[bi_sum][0:1, :],
                 scalar1=1.0 / 2048.0, waits=[(S.e_pe, pe_n)])
        c2 = dve("tensor_scalar_mul", out=rows[0:1, 1, :], in0=Pap[bi_sq][0:1, :],
                 scalar1=1.0 / 2048.0)
        c3 = dve("scalar_tensor_tensor", out=Pap[bi_sum][0:1, :], in0=rows[0:1, 0, :],
                 scalar=1.0, in1=rows[0:1, 0, :], op0=ALU.mult, op1=ALU.mult)
        c4 = dve("tensor_tensor", out=rows[0:1, 1, :], in0=rows[0:1, 1, :],
                 in1=Pap[bi_sum][0:1, :], op=ALU.subtract)
        c5 = act("activation", out=rows[0:1, 1, :], in_=rows[0:1, 1, :], func=AF.Ln,
                 bias=EPS_LN, waits=[(S.e_dve, c4)])
        c6 = act("activation", out=rstdr[0:1, 0, :], in_=rows[0:1, 1, :], func=AF.Exp,
                 scale=-0.5)
        n1 = pe("matmul", out=Pap[bi_sum], lhsT=ones1f.ap()[:], rhs=rows[0:1, 0, :],
                start=True, stop=True, waits=[(S.e_dve, c4)] + bufs["P"][bi_sum].take())
        n2 = pe("matmul", out=Pap[bi_sq], lhsT=ones1bf.ap()[:], rhs=rstdr[0:1, 0, :],
                start=True, stop=True, waits=[(S.e_act, c6)] + bufs["P"][bi_sq].take())
        for mc in range(DC):
            wts = ([(S.e_pe, max(n1, n2))] if mc == 0 else []) + bufs["scrA"].take()
            dve("tensor_tensor", out=scrA, in0=rsrc[:, mc * 512:(mc + 1) * 512],
                in1=Pap[bi_sum], op=ALU.subtract, waits=wts)
            if dst_bf is not None:
                cx = dve("tensor_tensor", out=dst_bf[:, mc * 512:(mc + 1) * 512],
                         in0=scrA, in1=Pap[bi_sq], op=ALU.mult)
            else:
                (so, t0) = out_spec
                oslot = mc % 2
                cx = dve("tensor_tensor", out=scrB[:, oslot, 0:512], in0=scrA,
                         in1=Pap[bi_sq], op=ALU.mult, waits=bufs["scrB"][oslot].take())
                nout = dma("sync", outT[so][mc * 128:(mc + 1) * 128, t0:t0 + TT],
                           scrB[:, oslot, 0:512], RS["out"][oslot], waits=[(S.e_dve, cx)])
                bufs["scrB"][oslot].consumed(RS["out"][oslot], nout)
            bufs["scrA"].consumed(S.e_dve, cx)
        bufs["P"][bi_sum].consumed(S.e_dve, S.e_dve.n)
        bufs["P"][bi_sq].consumed(S.e_dve, S.e_dve.n)
        if dst_bf is not None:
            X1_DONE[0] = S.e_dve.n

    def emit_tile(s, t, nxt, post_attn_hook=None):
        t0 = t * TT

        pr.phase = "qkv"
        first_tile = (s == 0 and t == 0)
        XW = x0_info[0]
        qk_pend = []
        for oc in range(NKV):
            slot, wql = load_wq("k", oc, first_tile)
            bi = oc % 2
            for dc in range(DC):
                wts = ()
                if dc == 0:
                    wts = [wql, XW, CONSTW,
                           (S.e_dve, MEMSET)] + bufs["P"][bi].take()
                pe("matmul", out=Pap[bi], lhsT=wqb[:, slot, dc * 128:(dc + 1) * 128],
                   rhs=x0bf[:, dc * 512:dc * 512 + TT], start=(dc == 0),
                   stop=(dc == DC - 1), waits=wts, mark=(dc == DC - 1))
            bufs["wqb"][slot].consumed(S.e_pe, S.e_pe.n)
            if qk_pend:
                qknorm(*qk_pend.pop(0))
            qk_pend.append((bi, khat[:, oc * NTOK + t0: oc * NTOK + t0 + TT],
                            ks.ap()[:], False, oc % 2, S.e_pe.n))
        qknorm(*qk_pend.pop(0))
        KHAT_DONE = S.e_dve.n

        for dc in range(DC):
            slot, wvl = load_wv(dc, first_tile)
            for sc in range(4):
                wts = []
                if dc == 0:
                    wts += bufs["P"][sc].take() + [XW]
                if sc == 0:
                    wts += [wvl]
                pe("matmul", out=Pap[sc],
                   lhsT=x0bf[:, dc * 512 + sc * 128:dc * 512 + (sc + 1) * 128],
                   rhs=wvb[:, slot, :], start=(dc == 0), stop=(dc == DC - 1),
                   waits=wts, mark=(sc == 3))
            bufs["wvb"][slot].consumed(S.e_pe, S.e_pe.n)
        for sc in range(4):
            scg = (t0 // 128) + sc
            c = dve("tensor_copy", out=vsb[:, scg * 512:(scg + 1) * 512],
                    in_=Pap[sc], waits=[(S.e_pe, S.e_pe.n)])
            bufs["P"][sc].consumed(S.e_dve, c)
        V_DONE = S.e_dve.n

        for oc in range(NH):
            slot, wql = load_wq("q", oc, first_tile)
            bi = oc % 2
            for dc in range(DC):
                wts = ()
                if dc == 0:
                    wts = [wql] + bufs["P"][bi].take()
                pe("matmul", out=Pap[bi], lhsT=wqb[:, slot, dc * 128:(dc + 1) * 128],
                   rhs=x0bf[:, dc * 512:dc * 512 + TT], start=(dc == 0),
                   stop=(dc == DC - 1), waits=wts, mark=(dc == DC - 1))
            bufs["wqb"][slot].consumed(S.e_pe, S.e_pe.n)
            if qk_pend:
                qknorm(*qk_pend.pop(0))
            qk_pend.append((bi, qhat[:, oc * 512:(oc + 1) * 512], qs.ap()[:],
                            True, oc % 2, S.e_pe.n))
        qknorm(*qk_pend.pop(0))
        QHAT_DONE = S.e_dve.n

        def emit_A(h):
            pr.phase = "attnA"
            par = h % 2
            stt = stt_t[:, par, :]
            sttl = sttl_t[:, par, :]
            sttw = sttw_t[:, par, :]
            sttlf = sttlf_t[:, par, :]
            stt2 = stt2_t[:, par, :]
            kv = h // 4
            for qc in range(4):
                n_parts = t + 1
                for sti in range(n_parts):
                    bi = 4 + (sti % 2)
                    na = 512 if sti < t else (qc + 1) * 128
                    wts = bufs["P"][bi].take() + [(S.e_dve, max(KHAT_DONE, QHAT_DONE))]
                    n = pe("matmul", out=Pap[bi][:, 0:na],
                           lhsT=qhat[:, h * 512 + qc * 128:h * 512 + (qc + 1) * 128],
                           rhs=khat[:, kv * NTOK + sti * 512:kv * NTOK + sti * 512 + na],
                           start=True, stop=True, waits=wts)
                    if sti == t:
                        cmask = dve("tensor_tensor", out=Pap[bi][:, 0:na], in0=Pap[bi][:, 0:na],
                                    in1=mka.ap()[:, qc * 512:qc * 512 + na], op=ALU.add,
                                    waits=[(S.e_pe, n), CONSTW])
                        wmax = [(S.e_dve, cmask)]
                    else:
                        wmax = [(S.e_pe, n)]
                    if sti == 0:
                        wmax = wmax + [(S.e_act, S.e_act.n)]
                    cm = dve("tensor_reduce", out=stt[:, sti:sti + 1], in_=Pap[bi][:, 0:na],
                             axis=AX.X, op=ALU.max, negate=True, waits=wmax)
                    ce = act("activation", out=dump[:, 0:na], in_=Pap[bi][:, 0:na], func=AF.Exp,
                             bias=stt[:, sti:sti + 1], accum_out=sttl[:, sti:sti + 1],
                             waits=[(S.e_dve, cm)])
                    bufs["P"][bi].consumed(S.e_act, ce)
                pr.phase = "attnC"
                w2 = bufs["stt2"][par].take() if qc == 0 else []
                cfl = act("copy", out=sttlf[:, 0:n_parts], in_=sttl[:, 0:n_parts])
                if t == 0:
                    cl = act("activation", out=sttl[:, 5:6], in_=sttl[:, 0:1], func=AF.Ln)
                    dve("tensor_sub", out=stt2[:, qc:qc + 1], in0=sttl[:, 5:6],
                        in1=stt[:, 0:1], waits=[(S.e_act, cl)] + w2)
                else:
                    c1 = dve("tensor_reduce", out=stt[:, 4:5], in_=stt[:, 0:n_parts],
                             axis=AX.X, op=ALU.min,
                             waits=[(S.e_act, S.e_act.n)] + ([(S.e_dve, S.e_dve.n)] if "c1" in SER else []))
                    c2 = act("activation", out=sttw[:, 0:n_parts], in_=stt[:, 0:n_parts],
                             func=AF.Exp, scale=-1.0, bias=stt[:, 4:5],
                             waits=[(S.e_dve, c1)])
                    c3 = dve("tensor_mul", out=sttw[:, 0:n_parts], in0=sttw[:, 0:n_parts],
                             in1=sttlf[:, 0:n_parts],
                             waits=[(S.e_act, c2)] + ([(S.e_dve, S.e_dve.n)] if "c3" in SER else []))
                    dve("tensor_reduce", out=sttl[:, 4:5], in_=sttw[:, 0:n_parts],
                        axis=AX.X, op=ALU.add,
                        waits=([(S.e_dve, S.e_dve.n)] if "c4" in SER else []))
                    cl = act("activation", out=sttl[:, 5:6], in_=sttl[:, 4:5], func=AF.Ln,
                             waits=[(S.e_dve, S.e_dve.n)])
                    dve("tensor_sub", out=stt2[:, qc:qc + 1], in0=sttl[:, 5:6],
                        in1=stt[:, 4:5],
                        waits=[(S.e_act, cl)] + w2 + ([(S.e_dve, S.e_dve.n)] if "c6" in SER else []))
                pr.phase = "attnA"
            return S.e_dve.n

        def emit_B(h, STT2):
            pr.phase = "attnB"
            par = h % 2
            stt2 = stt2_t[:, par, :]
            kv = h // 4
            n = pe("transpose", out=Pap[2][0:4, 0:128], in_=stt2[:, 0:4],
                   identity=identf.ap()[:],
                   waits=[(S.e_dve, STT2), CONSTW] + bufs["P"][2].take())
            bufs["stt2"][par].consumed(S.e_pe, n)
            c4 = dve("tensor_copy", out=st4, in_=Pap[2][0:4, 0:128],
                     waits=[(S.e_pe, n)] + bufs["st4"].take())
            bufs["P"][2].consumed(S.e_dve, c4)
            nrow = dma("sync", strow, st4, S.s_stat,
                       waits=bufs["strow"].take() + [(S.e_dve, c4)])
            bufs["st4"].consumed(S.s_stat, nrow)
            n = pe("matmul", out=Pap[7], lhsT=ones1f.ap()[:], rhs=strow,
                   start=True, stop=True,
                   waits=[(S.s_stat, nrow), (S.e_dve, MEMSET)] + bufs["P"][7].take())
            bufs["strow"].consumed(S.e_pe, n)
            c5 = dve("tensor_copy", out=scrA, in_=Pap[7],
                     waits=bufs["scrA"].take() + [(S.e_pe, n)])
            bufs["P"][7].consumed(S.e_dve, c5)
            n_sc = 4 * (t + 1)
            sc = 0
            it = 0
            while sc < n_sc:
                sslot = it % 2
                it += 1
                pair = (sc + 1 < 4 * t)
                if pair:
                    w0 = bufs["P"][2].take() + bufs["P"][3].take()
                    pe("matmul", out=Pap[2],
                       lhsT=khat[:, kv * NTOK + sc * 128:kv * NTOK + (sc + 1) * 128],
                       rhs=qhat[:, h * 512:(h + 1) * 512],
                       start=True, stop=True, waits=w0)
                    n = pe("matmul", out=Pap[3],
                           lhsT=khat[:, kv * NTOK + (sc + 1) * 128:kv * NTOK + (sc + 2) * 128],
                           rhs=qhat[:, h * 512:(h + 1) * 512],
                           start=True, stop=True)
                    wts = bufs["scrB"][sslot].take() + [(S.e_pe, n)]
                    csub = dve("tensor_tensor",
                               out=scrB[:, sslot, :].rearrange("p (a b) -> p a b", a=2),
                               in0=P23w.rearrange("p (a b) -> p a b", a=2),
                               in1=scrA.unsqueeze(1).broadcast_to([128, 2, 512]),
                               op=ALU.subtract, waits=wts)
                    bufs["scrA"].consumed(S.e_dve, csub)
                    bufs["P"][2].consumed(S.e_dve, csub)
                    bufs["P"][3].consumed(S.e_dve, csub)
                    cexp = act("activation", out=ptb[:, sslot, :], in_=scrB[:, sslot, :],
                               func=AF.Exp,
                               waits=bufs["ptb"][sslot].take() + [(S.e_dve, csub)])
                    bufs["scrB"][sslot].consumed(S.e_act, cexp)
                    wts = [(S.e_act, cexp), (S.e_dve, V_DONE)]
                    if sc == 0:
                        wts += bufs["P"][6].take()
                    pe("matmul", out=Pap[6],
                       lhsT=vsb[:, sc * 512 + kv * 128:sc * 512 + (kv + 1) * 128],
                       rhs=ptb[:, sslot, 0:512], start=(sc == 0), stop=False,
                       waits=wts)
                    n = pe("matmul", out=Pap[6],
                           lhsT=vsb[:, (sc + 1) * 512 + kv * 128:(sc + 1) * 512 + (kv + 1) * 128],
                           rhs=ptb[:, sslot, 512:1024], start=False,
                           stop=(sc + 1 == n_sc - 1))
                    bufs["ptb"][sslot].consumed(S.e_pe, n)
                    sc += 2
                    continue
                bi = 2 + (sc % 2)
                qlo = 0 if sc < 4 * t else (sc - 4 * t) * 128
                n = pe("matmul", out=Pap[bi][:, qlo:512],
                       lhsT=khat[:, kv * NTOK + sc * 128:kv * NTOK + (sc + 1) * 128],
                       rhs=qhat[:, h * 512 + qlo:(h + 1) * 512],
                       start=True, stop=True, waits=bufs["P"][bi].take())
                wts = bufs["scrB"][sslot].take() + [(S.e_pe, n)]
                csub = dve("tensor_tensor", out=scrB[:, sslot, qlo:512],
                           in0=Pap[bi][:, qlo:512],
                           in1=scrA[:, qlo:512], op=ALU.subtract, waits=wts)
                bufs["scrA"].consumed(S.e_dve, csub)
                bufs["P"][bi].consumed(S.e_dve, csub)
                if sc >= 4 * t:
                    csub = dve("tensor_tensor", out=scrB[:, sslot, qlo:512],
                               in0=scrB[:, sslot, qlo:512],
                               in1=mkb.ap()[:, (sc - 4 * t) * 512 + qlo:(sc - 4 * t + 1) * 512],
                               op=ALU.add, waits=[CONSTW])
                cexp = act("activation", out=ptb[:, sslot, qlo:512], in_=scrB[:, sslot, qlo:512],
                           func=AF.Exp, waits=bufs["ptb"][sslot].take() + [(S.e_dve, csub)])
                bufs["scrB"][sslot].consumed(S.e_act, cexp)
                wts = [(S.e_act, cexp), (S.e_dve, V_DONE)]
                if sc == 0:
                    wts += bufs["P"][6].take()
                n = pe("matmul", out=Pap[6][:, qlo:512],
                       lhsT=vsb[:, sc * 512 + kv * 128:sc * 512 + (kv + 1) * 128],
                       rhs=ptb[:, sslot, qlo:512], start=(sc == 0), stop=(sc == n_sc - 1),
                       waits=wts)
                bufs["ptb"][sslot].consumed(S.e_pe, n)
                sc += 1
            co = act("copy", out=ocat[:, h * 512:(h + 1) * 512], in_=Pap[6],
                     waits=[(S.e_pe, S.e_pe.n)])
            bufs["P"][6].consumed(S.e_act, co)

        pend = None
        for h in range(NH):
            st2n = emit_A(h)
            if pend is not None:
                emit_B(*pend)
            pend = (h, st2n)
        emit_B(*pend)
        OCAT_DONE = S.e_act.n
        pr.phase = "wo"

        def res_load(mc):
            rslot = mc % 4
            nres = dma("sync", x0res[:, rslot, :],
                       xT[s][mc * 128:(mc + 1) * 128, t0:t0 + TT], RS["res"][rslot],
                       waits=bufs["x0res"][rslot].take())
            return rslot, nres
        res_info = {mc: res_load(mc) for mc in range(2)}
        ones_q = []

        def ones_mm(mc):
            (csq, qslot) = ones_q.pop(0)
            wts = [(S.e_dve, csq)]
            if mc == 0:
                wts += bufs["P"][4].take() + bufs["P"][5].take()
            pe("matmul", out=Pap[4][0:1, :], lhsT=onesbf.ap()[:],
               rhs=r1v[:, mc * 512:(mc + 1) * 512], start=(mc == 0),
               stop=(mc == DC - 1), waits=wts, mark=(mc == DC - 1))
            nss = pe("matmul", out=Pap[5][0:1, :], lhsT=onesbf.ap()[:],
                     rhs=sqb[:, qslot, :], start=(mc == 0), stop=(mc == DC - 1),
                     mark=True)
            bufs["sqb"][qslot].consumed(S.e_pe, nss)

        for mc in range(DC):
            slot, wql = load_wq("o", mc, first_tile)
            bi = mc % 2
            for cc in range(DC):
                wts = ()
                if cc == 0:
                    wts = [wql,
                           (S.e_act, OCAT_DONE)] + bufs["P"][bi].take()
                pe("matmul", out=Pap[bi], lhsT=wqb[:, slot, cc * 128:(cc + 1) * 128],
                   rhs=ocat[:, cc * 512:(cc + 1) * 512], start=(cc == 0),
                   stop=(cc == DC - 1), waits=wts, mark=(cc == DC - 1))
            MMN = S.e_pe.n
            bufs["wqb"][slot].consumed(S.e_pe, MMN)
            if mc + 2 < DC:
                res_info[mc + 2] = res_load(mc + 2)
            if mc >= 1:
                ones_mm(mc - 1)
            rslot, nres = res_info.pop(mc)
            cr = dve("tensor_tensor", out=r1v[:, mc * 512:(mc + 1) * 512], in0=Pap[bi],
                     in1=x0res[:, rslot, :], op=ALU.add,
                     waits=[(S.e_pe, MMN), (RS["res"][rslot], nres)])
            bufs["P"][bi].consumed(S.e_dve, cr)
            bufs["x0res"][rslot].consumed(S.e_dve, cr)
            qslot = mc % 2
            csq = dve("tensor_mul", out=sqb[:, qslot, :],
                      in0=r1v[:, mc * 512:(mc + 1) * 512],
                      in1=r1v[:, mc * 512:(mc + 1) * 512],
                      waits=bufs["sqb"][qslot].take())
            ones_q.append((csq, qslot))
        ones_mm(DC - 1)
        ln_apply(4, 5, r1v, x1bf, None, S.e_pe.n)
        if post_attn_hook is not None:
            post_attn_hook()

        pr.phase = "ffn"
        first = (s == 0 and t == 0)
        for ic in range(ICN):
            gslot = ic % 2
            w = bufs["wgb"][gslot].take()
            if first:
                w += [pw_wait(2 * ic)]
            ng = dma("sync", wgb[:, gslot, :], gs_s[ic], RS["g"][gslot], waits=w)
            w = bufs["wub"][gslot].take()
            if first:
                w += [pw_wait(2 * ic + 1)]
            nu = dma("sync", wub[:, gslot, :], us_s[ic], RS["u"][gslot], waits=w)
            bg, bu = gslot, 2 + gslot
            for hc in range(HCN):
                wts = ()
                if hc == 0:
                    wts = [(RS["g"][gslot], ng), (S.e_dve, X1_DONE[0])] + bufs["P"][bg].take()
                pe("matmul", out=Pap[bg], lhsT=wgb[:, gslot, hc * 128:(hc + 1) * 128],
                   rhs=x1bf[:, hc * 512:(hc + 1) * 512], start=(hc == 0),
                   stop=(hc == HCN - 1), waits=wts, mark=(hc == HCN - 1))
            bufs["wgb"][gslot].consumed(S.e_pe, S.e_pe.n)
            for hc in range(HCN):
                wts = ()
                if hc == 0:
                    wts = [(RS["u"][gslot], nu)] + bufs["P"][bu].take()
                pe("matmul", out=Pap[bu], lhsT=wub[:, gslot, hc * 128:(hc + 1) * 128],
                   rhs=x1bf[:, hc * 512:(hc + 1) * 512], start=(hc == 0),
                   stop=(hc == HCN - 1), waits=wts, mark=(hc == HCN - 1))
            NU = S.e_pe.n
            bufs["wub"][gslot].consumed(S.e_pe, NU)
            sslot = ic % 2
            cs = act("activation", out=scrB[:, sslot, 0:512], in_=Pap[bg], func=AF.Silu,
                     scale=gsc.ap()[:, ic:ic + 1],
                     waits=bufs["scrB"][sslot].take() + [(S.e_pe, NU), CONSTW])
            bufs["P"][bg].consumed(S.e_act, cs)
            ch = dve("scalar_tensor_tensor", out=hact[:, ic * 512:(ic + 1) * 512],
                     in0=Pap[bu], scalar=usc.ap()[:, ic:ic + 1], in1=scrB[:, sslot, 0:512],
                     op0=ALU.mult, op1=ALU.mult, waits=[(S.e_act, cs)])
            bufs["P"][bu].consumed(S.e_dve, ch)
            bufs["scrB"][sslot].consumed(S.e_dve, ch)
        HACT_LAST = S.e_dve.n
        ones2_q = []

        def ones2_mm():
            (csq, qslot, hc2) = ones2_q.pop(0)
            wts = [(S.e_dve, csq)]
            if hc2 == 0:
                wts += bufs["P"][6].take() + bufs["P"][7].take()
            pe("matmul", out=Pap[6][0:1, :], lhsT=onesbf.ap()[:],
               rhs=r2v[:, hc2 * 512:(hc2 + 1) * 512], start=(hc2 == 0),
               stop=(hc2 == HCN - 1), waits=wts, mark=(hc2 == HCN - 1))
            pe("matmul", out=Pap[7][0:1, :], lhsT=onesbf.ap()[:], rhs=sqb[:, qslot, :],
               start=(hc2 == 0), stop=(hc2 == HCN - 1), mark=True)
            bufs["sqb"][qslot].consumed(S.e_pe, S.e_pe.n)

        for hc in range(HCN):
            bd = 4 + (hc % 2)
            for qtr in range(4):
                dslot = (hc * 4 + qtr) % 2
                w = bufs["wdb"][dslot].take()
                if first:
                    w += [pw_wait(2 * ICN + DC + hc * 4 + qtr)]
                nd = dma("sync", wdb[:, dslot, :],
                         ds_s[hc][:, qtr * 2048:(qtr + 1) * 2048], RS["d"][dslot], waits=w)
                for ih in range(16):
                    ic = qtr * 16 + ih
                    wts = []
                    if ic == 0:
                        wts += [(S.e_dve, HACT_LAST)] + bufs["P"][bd].take()
                    if ih == 0:
                        wts += [(RS["d"][dslot], nd)]
                    pe("matmul", out=Pap[bd], lhsT=wdb[:, dslot, ih * 128:(ih + 1) * 128],
                       rhs=hact[:, ic * 512:(ic + 1) * 512], start=(ic == 0),
                       stop=(ic == ICN - 1), waits=wts, mark=(ih == 15))
                bufs["wdb"][dslot].consumed(S.e_pe, S.e_pe.n)
            MMN = S.e_pe.n
            wts = [(S.e_pe, MMN), CONSTW]
            if first and hc >= 8:
                wts += [(RS["pwo"][0], PW_DONE[0]), (RS["pwo"][1], PW_DONE[1])]
            cr = dve("scalar_tensor_tensor", out=r2v[:, hc * 512:(hc + 1) * 512],
                     in0=Pap[bd], scalar=dsc.ap()[:, hc:hc + 1],
                     in1=x1bf[:, hc * 512:(hc + 1) * 512],
                     op0=ALU.mult, op1=ALU.add, waits=wts)
            bufs["P"][bd].consumed(S.e_dve, cr)
            qslot = hc % 2
            csq = dve("tensor_mul", out=sqb[:, qslot, :],
                      in0=r2v[:, hc * 512:(hc + 1) * 512],
                      in1=r2v[:, hc * 512:(hc + 1) * 512],
                      waits=bufs["sqb"][qslot].take())
            ones2_q.append((csq, qslot, hc))
            if hc >= 1:
                ones2_mm()
        ones2_mm()
        bufs["arena"].consumed(S.e_pe, S.e_pe.n)
        if nxt is not None:
            emit_cast_x0(*nxt)
        ln_apply(6, 7, r2v, None, (s, t0), S.e_pe.n)

    tiles = [(s, t) for s in range(n_seq) for t in range(n_tiles)]
    emit_cast_x0(*tiles[0], first_tile=True)
    for i, (s, t) in enumerate(tiles):
        nxt = tiles[i + 1] if i + 1 < len(tiles) else None
        emit_tile(s, t, nxt, post_attn_hook=emit_phasew if i == 0 else None)
    pr.rec("sync", "wait_ge", RS["out"][0].h, RS["out"][0].n)
    pr.rec("sync", "wait_ge", RS["out"][1].h, RS["out"][1].n)
    pr.replay()
    nc._prog = pr
    return nc



def _prep(inputs):
    f32 = np.float32
    Wq = np.asarray(inputs["L0_attn_Wq"], f32)
    Wk = np.asarray(inputs["L0_attn_Wk"], f32)
    Wv = np.asarray(inputs["L0_attn_Wv"], f32)
    Wo = np.asarray(inputs["L0_attn_Wo"], f32)
    gw = np.asarray(inputs["L0_ffn_gate_w"], np.int32)
    uw = np.asarray(inputs["L0_ffn_up_w"], np.int32)
    dw = np.asarray(inputs["L0_ffn_down_w"], np.int32)

    def wtile(W, n_out):
        r = W.reshape(n_out, 128, DC, 128)
        return np.ascontiguousarray(r.transpose(0, 3, 2, 1)).reshape(n_out, 128, D)

    d = {}
    d["wq_t"] = wtile(Wq, NH)
    d["wk_t"] = wtile(Wk, NKV)
    d["wo_t"] = wtile(Wo, DC)
    d["wv_t"] = np.ascontiguousarray(Wv.T.reshape(DC, 128, 512))
    d["g_t"] = np.ascontiguousarray(
        gw.reshape(ICN, 128, DC, 128).transpose(0, 3, 2, 1)).reshape(ICN, 128, D)
    d["u_t"] = np.ascontiguousarray(
        uw.reshape(ICN, 128, DC, 128).transpose(0, 3, 2, 1)).reshape(ICN, 128, D)
    d["d_t"] = np.ascontiguousarray(
        dw.reshape(HCN, 128, ICN, 128).transpose(0, 3, 2, 1)).reshape(HCN, 128, II)
    d["qs"] = np.ascontiguousarray(np.asarray(inputs["L0_attn_q_scale"], f32).reshape(128, 1))
    d["ks"] = np.ascontiguousarray(np.asarray(inputs["L0_attn_k_scale"], f32).reshape(128, 1))
    d["gs"] = np.ascontiguousarray(np.asarray(inputs["L0_ffn_gate_scale"], f32).reshape(ICN, 128).T)
    d["us"] = np.ascontiguousarray(np.asarray(inputs["L0_ffn_up_scale"], f32).reshape(ICN, 128).T)
    d["ds"] = np.ascontiguousarray(np.asarray(inputs["L0_ffn_down_scale"], f32).reshape(HCN, 128).T)
    i_idx = np.arange(128)[:, None]
    j_idx = np.arange(512)[None, :]
    d["mka"] = np.concatenate([np.where(j_idx <= qc * 128 + i_idx, 0.0, NEG)
                               for qc in range(4)], axis=1).astype(ml_dtypes.bfloat16)
    d["mkb"] = np.concatenate([np.where(p * 128 + i_idx <= j_idx, 0.0, NEG)
                               for p in range(4)], axis=1).astype(ml_dtypes.bfloat16)
    d["idn"] = np.eye(128, dtype=f32)
    return d


def kernel(**inputs):
    x = np.asarray(inputs["L0_attn_x"], np.float32)
    shared = _prep(inputs)
    nc = build_core_program()
    in_maps = []
    for c in range(NCORE):
        m = dict(shared)
        m["xT"] = np.ascontiguousarray(x[2 * c:2 * c + 2].transpose(0, 2, 1))
        in_maps.append(m)
    res = run_bass_kernel_spmd(nc, in_maps, core_ids=list(range(NCORE)))
    out = np.empty((B, T, D), np.float32)
    for c in range(NCORE):
        o = np.asarray(res.results[c]["outT"], np.float32)
        out[2 * c:2 * c + 2] = o.transpose(0, 2, 1)
    return out


# version 08
# speedup vs baseline: 1.0473x; 1.0473x over previous
import os
import sys
import numpy as np
import ml_dtypes

try:
    import concourse.bass as bass
except ImportError:
    sys.path.insert(0, "/opt/trn_rl_repo")
    import concourse.bass as bass
import concourse.mybir as mybir
from concourse.bass_utils import run_bass_kernel_spmd

F32 = mybir.dt.float32
BF16 = mybir.dt.bfloat16
I32 = mybir.dt.int32
AF = mybir.ActivationFunctionType
ALU = mybir.AluOpType
AX = mybir.AxisListType

B, T, D = 16, 2048, 2048
NH, NKV, HD = 16, 4, 128
II = 8192
TT = 512
DC = D // 128
ICN = II // 128
HCN = DC
NCORE = 8
SER = {"c4", "c6"}
NEG = -30000.0
EPS_QK = 1e-6
EPS_LN = 1e-5


class SemC:
    def __init__(self, nc, name):
        self.h = nc.alloc_semaphore(name)
        self.n = 0
        self.name = name


class Buf:
    def __init__(self):
        self.need = {}

    def consumed(self, sem, cnt):
        if sem is not None and (sem not in self.need or self.need[sem] < cnt):
            self.need[sem] = cnt

    def take(self):
        w = list(self.need.items())
        self.need = {}
        return w


class Prog:
    def __init__(self, nc, paranoid=False):
        self.nc = nc
        self.ops = []
        self.labels = []
        self.phase = ""
        self.label_of = {}
        if paranoid is True:
            paranoid = {"sync", "scalar", "vector", "tensor", "gpsimd"}
        self.paranoid = paranoid or set()
        if self.paranoid:
            self.pp = {e: SemC(nc, "pp_" + e) for e in self.paranoid}

    def rec(self, eng, fn, *args, waits=(), inc=None, **kwargs):
        ws = [(s, c) for (s, c) in waits if s is not None]
        if inc is not None:
            inc[0].n += inc[1]
        self.ops.append((eng, fn, args, kwargs, ws, inc, self.phase))
        self.labels.append(self.phase + "/" + fn)

    def replay(self):
        hw = {}
        dep = mybir.DependencyInfo(mybir.DependencyClass.NoSync,
                                   mybir.DependencyReason.DATA)

        def run_engine(engobj, engname):
            prev = None
            prev_done = None
            rn = {}
            for oi, (eng, fn, args, kwargs, ws, inc, phase) in enumerate(self.ops):
                if eng != engname:
                    continue
                lbl = self.labels[oi]
                par = engname in self.paranoid or (engname + ":" + phase) in self.paranoid
                if par and fn != "wait_ge" and prev_done is not None:
                    engobj.wait_ge(prev_done[0].h, prev_done[1])
                for (sem, cnt) in ws:
                    key = (engname, sem.name)
                    if hw.get(key, -1) >= cnt:
                        continue
                    hw[key] = cnt
                    engobj.wait_ge(sem.h, cnt)
                ins = getattr(engobj, fn)(*args, **kwargs)
                self.label_of[ins.ins.name] = lbl
                if inc is not None:
                    ins.then_inc(inc[0].h, inc[1])
                if par and fn != "wait_ge":
                    if inc is not None:
                        rn[inc[0].name] = rn.get(inc[0].name, 0) + inc[1]
                        prev_done = (inc[0], rn[inc[0].name])
                    else:
                        amt = 16 if fn == "dma_start" else 1
                        pp = self.pp[engname]
                        pp.n += amt
                        ins.then_inc(pp.h, amt)
                        prev_done = (pp, pp.n)
                if prev is not None:
                    try:
                        ins.ins.add_dependency(prev, dep)
                    except Exception:
                        pass
                prev = ins.ins.name

        with self.nc.Block() as block:
            block.sync(lambda e: run_engine(e, "sync"))
            block.scalar(lambda e: run_engine(e, "scalar"))
            block.vector(lambda e: run_engine(e, "vector"))
            block.tensor(lambda e: run_engine(e, "tensor"))
            block.gpsimd(lambda e: run_engine(e, "gpsimd"))


def build_core_program(n_seq=2, n_tiles=T // TT, paranoid=False):
    nc = bass.Bass("TRN2", target_bir_lowering=False)
    NTOK = n_tiles * TT

    xT = nc.dram_tensor("xT", [n_seq, D, NTOK], F32, kind="ExternalInput").ap()
    wq_t = nc.dram_tensor("wq_t", [NH, 128, D], F32, kind="ExternalInput").ap()
    wk_t = nc.dram_tensor("wk_t", [NKV, 128, D], F32, kind="ExternalInput").ap()
    wv_t = nc.dram_tensor("wv_t", [DC, 128, 512], F32, kind="ExternalInput").ap()
    wo_t = nc.dram_tensor("wo_t", [DC, 128, D], F32, kind="ExternalInput").ap()
    g_t = nc.dram_tensor("g_t", [ICN, 128, D], I32, kind="ExternalInput").ap()
    u_t = nc.dram_tensor("u_t", [ICN, 128, D], I32, kind="ExternalInput").ap()
    d_t = nc.dram_tensor("d_t", [HCN, 128, II], I32, kind="ExternalInput").ap()
    qs_d = nc.dram_tensor("qs", [128, 1], F32, kind="ExternalInput").ap()
    ks_d = nc.dram_tensor("ks", [128, 1], F32, kind="ExternalInput").ap()
    gs_d = nc.dram_tensor("gs", [128, ICN], F32, kind="ExternalInput").ap()
    us_d = nc.dram_tensor("us", [128, ICN], F32, kind="ExternalInput").ap()
    ds_d = nc.dram_tensor("ds", [128, HCN], F32, kind="ExternalInput").ap()
    mka_d = nc.dram_tensor("mka", [128, 4 * 512], BF16, kind="ExternalInput").ap()
    mkb_d = nc.dram_tensor("mkb", [128, 4 * 512], BF16, kind="ExternalInput").ap()
    idn_d = nc.dram_tensor("idn", [128, 128], F32, kind="ExternalInput").ap()
    outT = nc.dram_tensor("outT", [n_seq, D, NTOK], F32, kind="ExternalOutput").ap()
    gs_s = nc.dram_tensor("gs_s", [ICN, 128, D], BF16, kind="Internal").ap()
    us_s = nc.dram_tensor("us_s", [ICN, 128, D], BF16, kind="Internal").ap()
    ds_s = nc.dram_tensor("ds_s", [HCN, 128, II], BF16, kind="Internal").ap()
    wq_s = nc.dram_tensor("wq_s", [NH, 128, D], BF16, kind="Internal").ap()
    wk_s = nc.dram_tensor("wk_s", [NKV, 128, D], BF16, kind="Internal").ap()
    wv_s = nc.dram_tensor("wv_s", [DC, 128, 512], BF16, kind="Internal").ap()
    wo_s = nc.dram_tensor("wo_s", [DC, 128, D], BF16, kind="Internal").ap()
    xs_s = nc.dram_tensor("xs_s", [n_seq, D, NTOK], BF16, kind="Internal").ap()

    A = nc.alloc_sbuf_tensor
    identf = A("identf", [128, 128], F32)
    onesbf = A("onesbf", [128, 1], BF16)
    ones1f = A("ones1f", [1, 128], F32)
    ones1bf = A("ones1bf", [1, 128], BF16)
    qs = A("qs_s", [128, 1], F32)
    ks = A("ks_s", [128, 1], F32)
    gsc = A("gsc", [128, ICN], F32)
    usc = A("usc", [128, ICN], F32)
    dsc = A("dsc", [128, HCN], F32)
    mka = A("mka_s", [128, 4 * 512], BF16)
    mkb = A("mkb_s", [128, 4 * 512], BF16)
    khat_t = A("khat", [128, NKV * NTOK], BF16)
    vsb_t = A("vsb", [128, (NTOK // 128) * 512], BF16)
    arena_t = A("arena", [128, ICN * 512], BF16)
    khat = khat_t.ap()
    vsb = vsb_t.ap()
    arena = arena_t.ap()
    x0bf = arena[:, 0:DC * 512]
    qhat = arena[:, DC * 512:2 * DC * 512]
    ocat = arena[:, 2 * DC * 512:3 * DC * 512]
    r1v = arena[:, 3 * DC * 512:4 * DC * 512]
    hact = arena
    x1bf = A("x1bf", [128, DC * 512], BF16).ap()
    r2v = A("r2v", [128, DC * 512], BF16).ap()
    pwin = r2v[:, 4096:4096 + 2 * 2048]
    sqb = A("sqb", [128, 2, 512], BF16).ap()
    wqb = A("wqb", [128, 2, DC * 128], BF16).ap()
    wvb = A("wvb", [128, 2, 512], BF16).ap()
    wgb = A("wgb", [128, 2, HCN * 128], BF16).ap()
    wub = A("wub", [128, 2, HCN * 128], BF16).ap()
    wdb = A("wdb", [128, 2, 16 * 128], BF16).ap()
    scrA = A("scrA", [128, 512], F32).ap()
    scrB = A("scrB", [128, 2, 1024], F32).ap()
    dump = A("dump", [128, 1024], BF16).ap()
    ptb = A("ptb", [128, 2, 1024], BF16).ap()
    stt_t = A("stt", [128, 2, 8], F32).ap()
    sttl_t = A("sttl", [128, 2, 8], F32).ap()
    sttw_t = A("sttw", [128, 2, 4], F32).ap()
    sttlf_t = A("sttlf", [128, 2, 4], F32).ap()
    stt2_t = A("stt2", [128, 2, 4], F32).ap()
    st4 = A("st4", [4, 128], F32).ap()
    strow = A("strow", [1, 512], F32).ap()
    rows = A("rows", [1, 2, 512], F32).ap()
    rstdr = A("rstdr", [1, 2, 512], BF16).ap()
    x0res = A("x0res", [128, 4, 512], F32).ap()
    constt = A("constt", [128, 8], F32).ap()
    if os.environ.get("KSBUF"):
        print("sbuf remaining bytes/partition:", nc.sbuf_bytes_remaining)
    assert nc.sbuf_bytes_remaining >= 0, nc.sbuf_bytes_remaining

    PPAIR = [nc.alloc_psum_tensor(f"PP{i}", [128, 1024], F32) for i in range(4)]
    PAw, PBw, PCw, PDw = [p.ap() for p in PPAIR]
    P23w = PBw
    Pap = [PAw[:, 0:512], PAw[:, 512:1024], PBw[:, 0:512], PBw[:, 512:1024],
           PCw[:, 0:512], PCw[:, 512:1024], PDw[:, 0:512], PDw[:, 512:1024]]

    pr = Prog(nc, paranoid=paranoid)
    sems = {}
    for name in ("s_const", "s_stat", "g_x0", "e_pe", "e_dve", "e_act", "e_pool"):
        sems[name] = SemC(nc, name)
    for base in ("wq", "wv", "g", "u", "d", "out", "pwi", "pwo",
                 "wqg", "wvg"):
        for j in range(2):
            sems[base + str(j)] = SemC(nc, base + str(j))
    for j in range(4):
        sems["res" + str(j)] = SemC(nc, "res" + str(j))
    sems["s_x0"] = SemC(nc, "s_x0")
    S = type("S", (), sems)
    RS = {b: (sems[b + "0"], sems[b + "1"]) for b in
          ("wq", "wv", "g", "u", "d", "out", "pwi", "pwo", "wqg", "wvg")}
    RS["res"] = tuple(sems["res" + str(j)] for j in range(4))

    bufs = {
        "wqb": [Buf(), Buf()], "wvb": [Buf(), Buf()], "wgb": [Buf(), Buf()],
        "wub": [Buf(), Buf()], "wdb": [Buf(), Buf()], "x0res": [Buf(), Buf(), Buf(), Buf()],
        "sqb": [Buf(), Buf()], "scrB": [Buf(), Buf()],
        "ptb": [Buf(), Buf()], "pwin": [Buf(), Buf()],
        "arena": Buf(), "strow": Buf(), "scrA": Buf(), "st4": Buf(),
        "stt2": [Buf(), Buf()],
        "P": [Buf() for _ in range(8)],
    }

    def dve(fn, *a, waits=(), **k):
        pr.rec("vector", fn, *a, waits=waits, inc=(S.e_dve, 1), **k)
        return S.e_dve.n

    def act(fn, *a, waits=(), **k):
        pr.rec("scalar", fn, *a, waits=waits, inc=(S.e_act, 1), **k)
        return S.e_act.n

    def pool(fn, *a, waits=(), **k):
        pr.rec("gpsimd", fn, *a, waits=waits, inc=(S.e_pool, 1), **k)
        return S.e_pool.n

    def pe(fn, *a, waits=(), mark=True, **k):
        pr.rec("tensor", fn, *a, waits=waits, inc=(S.e_pe, 1) if mark else None, **k)
        return S.e_pe.n if mark else None

    def dma(queue, out, in_, sem, waits=()):
        pr.rec(queue, "dma_start", waits=waits, inc=(sem, 16), out=out, in_=in_)
        return sem.n

    for dst, src in ((qs, qs_d), (ks, ks_d), (gsc, gs_d), (usc, us_d), (dsc, ds_d),
                     (mka, mka_d), (mkb, mkb_d), (identf, idn_d)):
        dma("sync", dst.ap()[:], src, S.s_const)
    CONSTW = (S.s_const, S.s_const.n)
    dve("memset", onesbf.ap()[:], 1.0)
    dve("memset", ones1f.ap()[:], 1.0)
    dve("memset", ones1bf.ap()[:], 1.0)
    cvals = [EPS_QK, EPS_LN, -0.5 * float(np.log(128.0)), 1.0 / 128.0, -0.5, -1.0]
    for j, v in enumerate(cvals):
        dve("memset", constt[:, j:j + 1], v)
        nc.const_aps.register(F32, v, constt[:, j:j + 1])
    MEMSET = S.e_dve.n

    X1_DONE = [0]
    x0_info = [None]
    wq_i = [0]
    wv_i = [0]
    x0_i = [0]
    PW_TOTAL = 2 * ICN + HCN * 4

    def pw_wait(c):
        return (RS["pwo"][c % 2], 16 * (c // 2 + 1))

    conv_info = {}
    PW_DONE = [0, 0]

    def load_wq(kind, idx, first_tile):
        slot = wq_i[0] % 2
        wq_i[0] += 1
        w = bufs["wqb"][slot].take()
        if first_tile:
            srcap = {"k": wk_t, "q": wq_t, "o": wo_t}[kind][idx]
            cnt = dma("gpsimd", wqb[:, slot, :], srcap, RS["wqg"][slot], waits=w)
            return slot, (RS["wqg"][slot], cnt)
        else:
            srcap = {"k": wk_s, "q": wq_s, "o": wo_s}[kind][idx]
            w = w + [conv_info[(kind, idx)]]
            cnt = dma("sync", wqb[:, slot, :], srcap, RS["wq"][slot], waits=w)
        return slot, (RS["wq"][slot], cnt)

    def load_wv(dc, first_tile):
        slot = wv_i[0] % 2
        wv_i[0] += 1
        w = bufs["wvb"][slot].take()
        if first_tile:
            cnt = dma("gpsimd", wvb[:, slot, :], wv_t[dc], RS["wvg"][slot], waits=w)
            return slot, (RS["wvg"][slot], cnt)
        else:
            w = w + [conv_info[("v", dc)]]
            cnt = dma("sync", wvb[:, slot, :], wv_s[dc], RS["wv"][slot], waits=w)
        return slot, (RS["wv"][slot], cnt)

    def emit_cast_x0(s, t, first_tile=False):
        t0 = t * TT
        w = bufs["arena"].take()
        if first_tile:
            src3 = xT[s].rearrange("(dc p) t -> p dc t", p=128)[:, :, t0:t0 + TT]
            cnt = dma("gpsimd", x0bf.rearrange("p (dc t) -> p dc t", dc=DC), src3,
                      S.g_x0, waits=w)
            x0_info[0] = (S.g_x0, cnt)
        else:
            src3 = xs_s[s].rearrange("(dc p) t -> p dc t", p=128)[:, :, t0:t0 + TT]
            w = w + [conv_info[("x", s, dc)] for dc in range(DC)]
            cnt = dma("sync", x0bf.rearrange("p (dc t) -> p dc t", dc=DC), src3,
                      S.s_x0, waits=w)
            x0_info[0] = (S.s_x0, cnt)
        x0_i[0] += 1

    def emit_phasew():
        chunks = []
        for ic in range(ICN):
            chunks.append((g_t[ic], gs_s[ic], True, None))
            chunks.append((u_t[ic], us_s[ic], True, None))
        for hc in range(HCN):
            for qtr in range(4):
                chunks.append((d_t[hc][:, qtr * 2048:(qtr + 1) * 2048],
                               ds_s[hc][:, qtr * 2048:(qtr + 1) * 2048], True, None))
        xchunks = [[(xT[s][dc * 128:(dc + 1) * 128, :],
                     xs_s[s][dc * 128:(dc + 1) * 128, :], False, ("x", s, dc))
                    for dc in range(DC)] for s in range(n_seq)]
        d_at = len(chunks) - HCN * 4
        chunks[d_at:d_at] = xchunks[0]
        for oc in range(NKV):
            chunks.append((wk_t[oc], wk_s[oc], False, ("k", oc)))
        for oc in range(NH):
            chunks.append((wq_t[oc], wq_s[oc], False, ("q", oc)))
        for mc in range(DC):
            chunks.append((wo_t[mc], wo_s[mc], False, ("o", mc)))
        for dc in range(DC):
            chunks.append((wv_t[dc], wv_s[dc], False, ("v", dc)))
        for s in range(1, n_seq):
            chunks.extend(xchunks[s])
        for k, (csrc, dst, sub, key) in enumerate(chunks):
            slot = k % 2
            view = pwin[:, slot * 2048:(slot + 1) * 2048]
            nfree = csrc.shape[-1] if len(csrc.shape) == 2 else None
            v = view if (nfree is None or nfree == 2048) else view[:, 0:nfree]
            n_in = dma("gpsimd", v, csrc, RS["pwi"][slot], waits=bufs["pwin"][slot].take())
            if sub:
                c = dve("tensor_scalar_add", out=v, in0=v, scalar1=-128.0,
                        waits=[(RS["pwi"][slot], n_in)])
                w_out = [(S.e_dve, c)]
            else:
                w_out = [(RS["pwi"][slot], n_in)]
            n_out = dma("gpsimd", dst, v, RS["pwo"][slot], waits=w_out)
            bufs["pwin"][slot].consumed(RS["pwo"][slot], n_out)
            if key is not None:
                conv_info[key] = (RS["pwo"][slot], n_out)
        PW_DONE[0] = RS["pwo"][0].n
        PW_DONE[1] = RS["pwo"][1].n

    def qknorm(bi, out_slice, scale_vec, is_q, rst_slot, mm_n):
        c_copy = dve("tensor_copy", out=scrA, in_=Pap[bi],
                     waits=[(S.e_pe, mm_n)] + bufs["scrA"].take())
        bufs["P"][bi].consumed(S.e_dve, c_copy)
        c_sq = dve("tensor_mul", out=sqb[:, 0, :], in0=scrA, in1=scrA,
                   waits=bufs["sqb"][0].take())
        n = pe("matmul", out=Pap[6][0:1, :], lhsT=onesbf.ap()[:], rhs=sqb[:, 0, :],
               start=True, stop=True, waits=[(S.e_dve, c_sq)] + bufs["P"][6].take())
        bufs["sqb"][0].consumed(S.e_pe, n)
        c1 = act("activation", out=rows[0:1, 1, :], in_=Pap[6][0:1, :], func=AF.Ln,
                 scale=1.0 / 128.0, bias=EPS_QK, waits=[(S.e_pe, n)])
        bufs["P"][6].consumed(S.e_act, c1)
        bias2 = -0.5 * float(np.log(128.0)) if is_q else 0.0
        c2 = act("activation", out=rstdr[0:1, rst_slot, :], in_=rows[0:1, 1, :],
                 func=AF.Exp, scale=-0.5, bias=bias2)
        n2 = pe("matmul", out=Pap[7], lhsT=ones1bf.ap()[:],
                rhs=rstdr[0:1, rst_slot, :], start=True, stop=True,
                waits=[(S.e_act, c2)] + bufs["P"][7].take())
        c3 = dve("scalar_tensor_tensor", out=out_slice, in0=scrA, scalar=scale_vec,
                 in1=Pap[7], op0=ALU.mult, op1=ALU.mult,
                 waits=[(S.e_pe, n2), CONSTW])
        bufs["P"][7].consumed(S.e_dve, c3)
        bufs["scrA"].consumed(S.e_dve, c3)
        return c3

    def ln_apply(bi_sum, bi_sq, rsrc, dst_bf, out_spec, pe_n):
        c1 = dve("tensor_scalar_mul", out=rows[0:1, 0, :], in0=Pap[bi_sum][0:1, :],
                 scalar1=1.0 / 2048.0, waits=[(S.e_pe, pe_n)])
        c2 = dve("tensor_scalar_mul", out=rows[0:1, 1, :], in0=Pap[bi_sq][0:1, :],
                 scalar1=1.0 / 2048.0)
        c3 = dve("scalar_tensor_tensor", out=Pap[bi_sum][0:1, :], in0=rows[0:1, 0, :],
                 scalar=1.0, in1=rows[0:1, 0, :], op0=ALU.mult, op1=ALU.mult)
        c4 = dve("tensor_tensor", out=rows[0:1, 1, :], in0=rows[0:1, 1, :],
                 in1=Pap[bi_sum][0:1, :], op=ALU.subtract)
        c5 = act("activation", out=rows[0:1, 1, :], in_=rows[0:1, 1, :], func=AF.Ln,
                 bias=EPS_LN, waits=[(S.e_dve, c4)])
        c6 = act("activation", out=rstdr[0:1, 0, :], in_=rows[0:1, 1, :], func=AF.Exp,
                 scale=-0.5)
        n1 = pe("matmul", out=Pap[bi_sum], lhsT=ones1f.ap()[:], rhs=rows[0:1, 0, :],
                start=True, stop=True, waits=[(S.e_dve, c4)] + bufs["P"][bi_sum].take())
        n2 = pe("matmul", out=Pap[bi_sq], lhsT=ones1bf.ap()[:], rhs=rstdr[0:1, 0, :],
                start=True, stop=True, waits=[(S.e_act, c6)] + bufs["P"][bi_sq].take())
        for mc in range(DC):
            wts = ([(S.e_pe, max(n1, n2))] if mc == 0 else []) + bufs["scrA"].take()
            dve("tensor_tensor", out=scrA, in0=rsrc[:, mc * 512:(mc + 1) * 512],
                in1=Pap[bi_sum], op=ALU.subtract, waits=wts)
            if dst_bf is not None:
                cx = dve("tensor_tensor", out=dst_bf[:, mc * 512:(mc + 1) * 512],
                         in0=scrA, in1=Pap[bi_sq], op=ALU.mult)
            else:
                (so, t0) = out_spec
                oslot = mc % 2
                cx = dve("tensor_tensor", out=scrB[:, oslot, 0:512], in0=scrA,
                         in1=Pap[bi_sq], op=ALU.mult, waits=bufs["scrB"][oslot].take())
                nout = dma("sync", outT[so][mc * 128:(mc + 1) * 128, t0:t0 + TT],
                           scrB[:, oslot, 0:512], RS["out"][oslot], waits=[(S.e_dve, cx)])
                bufs["scrB"][oslot].consumed(RS["out"][oslot], nout)
            bufs["scrA"].consumed(S.e_dve, cx)
        bufs["P"][bi_sum].consumed(S.e_dve, S.e_dve.n)
        bufs["P"][bi_sq].consumed(S.e_dve, S.e_dve.n)
        if dst_bf is not None:
            X1_DONE[0] = S.e_dve.n

    def emit_tile(s, t, nxt, post_attn_hook=None):
        t0 = t * TT

        pr.phase = "qkv"
        first_tile = (s == 0 and t == 0)
        XW = x0_info[0]
        qk_pend = []
        for oc in range(NKV):
            slot, wql = load_wq("k", oc, first_tile)
            bi = oc % 2
            for dc in range(DC):
                wts = ()
                if dc == 0:
                    wts = [wql, XW, CONSTW,
                           (S.e_dve, MEMSET)] + bufs["P"][bi].take()
                pe("matmul", out=Pap[bi], lhsT=wqb[:, slot, dc * 128:(dc + 1) * 128],
                   rhs=x0bf[:, dc * 512:dc * 512 + TT], start=(dc == 0),
                   stop=(dc == DC - 1), waits=wts, mark=(dc == DC - 1))
            bufs["wqb"][slot].consumed(S.e_pe, S.e_pe.n)
            if qk_pend:
                qknorm(*qk_pend.pop(0))
            qk_pend.append((bi, khat[:, oc * NTOK + t0: oc * NTOK + t0 + TT],
                            ks.ap()[:], False, oc % 2, S.e_pe.n))
        qknorm(*qk_pend.pop(0))
        KHAT_DONE = S.e_dve.n

        for dc in range(DC):
            slot, wvl = load_wv(dc, first_tile)
            for sc in range(4):
                wts = []
                if dc == 0:
                    wts += bufs["P"][sc].take() + [XW]
                if sc == 0:
                    wts += [wvl]
                pe("matmul", out=Pap[sc],
                   lhsT=x0bf[:, dc * 512 + sc * 128:dc * 512 + (sc + 1) * 128],
                   rhs=wvb[:, slot, :], start=(dc == 0), stop=(dc == DC - 1),
                   waits=wts, mark=(sc == 3))
            bufs["wvb"][slot].consumed(S.e_pe, S.e_pe.n)
        for sc in range(4):
            scg = (t0 // 128) + sc
            c = dve("tensor_copy", out=vsb[:, scg * 512:(scg + 1) * 512],
                    in_=Pap[sc], waits=[(S.e_pe, S.e_pe.n)])
            bufs["P"][sc].consumed(S.e_dve, c)
        V_DONE = S.e_dve.n

        for oc in range(NH):
            slot, wql = load_wq("q", oc, first_tile)
            bi = oc % 2
            for dc in range(DC):
                wts = ()
                if dc == 0:
                    wts = [wql] + bufs["P"][bi].take()
                pe("matmul", out=Pap[bi], lhsT=wqb[:, slot, dc * 128:(dc + 1) * 128],
                   rhs=x0bf[:, dc * 512:dc * 512 + TT], start=(dc == 0),
                   stop=(dc == DC - 1), waits=wts, mark=(dc == DC - 1))
            bufs["wqb"][slot].consumed(S.e_pe, S.e_pe.n)
            if qk_pend:
                qknorm(*qk_pend.pop(0))
            qk_pend.append((bi, qhat[:, oc * 512:(oc + 1) * 512], qs.ap()[:],
                            True, oc % 2, S.e_pe.n))
        qknorm(*qk_pend.pop(0))
        QHAT_DONE = S.e_dve.n

        def take2(i):
            return bufs["P"][i].take() + bufs["P"][i + 1].take()

        def cons2(i, sem, cnt):
            bufs["P"][i].consumed(sem, cnt)
            bufs["P"][i + 1].consumed(sem, cnt)

        arot = [0]

        def build_A(h):
            par = h % 2
            stt = stt_t[:, par, :]
            sttl = sttl_t[:, par, :]
            sttw = sttw_t[:, par, :]
            sttlf = sttlf_t[:, par, :]
            stt2 = stt2_t[:, par, :]
            kv = h // 4
            units = []
            for qc in range(4):
                plan = []
                sti = 0
                while sti + 1 < t:
                    plan.append(("pair", sti))
                    sti += 2
                if sti < t:
                    plan.append(("one", sti))
                plan.append(("diag", t))
                nu = len(plan)

                def unit(kind, sti, u, qc=qc):
                    def f():
                        pr.phase = "attnA"
                        base = 0 if (arot[0] % 2 == 0) else 4
                        arot[0] += 1
                        wb = PAw if base == 0 else PCw
                        wts = take2(base) + [(S.e_dve, max(KHAT_DONE, QHAT_DONE))]
                        if kind == "pair":
                            pe("matmul", out=wb[:, 0:512],
                               lhsT=qhat[:, h * 512 + qc * 128:h * 512 + (qc + 1) * 128],
                               rhs=khat[:, kv * NTOK + sti * 512:kv * NTOK + (sti + 1) * 512],
                               start=True, stop=True, waits=wts)
                            n = pe("matmul", out=wb[:, 512:1024],
                                   lhsT=qhat[:, h * 512 + qc * 128:h * 512 + (qc + 1) * 128],
                                   rhs=khat[:, kv * NTOK + (sti + 1) * 512:kv * NTOK + (sti + 2) * 512],
                                   start=True, stop=True)
                            na = 1024
                        else:
                            na = 512 if kind == "one" else (qc + 1) * 128
                            n = pe("matmul", out=wb[:, 0:na],
                                   lhsT=qhat[:, h * 512 + qc * 128:h * 512 + (qc + 1) * 128],
                                   rhs=khat[:, kv * NTOK + sti * 512:kv * NTOK + sti * 512 + na],
                                   start=True, stop=True, waits=wts)
                        if kind == "diag":
                            cmask = dve("tensor_tensor", out=wb[:, 0:na], in0=wb[:, 0:na],
                                        in1=mka.ap()[:, qc * 512:qc * 512 + na], op=ALU.add,
                                        waits=[(S.e_pe, n), CONSTW])
                            wmax = [(S.e_dve, cmask)]
                        else:
                            wmax = [(S.e_pe, n)]
                        if u == 0:
                            wmax = wmax + [(S.e_act, S.e_act.n)]
                        cm = dve("tensor_reduce", out=stt[:, u:u + 1], in_=wb[:, 0:na],
                                 axis=AX.X, op=ALU.max, negate=True, waits=wmax)
                        ce = act("activation", out=dump[:, 0:na], in_=wb[:, 0:na],
                                 func=AF.Exp, bias=stt[:, u:u + 1],
                                 accum_out=sttl[:, u:u + 1],
                                 waits=[(S.e_dve, cm)])
                        cons2(base, S.e_act, ce)
                    return f
                for u, (kind, sti) in enumerate(plan):
                    units.append(unit(kind, sti, u))

                def combine(nu=nu, qc=qc):
                    def f():
                        pr.phase = "attnC"
                        w2 = bufs["stt2"][par].take() if qc == 0 else []
                        act("copy", out=sttlf[:, 0:nu], in_=sttl[:, 0:nu])
                        if nu == 1:
                            cl = act("activation", out=sttl[:, 5:6], in_=sttl[:, 0:1], func=AF.Ln)
                            dve("tensor_sub", out=stt2[:, qc:qc + 1], in0=sttl[:, 5:6],
                                in1=stt[:, 0:1], waits=[(S.e_act, cl)] + w2)
                        else:
                            c1 = dve("tensor_reduce", out=stt[:, 4:5], in_=stt[:, 0:nu],
                                     axis=AX.X, op=ALU.min,
                                     waits=[(S.e_act, S.e_act.n), (S.e_dve, S.e_dve.n)])
                            c2 = act("activation", out=sttw[:, 0:nu], in_=stt[:, 0:nu],
                                     func=AF.Exp, scale=-1.0, bias=stt[:, 4:5],
                                     waits=[(S.e_dve, c1)])
                            c3 = dve("tensor_mul", out=sttw[:, 0:nu], in0=sttw[:, 0:nu],
                                     in1=sttlf[:, 0:nu], waits=[(S.e_act, c2)])
                            dve("tensor_reduce", out=sttl[:, 4:5], in_=sttw[:, 0:nu],
                                axis=AX.X, op=ALU.add, waits=[(S.e_dve, S.e_dve.n)])
                            cl = act("activation", out=sttl[:, 5:6], in_=sttl[:, 4:5], func=AF.Ln,
                                     waits=[(S.e_dve, S.e_dve.n)])
                            dve("tensor_sub", out=stt2[:, qc:qc + 1], in0=sttl[:, 5:6],
                                in1=stt[:, 4:5],
                                waits=[(S.e_act, cl)] + w2 + [(S.e_dve, S.e_dve.n)])
                    return f
                units.append(combine())
            return units

        def build_B(h, STT2):
            par = h % 2
            stt2 = stt2_t[:, par, :]
            kv = h // 4
            units = []

            def stats():
                pr.phase = "attnB"
                n = pe("transpose", out=Pap[6][0:4, 0:128], in_=stt2[:, 0:4],
                       identity=identf.ap()[:],
                       waits=[(S.e_dve, STT2), CONSTW] + bufs["P"][6].take())
                bufs["stt2"][par].consumed(S.e_pe, n)
                c4 = dve("tensor_copy", out=st4, in_=Pap[6][0:4, 0:128],
                         waits=[(S.e_pe, n)] + bufs["st4"].take())
                bufs["P"][6].consumed(S.e_dve, c4)
                nrow = dma("sync", strow, st4, S.s_stat,
                           waits=bufs["strow"].take() + [(S.e_dve, c4)])
                bufs["st4"].consumed(S.s_stat, nrow)
                n = pe("matmul", out=Pap[7], lhsT=ones1f.ap()[:], rhs=strow,
                       start=True, stop=True,
                       waits=[(S.s_stat, nrow), (S.e_dve, MEMSET)] + bufs["P"][7].take())
                bufs["strow"].consumed(S.e_pe, n)
                c5 = dve("tensor_copy", out=scrA, in_=Pap[7],
                         waits=bufs["scrA"].take() + [(S.e_pe, n)])
                bufs["P"][7].consumed(S.e_dve, c5)
            units.append(stats)

            n_sc = 4 * (t + 1)
            plan = []
            sc = 0
            while sc < n_sc:
                if sc + 1 < 4 * t:
                    plan.append(("pair", sc)); sc += 2
                else:
                    plan.append(("one", sc)); sc += 1

            def bunit(kind, sc, it, last):
                def f():
                    pr.phase = "attnB"
                    sslot = it % 2
                    if kind == "pair":
                        w0 = take2(2)
                        pe("matmul", out=Pap[2],
                           lhsT=khat[:, kv * NTOK + sc * 128:kv * NTOK + (sc + 1) * 128],
                           rhs=qhat[:, h * 512:(h + 1) * 512],
                           start=True, stop=True, waits=w0)
                        n = pe("matmul", out=Pap[3],
                               lhsT=khat[:, kv * NTOK + (sc + 1) * 128:kv * NTOK + (sc + 2) * 128],
                               rhs=qhat[:, h * 512:(h + 1) * 512],
                               start=True, stop=True)
                        wts = bufs["scrB"][sslot].take() + [(S.e_pe, n)]
                        csub = dve("tensor_tensor",
                                   out=scrB[:, sslot, :].rearrange("p (a b) -> p a b", a=2),
                                   in0=PBw.rearrange("p (a b) -> p a b", a=2),
                                   in1=scrA.unsqueeze(1).broadcast_to([128, 2, 512]),
                                   op=ALU.subtract, waits=wts)
                        bufs["scrA"].consumed(S.e_dve, csub)
                        cons2(2, S.e_dve, csub)
                        cexp = act("activation", out=ptb[:, sslot, :], in_=scrB[:, sslot, :],
                                   func=AF.Exp,
                                   waits=bufs["ptb"][sslot].take() + [(S.e_dve, csub)])
                        bufs["scrB"][sslot].consumed(S.e_act, cexp)
                        wts = [(S.e_act, cexp), (S.e_dve, V_DONE)]
                        if sc == 0:
                            wts += bufs["P"][6].take()
                        pe("matmul", out=Pap[6],
                           lhsT=vsb[:, sc * 512 + kv * 128:sc * 512 + (kv + 1) * 128],
                           rhs=ptb[:, sslot, 0:512], start=(sc == 0), stop=False,
                           waits=wts)
                        n = pe("matmul", out=Pap[6],
                               lhsT=vsb[:, (sc + 1) * 512 + kv * 128:(sc + 1) * 512 + (kv + 1) * 128],
                               rhs=ptb[:, sslot, 512:1024], start=False, stop=last)
                        bufs["ptb"][sslot].consumed(S.e_pe, n)
                    else:
                        qlo = 0 if sc < 4 * t else (sc - 4 * t) * 128
                        bi = 2 + (it % 2)
                        n = pe("matmul", out=Pap[bi][:, qlo:512],
                               lhsT=khat[:, kv * NTOK + sc * 128:kv * NTOK + (sc + 1) * 128],
                               rhs=qhat[:, h * 512 + qlo:(h + 1) * 512],
                               start=True, stop=True, waits=bufs["P"][bi].take())
                        wts = bufs["scrB"][sslot].take() + [(S.e_pe, n)]
                        csub = dve("tensor_tensor", out=scrB[:, sslot, qlo:512],
                                   in0=Pap[bi][:, qlo:512],
                                   in1=scrA[:, qlo:512], op=ALU.subtract, waits=wts)
                        bufs["scrA"].consumed(S.e_dve, csub)
                        bufs["P"][bi].consumed(S.e_dve, csub)
                        if sc >= 4 * t:
                            csub = dve("tensor_tensor", out=scrB[:, sslot, qlo:512],
                                       in0=scrB[:, sslot, qlo:512],
                                       in1=mkb.ap()[:, (sc - 4 * t) * 512 + qlo:(sc - 4 * t + 1) * 512],
                                       op=ALU.add, waits=[CONSTW])
                        cexp = act("activation", out=ptb[:, sslot, qlo:512],
                                   in_=scrB[:, sslot, qlo:512], func=AF.Exp,
                                   waits=bufs["ptb"][sslot].take() + [(S.e_dve, csub)])
                        bufs["scrB"][sslot].consumed(S.e_act, cexp)
                        wts = [(S.e_act, cexp), (S.e_dve, V_DONE)]
                        if sc == 0:
                            wts += bufs["P"][6].take()
                        n = pe("matmul", out=Pap[6][:, qlo:512],
                               lhsT=vsb[:, sc * 512 + kv * 128:sc * 512 + (kv + 1) * 128],
                               rhs=ptb[:, sslot, qlo:512], start=(sc == 0), stop=last,
                               waits=wts)
                        bufs["ptb"][sslot].consumed(S.e_pe, n)
                return f
            for it, (kind, sc) in enumerate(plan):
                units.append(bunit(kind, sc, it, it == len(plan) - 1))

            def evac():
                co = act("copy", out=ocat[:, h * 512:(h + 1) * 512], in_=Pap[6],
                         waits=[(S.e_pe, S.e_pe.n)])
                bufs["P"][6].consumed(S.e_act, co)
            units.append(evac)
            return units

        def zip_emit(A, Bu):
            na, nb = len(A), len(Bu)
            ia = ib = 0
            while ia < na or ib < nb:
                if ib < nb and (ia >= na or ib * na <= ia * nb):
                    Bu[ib](); ib += 1
                else:
                    A[ia](); ia += 1

        pend = None
        for h in range(NH):
            A = build_A(h)
            if pend is None:
                for f in A:
                    f()
            else:
                zip_emit(A, build_B(*pend))
            pend = (h, S.e_dve.n)
        for f in build_B(*pend):
            f()
        OCAT_DONE = S.e_act.n
        pr.phase = "wo"

        def res_load(mc):
            rslot = mc % 4
            nres = dma("sync", x0res[:, rslot, :],
                       xT[s][mc * 128:(mc + 1) * 128, t0:t0 + TT], RS["res"][rslot],
                       waits=bufs["x0res"][rslot].take())
            return rslot, nres
        res_info = {mc: res_load(mc) for mc in range(2)}
        ones_q = []

        def ones_mm(mc):
            (csq, qslot) = ones_q.pop(0)
            wts = [(S.e_dve, csq)]
            if mc == 0:
                wts += bufs["P"][4].take() + bufs["P"][5].take()
            pe("matmul", out=Pap[4][0:1, :], lhsT=onesbf.ap()[:],
               rhs=r1v[:, mc * 512:(mc + 1) * 512], start=(mc == 0),
               stop=(mc == DC - 1), waits=wts, mark=(mc == DC - 1))
            nss = pe("matmul", out=Pap[5][0:1, :], lhsT=onesbf.ap()[:],
                     rhs=sqb[:, qslot, :], start=(mc == 0), stop=(mc == DC - 1),
                     mark=True)
            bufs["sqb"][qslot].consumed(S.e_pe, nss)

        for mc in range(DC):
            slot, wql = load_wq("o", mc, first_tile)
            bi = mc % 2
            for cc in range(DC):
                wts = ()
                if cc == 0:
                    wts = [wql,
                           (S.e_act, OCAT_DONE)] + bufs["P"][bi].take()
                pe("matmul", out=Pap[bi], lhsT=wqb[:, slot, cc * 128:(cc + 1) * 128],
                   rhs=ocat[:, cc * 512:(cc + 1) * 512], start=(cc == 0),
                   stop=(cc == DC - 1), waits=wts, mark=(cc == DC - 1))
            MMN = S.e_pe.n
            bufs["wqb"][slot].consumed(S.e_pe, MMN)
            if mc + 2 < DC:
                res_info[mc + 2] = res_load(mc + 2)
            if mc >= 1:
                ones_mm(mc - 1)
            rslot, nres = res_info.pop(mc)
            cr = dve("tensor_tensor", out=r1v[:, mc * 512:(mc + 1) * 512], in0=Pap[bi],
                     in1=x0res[:, rslot, :], op=ALU.add,
                     waits=[(S.e_pe, MMN), (RS["res"][rslot], nres)])
            bufs["P"][bi].consumed(S.e_dve, cr)
            bufs["x0res"][rslot].consumed(S.e_dve, cr)
            qslot = mc % 2
            csq = dve("tensor_mul", out=sqb[:, qslot, :],
                      in0=r1v[:, mc * 512:(mc + 1) * 512],
                      in1=r1v[:, mc * 512:(mc + 1) * 512],
                      waits=bufs["sqb"][qslot].take())
            ones_q.append((csq, qslot))
        ones_mm(DC - 1)
        ln_apply(4, 5, r1v, x1bf, None, S.e_pe.n)
        if post_attn_hook is not None:
            post_attn_hook()

        pr.phase = "ffn"
        first = (s == 0 and t == 0)
        for ic in range(ICN):
            gslot = ic % 2
            w = bufs["wgb"][gslot].take()
            if first:
                w += [pw_wait(2 * ic)]
            ng = dma("sync", wgb[:, gslot, :], gs_s[ic], RS["g"][gslot], waits=w)
            w = bufs["wub"][gslot].take()
            if first:
                w += [pw_wait(2 * ic + 1)]
            nu = dma("sync", wub[:, gslot, :], us_s[ic], RS["u"][gslot], waits=w)
            bg, bu = gslot, 2 + gslot
            for hc in range(HCN):
                wts = ()
                if hc == 0:
                    wts = [(RS["g"][gslot], ng), (S.e_dve, X1_DONE[0])] + bufs["P"][bg].take()
                pe("matmul", out=Pap[bg], lhsT=wgb[:, gslot, hc * 128:(hc + 1) * 128],
                   rhs=x1bf[:, hc * 512:(hc + 1) * 512], start=(hc == 0),
                   stop=(hc == HCN - 1), waits=wts, mark=(hc == HCN - 1))
            bufs["wgb"][gslot].consumed(S.e_pe, S.e_pe.n)
            for hc in range(HCN):
                wts = ()
                if hc == 0:
                    wts = [(RS["u"][gslot], nu)] + bufs["P"][bu].take()
                pe("matmul", out=Pap[bu], lhsT=wub[:, gslot, hc * 128:(hc + 1) * 128],
                   rhs=x1bf[:, hc * 512:(hc + 1) * 512], start=(hc == 0),
                   stop=(hc == HCN - 1), waits=wts, mark=(hc == HCN - 1))
            NU = S.e_pe.n
            bufs["wub"][gslot].consumed(S.e_pe, NU)
            sslot = ic % 2
            cs = act("activation", out=scrB[:, sslot, 0:512], in_=Pap[bg], func=AF.Silu,
                     scale=gsc.ap()[:, ic:ic + 1],
                     waits=bufs["scrB"][sslot].take() + [(S.e_pe, NU), CONSTW])
            bufs["P"][bg].consumed(S.e_act, cs)
            ch = dve("scalar_tensor_tensor", out=hact[:, ic * 512:(ic + 1) * 512],
                     in0=Pap[bu], scalar=usc.ap()[:, ic:ic + 1], in1=scrB[:, sslot, 0:512],
                     op0=ALU.mult, op1=ALU.mult, waits=[(S.e_act, cs)])
            bufs["P"][bu].consumed(S.e_dve, ch)
            bufs["scrB"][sslot].consumed(S.e_dve, ch)
        HACT_LAST = S.e_dve.n
        ones2_q = []

        def ones2_mm():
            (csq, qslot, hc2) = ones2_q.pop(0)
            wts = [(S.e_dve, csq)]
            if hc2 == 0:
                wts += bufs["P"][6].take() + bufs["P"][7].take()
            pe("matmul", out=Pap[6][0:1, :], lhsT=onesbf.ap()[:],
               rhs=r2v[:, hc2 * 512:(hc2 + 1) * 512], start=(hc2 == 0),
               stop=(hc2 == HCN - 1), waits=wts, mark=(hc2 == HCN - 1))
            pe("matmul", out=Pap[7][0:1, :], lhsT=onesbf.ap()[:], rhs=sqb[:, qslot, :],
               start=(hc2 == 0), stop=(hc2 == HCN - 1), mark=True)
            bufs["sqb"][qslot].consumed(S.e_pe, S.e_pe.n)

        for hc in range(HCN):
            bd = 4 + (hc % 2)
            for qtr in range(4):
                dslot = (hc * 4 + qtr) % 2
                w = bufs["wdb"][dslot].take()
                if first:
                    w += [pw_wait(2 * ICN + DC + hc * 4 + qtr)]
                nd = dma("sync", wdb[:, dslot, :],
                         ds_s[hc][:, qtr * 2048:(qtr + 1) * 2048], RS["d"][dslot], waits=w)
                for ih in range(16):
                    ic = qtr * 16 + ih
                    wts = []
                    if ic == 0:
                        wts += [(S.e_dve, HACT_LAST)] + bufs["P"][bd].take()
                    if ih == 0:
                        wts += [(RS["d"][dslot], nd)]
                    pe("matmul", out=Pap[bd], lhsT=wdb[:, dslot, ih * 128:(ih + 1) * 128],
                       rhs=hact[:, ic * 512:(ic + 1) * 512], start=(ic == 0),
                       stop=(ic == ICN - 1), waits=wts, mark=(ih == 15))
                bufs["wdb"][dslot].consumed(S.e_pe, S.e_pe.n)
            MMN = S.e_pe.n
            wts = [(S.e_pe, MMN), CONSTW]
            if first and hc >= 8:
                wts += [(RS["pwo"][0], PW_DONE[0]), (RS["pwo"][1], PW_DONE[1])]
            cr = dve("scalar_tensor_tensor", out=r2v[:, hc * 512:(hc + 1) * 512],
                     in0=Pap[bd], scalar=dsc.ap()[:, hc:hc + 1],
                     in1=x1bf[:, hc * 512:(hc + 1) * 512],
                     op0=ALU.mult, op1=ALU.add, waits=wts)
            bufs["P"][bd].consumed(S.e_dve, cr)
            qslot = hc % 2
            csq = dve("tensor_mul", out=sqb[:, qslot, :],
                      in0=r2v[:, hc * 512:(hc + 1) * 512],
                      in1=r2v[:, hc * 512:(hc + 1) * 512],
                      waits=bufs["sqb"][qslot].take())
            ones2_q.append((csq, qslot, hc))
            if hc >= 1:
                ones2_mm()
        ones2_mm()
        bufs["arena"].consumed(S.e_pe, S.e_pe.n)
        if nxt is not None:
            emit_cast_x0(*nxt)
        ln_apply(6, 7, r2v, None, (s, t0), S.e_pe.n)

    tiles = [(s, t) for s in range(n_seq) for t in range(n_tiles)]
    emit_cast_x0(*tiles[0], first_tile=True)
    for i, (s, t) in enumerate(tiles):
        nxt = tiles[i + 1] if i + 1 < len(tiles) else None
        emit_tile(s, t, nxt, post_attn_hook=emit_phasew if i == 0 else None)
    pr.rec("sync", "wait_ge", RS["out"][0].h, RS["out"][0].n)
    pr.rec("sync", "wait_ge", RS["out"][1].h, RS["out"][1].n)
    pr.replay()
    nc._prog = pr
    return nc



def _prep(inputs):
    f32 = np.float32
    Wq = np.asarray(inputs["L0_attn_Wq"], f32)
    Wk = np.asarray(inputs["L0_attn_Wk"], f32)
    Wv = np.asarray(inputs["L0_attn_Wv"], f32)
    Wo = np.asarray(inputs["L0_attn_Wo"], f32)
    gw = np.asarray(inputs["L0_ffn_gate_w"], np.int32)
    uw = np.asarray(inputs["L0_ffn_up_w"], np.int32)
    dw = np.asarray(inputs["L0_ffn_down_w"], np.int32)

    def wtile(W, n_out):
        r = W.reshape(n_out, 128, DC, 128)
        return np.ascontiguousarray(r.transpose(0, 3, 2, 1)).reshape(n_out, 128, D)

    d = {}
    d["wq_t"] = wtile(Wq, NH)
    d["wk_t"] = wtile(Wk, NKV)
    d["wo_t"] = wtile(Wo, DC)
    d["wv_t"] = np.ascontiguousarray(Wv.T.reshape(DC, 128, 512))
    d["g_t"] = np.ascontiguousarray(
        gw.reshape(ICN, 128, DC, 128).transpose(0, 3, 2, 1)).reshape(ICN, 128, D)
    d["u_t"] = np.ascontiguousarray(
        uw.reshape(ICN, 128, DC, 128).transpose(0, 3, 2, 1)).reshape(ICN, 128, D)
    d["d_t"] = np.ascontiguousarray(
        dw.reshape(HCN, 128, ICN, 128).transpose(0, 3, 2, 1)).reshape(HCN, 128, II)
    d["qs"] = np.ascontiguousarray(np.asarray(inputs["L0_attn_q_scale"], f32).reshape(128, 1))
    d["ks"] = np.ascontiguousarray(np.asarray(inputs["L0_attn_k_scale"], f32).reshape(128, 1))
    d["gs"] = np.ascontiguousarray(np.asarray(inputs["L0_ffn_gate_scale"], f32).reshape(ICN, 128).T)
    d["us"] = np.ascontiguousarray(np.asarray(inputs["L0_ffn_up_scale"], f32).reshape(ICN, 128).T)
    d["ds"] = np.ascontiguousarray(np.asarray(inputs["L0_ffn_down_scale"], f32).reshape(HCN, 128).T)
    i_idx = np.arange(128)[:, None]
    j_idx = np.arange(512)[None, :]
    d["mka"] = np.concatenate([np.where(j_idx <= qc * 128 + i_idx, 0.0, NEG)
                               for qc in range(4)], axis=1).astype(ml_dtypes.bfloat16)
    d["mkb"] = np.concatenate([np.where(p * 128 + i_idx <= j_idx, 0.0, NEG)
                               for p in range(4)], axis=1).astype(ml_dtypes.bfloat16)
    d["idn"] = np.eye(128, dtype=f32)
    return d


def kernel(**inputs):
    x = np.asarray(inputs["L0_attn_x"], np.float32)
    shared = _prep(inputs)
    nc = build_core_program()
    in_maps = []
    for c in range(NCORE):
        m = dict(shared)
        m["xT"] = np.ascontiguousarray(x[2 * c:2 * c + 2].transpose(0, 2, 1))
        in_maps.append(m)
    res = run_bass_kernel_spmd(nc, in_maps, core_ids=list(range(NCORE)))
    out = np.empty((B, T, D), np.float32)
    for c in range(NCORE):
        o = np.asarray(res.results[c]["outT"], np.float32)
        out[2 * c:2 * c + 2] = o.transpose(0, 2, 1)
    return out


# version 09
# speedup vs baseline: 1.1144x; 1.0640x over previous
import os
import sys
import numpy as np
import ml_dtypes

try:
    import concourse.bass as bass
except ImportError:
    sys.path.insert(0, "/opt/trn_rl_repo")
    import concourse.bass as bass
import concourse.mybir as mybir
from concourse.bass_utils import run_bass_kernel_spmd

F32 = mybir.dt.float32
BF16 = mybir.dt.bfloat16
I32 = mybir.dt.int32
AF = mybir.ActivationFunctionType
ALU = mybir.AluOpType
AX = mybir.AxisListType

B, T, D = 16, 2048, 2048
NH, NKV, HD = 16, 4, 128
II = 8192
TT = 512
DC = D // 128
ICN = II // 128
HCN = DC
NCORE = 8
SER = {"c4", "c6"}
NEG = -30000.0
EPS_QK = 1e-6
EPS_LN = 1e-5


class SemC:
    def __init__(self, nc, name):
        self.h = nc.alloc_semaphore(name)
        self.n = 0
        self.name = name


class Buf:
    def __init__(self):
        self.need = {}

    def consumed(self, sem, cnt):
        if sem is not None and (sem not in self.need or self.need[sem] < cnt):
            self.need[sem] = cnt

    def take(self):
        w = list(self.need.items())
        self.need = {}
        return w


class Prog:
    def __init__(self, nc, paranoid=False):
        self.nc = nc
        self.ops = []
        self.labels = []
        self.phase = ""
        self.label_of = {}
        if paranoid is True:
            paranoid = {"sync", "scalar", "vector", "tensor", "gpsimd"}
        self.paranoid = paranoid or set()
        if self.paranoid:
            self.pp = {e: SemC(nc, "pp_" + e) for e in self.paranoid}

    def rec(self, eng, fn, *args, waits=(), inc=None, **kwargs):
        ws = [(s, c) for (s, c) in waits if s is not None]
        if inc is not None:
            inc[0].n += inc[1]
        self.ops.append((eng, fn, args, kwargs, ws, inc, self.phase))
        self.labels.append(self.phase + "/" + fn)

    def replay(self):
        hw = {}
        dep = mybir.DependencyInfo(mybir.DependencyClass.NoSync,
                                   mybir.DependencyReason.DATA)

        def run_engine(engobj, engname):
            prev = None
            prev_done = None
            rn = {}
            for oi, (eng, fn, args, kwargs, ws, inc, phase) in enumerate(self.ops):
                if eng != engname:
                    continue
                lbl = self.labels[oi]
                par = engname in self.paranoid or (engname + ":" + phase) in self.paranoid
                if par and fn != "wait_ge" and prev_done is not None:
                    engobj.wait_ge(prev_done[0].h, prev_done[1])
                for (sem, cnt) in ws:
                    key = (engname, sem.name)
                    if hw.get(key, -1) >= cnt:
                        continue
                    hw[key] = cnt
                    engobj.wait_ge(sem.h, cnt)
                ins = getattr(engobj, fn)(*args, **kwargs)
                self.label_of[ins.ins.name] = lbl
                if inc is not None:
                    ins.then_inc(inc[0].h, inc[1])
                if par and fn != "wait_ge":
                    if inc is not None:
                        rn[inc[0].name] = rn.get(inc[0].name, 0) + inc[1]
                        prev_done = (inc[0], rn[inc[0].name])
                    else:
                        amt = 16 if fn == "dma_start" else 1
                        pp = self.pp[engname]
                        pp.n += amt
                        ins.then_inc(pp.h, amt)
                        prev_done = (pp, pp.n)
                if prev is not None:
                    try:
                        ins.ins.add_dependency(prev, dep)
                    except Exception:
                        pass
                prev = ins.ins.name

        with self.nc.Block() as block:
            block.sync(lambda e: run_engine(e, "sync"))
            block.scalar(lambda e: run_engine(e, "scalar"))
            block.vector(lambda e: run_engine(e, "vector"))
            block.tensor(lambda e: run_engine(e, "tensor"))
            block.gpsimd(lambda e: run_engine(e, "gpsimd"))


def build_core_program(n_seq=2, n_tiles=T // TT, paranoid=False):
    nc = bass.Bass("TRN2", target_bir_lowering=False)
    NTOK = n_tiles * TT

    xT = nc.dram_tensor("xT", [n_seq, D, NTOK], F32, kind="ExternalInput").ap()
    wq_t = nc.dram_tensor("wq_t", [NH, 128, D], F32, kind="ExternalInput").ap()
    wk_t = nc.dram_tensor("wk_t", [NKV, 128, D], F32, kind="ExternalInput").ap()
    wv_t = nc.dram_tensor("wv_t", [DC, 128, 512], F32, kind="ExternalInput").ap()
    wo_t = nc.dram_tensor("wo_t", [DC, 128, D], F32, kind="ExternalInput").ap()
    g_t = nc.dram_tensor("g_t", [ICN, 128, D], I32, kind="ExternalInput").ap()
    u_t = nc.dram_tensor("u_t", [ICN, 128, D], I32, kind="ExternalInput").ap()
    d_t = nc.dram_tensor("d_t", [HCN, 128, II], I32, kind="ExternalInput").ap()
    qs_d = nc.dram_tensor("qs", [128, 1], F32, kind="ExternalInput").ap()
    ks_d = nc.dram_tensor("ks", [128, 1], F32, kind="ExternalInput").ap()
    gs_d = nc.dram_tensor("gs", [128, ICN], F32, kind="ExternalInput").ap()
    us_d = nc.dram_tensor("us", [128, ICN], F32, kind="ExternalInput").ap()
    ds_d = nc.dram_tensor("ds", [128, HCN], F32, kind="ExternalInput").ap()
    mka_d = nc.dram_tensor("mka", [128, 4 * 512], BF16, kind="ExternalInput").ap()
    mkb_d = nc.dram_tensor("mkb", [128, 4 * 512], BF16, kind="ExternalInput").ap()
    idn_d = nc.dram_tensor("idn", [128, 128], F32, kind="ExternalInput").ap()
    outT = nc.dram_tensor("outT", [n_seq, D, NTOK], F32, kind="ExternalOutput").ap()
    gs_s = nc.dram_tensor("gs_s", [ICN, 128, D], BF16, kind="Internal").ap()
    us_s = nc.dram_tensor("us_s", [ICN, 128, D], BF16, kind="Internal").ap()
    ds_s = nc.dram_tensor("ds_s", [HCN, 128, II], BF16, kind="Internal").ap()
    wq_s = nc.dram_tensor("wq_s", [NH, 128, D], BF16, kind="Internal").ap()
    wk_s = nc.dram_tensor("wk_s", [NKV, 128, D], BF16, kind="Internal").ap()
    wv_s = nc.dram_tensor("wv_s", [DC, 128, 512], BF16, kind="Internal").ap()
    wo_s = nc.dram_tensor("wo_s", [DC, 128, D], BF16, kind="Internal").ap()
    xs_s = nc.dram_tensor("xs_s", [n_seq, D, NTOK], BF16, kind="Internal").ap()

    A = nc.alloc_sbuf_tensor
    identf = A("identf", [128, 128], F32)
    onesbf = A("onesbf", [128, 1], BF16)
    ones1f = A("ones1f", [1, 128], F32)
    ones1bf = A("ones1bf", [1, 128], BF16)
    qs = A("qs_s", [128, 1], F32)
    ks = A("ks_s", [128, 1], F32)
    gsc = A("gsc", [128, ICN], F32)
    usc = A("usc", [128, ICN], F32)
    dsc = A("dsc", [128, HCN], F32)
    mka = A("mka_s", [128, 4 * 512], BF16)
    mkb = A("mkb_s", [128, 4 * 512], BF16)
    khat_t = A("khat", [128, NKV * NTOK], BF16)
    vsb_t = A("vsb", [128, (NTOK // 128) * 512], BF16)
    arena_t = A("arena", [128, ICN * 512], BF16)
    khat = khat_t.ap()
    vsb = vsb_t.ap()
    arena = arena_t.ap()
    x0bf = arena[:, 0:DC * 512]
    qhat = arena[:, DC * 512:2 * DC * 512]
    ocat = arena[:, 2 * DC * 512:3 * DC * 512]
    r1v = arena[:, 3 * DC * 512:4 * DC * 512]
    hact = arena
    x1bf = A("x1bf", [128, DC * 512], BF16).ap()
    r2v = A("r2v", [128, DC * 512], BF16).ap()
    sqb = A("sqb", [128, 2, 512], BF16).ap()
    wqb = A("wqb", [128, 2, DC * 128], BF16).ap()
    wvb = A("wvb", [128, 2, 512], BF16).ap()
    wgb = A("wgb", [128, 2, HCN * 128], BF16).ap()
    wub = A("wub", [128, 2, HCN * 128], BF16).ap()
    wdb = A("wdb", [128, 2, 16 * 128], BF16).ap()
    scrA = A("scrA", [128, 512], F32).ap()
    scrB = A("scrB", [128, 2, 1024], F32).ap()
    dump = A("dump", [128, 1024], BF16).ap()
    ptb = A("ptb", [128, 2, 1024], BF16).ap()
    stt_t = A("stt", [128, 2, 8], F32).ap()
    sttl_t = A("sttl", [128, 2, 8], F32).ap()
    sttw_t = A("sttw", [128, 2, 4], F32).ap()
    sttlf_t = A("sttlf", [128, 2, 4], F32).ap()
    stt2_t = A("stt2", [128, 2, 4], F32).ap()
    st4 = A("st4", [4, 128], F32).ap()
    strow = A("strow", [1, 512], F32).ap()
    rows = A("rows", [1, 2, 512], F32).ap()
    rstdr = A("rstdr", [1, 2, 512], BF16).ap()
    x0res = A("x0res", [128, 4, 512], F32).ap()
    constt = A("constt", [128, 8], F32).ap()
    if os.environ.get("KSBUF"):
        print("sbuf remaining bytes/partition:", nc.sbuf_bytes_remaining)
    assert nc.sbuf_bytes_remaining >= 0, nc.sbuf_bytes_remaining

    PPAIR = [nc.alloc_psum_tensor(f"PP{i}", [128, 1024], F32) for i in range(4)]
    PAw, PBw, PCw, PDw = [p.ap() for p in PPAIR]
    P23w = PBw
    Pap = [PAw[:, 0:512], PAw[:, 512:1024], PBw[:, 0:512], PBw[:, 512:1024],
           PCw[:, 0:512], PCw[:, 512:1024], PDw[:, 0:512], PDw[:, 512:1024]]

    pr = Prog(nc, paranoid=paranoid)
    sems = {}
    for name in ("s_const", "s_stat", "g_x0", "e_pe", "e_dve", "e_act", "e_pool"):
        sems[name] = SemC(nc, name)
    for base in ("wq", "wv", "g", "u", "d", "out", "wbg", "wbu", "wbd",
                 "wqg", "wvg", "gg", "ug", "dg"):
        for j in range(2):
            sems[base + str(j)] = SemC(nc, base + str(j))
    for j in range(4):
        sems["res" + str(j)] = SemC(nc, "res" + str(j))
    sems["g_cv"] = SemC(nc, "g_cv")
    sems["s_x0"] = SemC(nc, "s_x0")
    S = type("S", (), sems)
    RS = {b: (sems[b + "0"], sems[b + "1"]) for b in
          ("wq", "wv", "g", "u", "d", "out", "wbg", "wbu", "wbd", "wqg", "wvg",
           "gg", "ug", "dg")}
    RS["res"] = tuple(sems["res" + str(j)] for j in range(4))

    bufs = {
        "wqb": [Buf(), Buf()], "wvb": [Buf(), Buf()], "wgb": [Buf(), Buf()],
        "wub": [Buf(), Buf()], "wdb": [Buf(), Buf()], "x0res": [Buf(), Buf(), Buf(), Buf()],
        "sqb": [Buf(), Buf()], "scrB": [Buf(), Buf()],
        "ptb": [Buf(), Buf()],
        "arena": Buf(), "strow": Buf(), "scrA": Buf(), "st4": Buf(),
        "stt2": [Buf(), Buf()],
        "P": [Buf() for _ in range(8)],
    }

    def dve(fn, *a, waits=(), **k):
        pr.rec("vector", fn, *a, waits=waits, inc=(S.e_dve, 1), **k)
        return S.e_dve.n

    def act(fn, *a, waits=(), **k):
        pr.rec("scalar", fn, *a, waits=waits, inc=(S.e_act, 1), **k)
        return S.e_act.n

    def pool(fn, *a, waits=(), **k):
        pr.rec("gpsimd", fn, *a, waits=waits, inc=(S.e_pool, 1), **k)
        return S.e_pool.n

    def pe(fn, *a, waits=(), mark=True, **k):
        pr.rec("tensor", fn, *a, waits=waits, inc=(S.e_pe, 1) if mark else None, **k)
        return S.e_pe.n if mark else None

    def dma(queue, out, in_, sem, waits=()):
        pr.rec(queue, "dma_start", waits=waits, inc=(sem, 16), out=out, in_=in_)
        return sem.n

    for dst, src in ((qs, qs_d), (ks, ks_d), (gsc, gs_d), (usc, us_d), (dsc, ds_d),
                     (mka, mka_d), (mkb, mkb_d), (identf, idn_d)):
        dma("sync", dst.ap()[:], src, S.s_const)
    CONSTW = (S.s_const, S.s_const.n)
    dve("memset", onesbf.ap()[:], 1.0)
    dve("memset", ones1f.ap()[:], 1.0)
    dve("memset", ones1bf.ap()[:], 1.0)
    cvals = [EPS_QK, EPS_LN, -0.5 * float(np.log(128.0)), 1.0 / 128.0, -0.5, -1.0]
    for j, v in enumerate(cvals):
        dve("memset", constt[:, j:j + 1], v)
        nc.const_aps.register(F32, v, constt[:, j:j + 1])
    MEMSET = S.e_dve.n

    X1_DONE = [0]
    x0_info = [None]
    wq_i = [0]
    wv_i = [0]
    x0_i = [0]
    PW_TOTAL = 2 * ICN + HCN * 4

    CV_DONE = [0]

    def load_wq(kind, idx, first_tile):
        slot = wq_i[0] % 2
        wq_i[0] += 1
        w = bufs["wqb"][slot].take()
        if first_tile:
            srcap = {"k": wk_t, "q": wq_t, "o": wo_t}[kind][idx]
            cnt = dma("gpsimd", wqb[:, slot, :], srcap, RS["wqg"][slot], waits=w)
            return slot, (RS["wqg"][slot], cnt)
        else:
            srcap = {"k": wk_s, "q": wq_s, "o": wo_s}[kind][idx]
            w = w + [(S.g_cv, CV_DONE[0])]
            cnt = dma("sync", wqb[:, slot, :], srcap, RS["wq"][slot], waits=w)
        return slot, (RS["wq"][slot], cnt)

    def load_wv(dc, first_tile):
        slot = wv_i[0] % 2
        wv_i[0] += 1
        w = bufs["wvb"][slot].take()
        if first_tile:
            cnt = dma("gpsimd", wvb[:, slot, :], wv_t[dc], RS["wvg"][slot], waits=w)
            return slot, (RS["wvg"][slot], cnt)
        else:
            w = w + [(S.g_cv, CV_DONE[0])]
            cnt = dma("sync", wvb[:, slot, :], wv_s[dc], RS["wv"][slot], waits=w)
        return slot, (RS["wv"][slot], cnt)

    def emit_cast_x0(s, t, first_tile=False):
        t0 = t * TT
        w = bufs["arena"].take()
        if first_tile:
            src3 = xT[s].rearrange("(dc p) t -> p dc t", p=128)[:, :, t0:t0 + TT]
            cnt = dma("gpsimd", x0bf.rearrange("p (dc t) -> p dc t", dc=DC), src3,
                      S.g_x0, waits=w)
            x0_info[0] = (S.g_x0, cnt)
        else:
            src3 = xs_s[s].rearrange("(dc p) t -> p dc t", p=128)[:, :, t0:t0 + TT]
            w = w + [(S.g_cv, CV_DONE[0])]
            cnt = dma("sync", x0bf.rearrange("p (dc t) -> p dc t", dc=DC), src3,
                      S.s_x0, waits=w)
            x0_info[0] = (S.s_x0, cnt)
        x0_i[0] += 1

    def emit_phasew():
        for oc in range(NKV):
            dma("gpsimd", wk_s[oc], wk_t[oc], S.g_cv)
        for oc in range(NH):
            dma("gpsimd", wq_s[oc], wq_t[oc], S.g_cv)
        for mc in range(DC):
            dma("gpsimd", wo_s[mc], wo_t[mc], S.g_cv)
        for dc in range(DC):
            dma("gpsimd", wv_s[dc], wv_t[dc], S.g_cv)
        for si in range(n_seq):
            for dc in range(DC):
                dma("gpsimd", xs_s[si][dc * 128:(dc + 1) * 128, :],
                    xT[si][dc * 128:(dc + 1) * 128, :], S.g_cv)
        CV_DONE[0] = S.g_cv.n

    def qknorm(bi, out_slice, scale_vec, is_q, rst_slot, mm_n):
        c_copy = dve("tensor_copy", out=scrA, in_=Pap[bi],
                     waits=[(S.e_pe, mm_n)] + bufs["scrA"].take())
        bufs["P"][bi].consumed(S.e_dve, c_copy)
        c_sq = dve("tensor_mul", out=sqb[:, 0, :], in0=scrA, in1=scrA,
                   waits=bufs["sqb"][0].take())
        n = pe("matmul", out=Pap[6][0:1, :], lhsT=onesbf.ap()[:], rhs=sqb[:, 0, :],
               start=True, stop=True, waits=[(S.e_dve, c_sq)] + bufs["P"][6].take())
        bufs["sqb"][0].consumed(S.e_pe, n)
        c1 = act("activation", out=rows[0:1, 1, :], in_=Pap[6][0:1, :], func=AF.Ln,
                 scale=1.0 / 128.0, bias=EPS_QK, waits=[(S.e_pe, n)])
        bufs["P"][6].consumed(S.e_act, c1)
        bias2 = -0.5 * float(np.log(128.0)) if is_q else 0.0
        c2 = act("activation", out=rstdr[0:1, rst_slot, :], in_=rows[0:1, 1, :],
                 func=AF.Exp, scale=-0.5, bias=bias2)
        n2 = pe("matmul", out=Pap[7], lhsT=ones1bf.ap()[:],
                rhs=rstdr[0:1, rst_slot, :], start=True, stop=True,
                waits=[(S.e_act, c2)] + bufs["P"][7].take())
        c3 = dve("scalar_tensor_tensor", out=out_slice, in0=scrA, scalar=scale_vec,
                 in1=Pap[7], op0=ALU.mult, op1=ALU.mult,
                 waits=[(S.e_pe, n2), CONSTW])
        bufs["P"][7].consumed(S.e_dve, c3)
        bufs["scrA"].consumed(S.e_dve, c3)
        return c3

    def ln_apply(bi_sum, bi_sq, rsrc, dst_bf, out_spec, pe_n):
        c1 = dve("tensor_scalar_mul", out=rows[0:1, 0, :], in0=Pap[bi_sum][0:1, :],
                 scalar1=1.0 / 2048.0, waits=[(S.e_pe, pe_n)])
        c2 = dve("tensor_scalar_mul", out=rows[0:1, 1, :], in0=Pap[bi_sq][0:1, :],
                 scalar1=1.0 / 2048.0)
        c3 = dve("scalar_tensor_tensor", out=Pap[bi_sum][0:1, :], in0=rows[0:1, 0, :],
                 scalar=1.0, in1=rows[0:1, 0, :], op0=ALU.mult, op1=ALU.mult)
        c4 = dve("tensor_tensor", out=rows[0:1, 1, :], in0=rows[0:1, 1, :],
                 in1=Pap[bi_sum][0:1, :], op=ALU.subtract)
        c5 = act("activation", out=rows[0:1, 1, :], in_=rows[0:1, 1, :], func=AF.Ln,
                 bias=EPS_LN, waits=[(S.e_dve, c4)])
        c6 = act("activation", out=rstdr[0:1, 0, :], in_=rows[0:1, 1, :], func=AF.Exp,
                 scale=-0.5)
        n1 = pe("matmul", out=Pap[bi_sum], lhsT=ones1f.ap()[:], rhs=rows[0:1, 0, :],
                start=True, stop=True, waits=[(S.e_dve, c4)] + bufs["P"][bi_sum].take())
        n2 = pe("matmul", out=Pap[bi_sq], lhsT=ones1bf.ap()[:], rhs=rstdr[0:1, 0, :],
                start=True, stop=True, waits=[(S.e_act, c6)] + bufs["P"][bi_sq].take())
        for mc in range(DC):
            wts = ([(S.e_pe, max(n1, n2))] if mc == 0 else []) + bufs["scrA"].take()
            dve("tensor_tensor", out=scrA, in0=rsrc[:, mc * 512:(mc + 1) * 512],
                in1=Pap[bi_sum], op=ALU.subtract, waits=wts)
            if dst_bf is not None:
                cx = dve("tensor_tensor", out=dst_bf[:, mc * 512:(mc + 1) * 512],
                         in0=scrA, in1=Pap[bi_sq], op=ALU.mult)
            else:
                (so, t0) = out_spec
                oslot = mc % 2
                cx = dve("tensor_tensor", out=scrB[:, oslot, 0:512], in0=scrA,
                         in1=Pap[bi_sq], op=ALU.mult, waits=bufs["scrB"][oslot].take())
                nout = dma("sync", outT[so][mc * 128:(mc + 1) * 128, t0:t0 + TT],
                           scrB[:, oslot, 0:512], RS["out"][oslot], waits=[(S.e_dve, cx)])
                bufs["scrB"][oslot].consumed(RS["out"][oslot], nout)
            bufs["scrA"].consumed(S.e_dve, cx)
        bufs["P"][bi_sum].consumed(S.e_dve, S.e_dve.n)
        bufs["P"][bi_sq].consumed(S.e_dve, S.e_dve.n)
        if dst_bf is not None:
            X1_DONE[0] = S.e_dve.n

    def emit_tile(s, t, nxt, post_attn_hook=None):
        t0 = t * TT

        pr.phase = "qkv"
        first_tile = (s == 0 and t == 0)
        XW = x0_info[0]
        qk_pend = []
        for oc in range(NKV):
            slot, wql = load_wq("k", oc, first_tile)
            bi = oc % 2
            for dc in range(DC):
                wts = ()
                if dc == 0:
                    wts = [wql, XW, CONSTW,
                           (S.e_dve, MEMSET)] + bufs["P"][bi].take()
                pe("matmul", out=Pap[bi], lhsT=wqb[:, slot, dc * 128:(dc + 1) * 128],
                   rhs=x0bf[:, dc * 512:dc * 512 + TT], start=(dc == 0),
                   stop=(dc == DC - 1), waits=wts, mark=(dc == DC - 1))
            bufs["wqb"][slot].consumed(S.e_pe, S.e_pe.n)
            if qk_pend:
                qknorm(*qk_pend.pop(0))
            qk_pend.append((bi, khat[:, oc * NTOK + t0: oc * NTOK + t0 + TT],
                            ks.ap()[:], False, oc % 2, S.e_pe.n))
        qknorm(*qk_pend.pop(0))
        KHAT_DONE = S.e_dve.n

        for dc in range(DC):
            slot, wvl = load_wv(dc, first_tile)
            for sc in range(4):
                wts = []
                if dc == 0:
                    wts += bufs["P"][sc].take() + [XW]
                if sc == 0:
                    wts += [wvl]
                pe("matmul", out=Pap[sc],
                   lhsT=x0bf[:, dc * 512 + sc * 128:dc * 512 + (sc + 1) * 128],
                   rhs=wvb[:, slot, :], start=(dc == 0), stop=(dc == DC - 1),
                   waits=wts, mark=(sc == 3))
            bufs["wvb"][slot].consumed(S.e_pe, S.e_pe.n)
        for sc in range(4):
            scg = (t0 // 128) + sc
            c = dve("tensor_copy", out=vsb[:, scg * 512:(scg + 1) * 512],
                    in_=Pap[sc], waits=[(S.e_pe, S.e_pe.n)])
            bufs["P"][sc].consumed(S.e_dve, c)
        V_DONE = S.e_dve.n

        for oc in range(NH):
            slot, wql = load_wq("q", oc, first_tile)
            bi = oc % 2
            for dc in range(DC):
                wts = ()
                if dc == 0:
                    wts = [wql] + bufs["P"][bi].take()
                pe("matmul", out=Pap[bi], lhsT=wqb[:, slot, dc * 128:(dc + 1) * 128],
                   rhs=x0bf[:, dc * 512:dc * 512 + TT], start=(dc == 0),
                   stop=(dc == DC - 1), waits=wts, mark=(dc == DC - 1))
            bufs["wqb"][slot].consumed(S.e_pe, S.e_pe.n)
            if qk_pend:
                qknorm(*qk_pend.pop(0))
            qk_pend.append((bi, qhat[:, oc * 512:(oc + 1) * 512], qs.ap()[:],
                            True, oc % 2, S.e_pe.n))
        qknorm(*qk_pend.pop(0))
        QHAT_DONE = S.e_dve.n

        def take2(i):
            return bufs["P"][i].take() + bufs["P"][i + 1].take()

        def cons2(i, sem, cnt):
            bufs["P"][i].consumed(sem, cnt)
            bufs["P"][i + 1].consumed(sem, cnt)

        arot = [0]

        def build_A(h):
            par = h % 2
            stt = stt_t[:, par, :]
            sttl = sttl_t[:, par, :]
            sttw = sttw_t[:, par, :]
            sttlf = sttlf_t[:, par, :]
            stt2 = stt2_t[:, par, :]
            kv = h // 4
            units = []
            for qc in range(4):
                plan = []
                sti = 0
                while sti + 1 < t:
                    plan.append(("pair", sti))
                    sti += 2
                if sti < t:
                    plan.append(("one", sti))
                plan.append(("diag", t))
                nu = len(plan)

                def unit(kind, sti, u, qc=qc):
                    def f():
                        pr.phase = "attnA"
                        base = 0 if (arot[0] % 2 == 0) else 4
                        arot[0] += 1
                        wb = PAw if base == 0 else PCw
                        wts = take2(base) + [(S.e_dve, max(KHAT_DONE, QHAT_DONE))]
                        if kind == "pair":
                            pe("matmul", out=wb[:, 0:512],
                               lhsT=qhat[:, h * 512 + qc * 128:h * 512 + (qc + 1) * 128],
                               rhs=khat[:, kv * NTOK + sti * 512:kv * NTOK + (sti + 1) * 512],
                               start=True, stop=True, waits=wts)
                            n = pe("matmul", out=wb[:, 512:1024],
                                   lhsT=qhat[:, h * 512 + qc * 128:h * 512 + (qc + 1) * 128],
                                   rhs=khat[:, kv * NTOK + (sti + 1) * 512:kv * NTOK + (sti + 2) * 512],
                                   start=True, stop=True)
                            na = 1024
                        else:
                            na = 512 if kind == "one" else (qc + 1) * 128
                            n = pe("matmul", out=wb[:, 0:na],
                                   lhsT=qhat[:, h * 512 + qc * 128:h * 512 + (qc + 1) * 128],
                                   rhs=khat[:, kv * NTOK + sti * 512:kv * NTOK + sti * 512 + na],
                                   start=True, stop=True, waits=wts)
                        if kind == "diag":
                            cmask = dve("tensor_tensor", out=wb[:, 0:na], in0=wb[:, 0:na],
                                        in1=mka.ap()[:, qc * 512:qc * 512 + na], op=ALU.add,
                                        waits=[(S.e_pe, n), CONSTW])
                            wmax = [(S.e_dve, cmask)]
                        else:
                            wmax = [(S.e_pe, n)]
                        if u == 0:
                            wmax = wmax + [(S.e_act, S.e_act.n)]
                        cm = dve("tensor_reduce", out=stt[:, u:u + 1], in_=wb[:, 0:na],
                                 axis=AX.X, op=ALU.max, negate=True, waits=wmax)
                        ce = act("activation", out=dump[:, 0:na], in_=wb[:, 0:na],
                                 func=AF.Exp, bias=stt[:, u:u + 1],
                                 accum_out=sttl[:, u:u + 1],
                                 waits=[(S.e_dve, cm)])
                        cons2(base, S.e_act, ce)
                    return f
                for u, (kind, sti) in enumerate(plan):
                    units.append(unit(kind, sti, u))

                def combine(nu=nu, qc=qc):
                    def f():
                        pr.phase = "attnC"
                        w2 = bufs["stt2"][par].take() if qc == 0 else []
                        act("copy", out=sttlf[:, 0:nu], in_=sttl[:, 0:nu])
                        if nu == 1:
                            cl = act("activation", out=sttl[:, 5:6], in_=sttl[:, 0:1], func=AF.Ln)
                            dve("tensor_sub", out=stt2[:, qc:qc + 1], in0=sttl[:, 5:6],
                                in1=stt[:, 0:1], waits=[(S.e_act, cl)] + w2)
                        else:
                            c1 = dve("tensor_reduce", out=stt[:, 4:5], in_=stt[:, 0:nu],
                                     axis=AX.X, op=ALU.min,
                                     waits=[(S.e_act, S.e_act.n), (S.e_dve, S.e_dve.n)])
                            c2 = act("activation", out=sttw[:, 0:nu], in_=stt[:, 0:nu],
                                     func=AF.Exp, scale=-1.0, bias=stt[:, 4:5],
                                     waits=[(S.e_dve, c1)])
                            c3 = dve("tensor_mul", out=sttw[:, 0:nu], in0=sttw[:, 0:nu],
                                     in1=sttlf[:, 0:nu], waits=[(S.e_act, c2)])
                            dve("tensor_reduce", out=sttl[:, 4:5], in_=sttw[:, 0:nu],
                                axis=AX.X, op=ALU.add, waits=[(S.e_dve, S.e_dve.n)])
                            cl = act("activation", out=sttl[:, 5:6], in_=sttl[:, 4:5], func=AF.Ln,
                                     waits=[(S.e_dve, S.e_dve.n)])
                            dve("tensor_sub", out=stt2[:, qc:qc + 1], in0=sttl[:, 5:6],
                                in1=stt[:, 4:5],
                                waits=[(S.e_act, cl)] + w2 + [(S.e_dve, S.e_dve.n)])
                    return f
                units.append(combine())
            return units

        def build_B(h, STT2):
            par = h % 2
            stt2 = stt2_t[:, par, :]
            kv = h // 4
            units = []

            def stats():
                pr.phase = "attnB"
                n = pe("transpose", out=Pap[6][0:4, 0:128], in_=stt2[:, 0:4],
                       identity=identf.ap()[:],
                       waits=[(S.e_dve, STT2), CONSTW] + bufs["P"][6].take())
                bufs["stt2"][par].consumed(S.e_pe, n)
                c4 = dve("tensor_copy", out=st4, in_=Pap[6][0:4, 0:128],
                         waits=[(S.e_pe, n)] + bufs["st4"].take())
                bufs["P"][6].consumed(S.e_dve, c4)
                nrow = dma("sync", strow, st4, S.s_stat,
                           waits=bufs["strow"].take() + [(S.e_dve, c4)])
                bufs["st4"].consumed(S.s_stat, nrow)
                n = pe("matmul", out=Pap[7], lhsT=ones1f.ap()[:], rhs=strow,
                       start=True, stop=True,
                       waits=[(S.s_stat, nrow), (S.e_dve, MEMSET)] + bufs["P"][7].take())
                bufs["strow"].consumed(S.e_pe, n)
                c5 = dve("tensor_copy", out=scrA, in_=Pap[7],
                         waits=bufs["scrA"].take() + [(S.e_pe, n)])
                bufs["P"][7].consumed(S.e_dve, c5)
            units.append(stats)

            n_sc = 4 * (t + 1)
            plan = []
            sc = 0
            while sc < n_sc:
                if sc + 1 < 4 * t:
                    plan.append(("pair", sc)); sc += 2
                else:
                    plan.append(("one", sc)); sc += 1

            def bunit(kind, sc, it, last):
                def f():
                    pr.phase = "attnB"
                    sslot = it % 2
                    if kind == "pair":
                        w0 = take2(2)
                        pe("matmul", out=Pap[2],
                           lhsT=khat[:, kv * NTOK + sc * 128:kv * NTOK + (sc + 1) * 128],
                           rhs=qhat[:, h * 512:(h + 1) * 512],
                           start=True, stop=True, waits=w0)
                        n = pe("matmul", out=Pap[3],
                               lhsT=khat[:, kv * NTOK + (sc + 1) * 128:kv * NTOK + (sc + 2) * 128],
                               rhs=qhat[:, h * 512:(h + 1) * 512],
                               start=True, stop=True)
                        wts = bufs["scrB"][sslot].take() + [(S.e_pe, n)]
                        csub = dve("tensor_tensor",
                                   out=scrB[:, sslot, :].rearrange("p (a b) -> p a b", a=2),
                                   in0=PBw.rearrange("p (a b) -> p a b", a=2),
                                   in1=scrA.unsqueeze(1).broadcast_to([128, 2, 512]),
                                   op=ALU.subtract, waits=wts)
                        bufs["scrA"].consumed(S.e_dve, csub)
                        cons2(2, S.e_dve, csub)
                        cexp = act("activation", out=ptb[:, sslot, :], in_=scrB[:, sslot, :],
                                   func=AF.Exp,
                                   waits=bufs["ptb"][sslot].take() + [(S.e_dve, csub)])
                        bufs["scrB"][sslot].consumed(S.e_act, cexp)
                        wts = [(S.e_act, cexp), (S.e_dve, V_DONE)]
                        if sc == 0:
                            wts += bufs["P"][6].take()
                        pe("matmul", out=Pap[6],
                           lhsT=vsb[:, sc * 512 + kv * 128:sc * 512 + (kv + 1) * 128],
                           rhs=ptb[:, sslot, 0:512], start=(sc == 0), stop=False,
                           waits=wts)
                        n = pe("matmul", out=Pap[6],
                               lhsT=vsb[:, (sc + 1) * 512 + kv * 128:(sc + 1) * 512 + (kv + 1) * 128],
                               rhs=ptb[:, sslot, 512:1024], start=False, stop=last)
                        bufs["ptb"][sslot].consumed(S.e_pe, n)
                    else:
                        qlo = 0 if sc < 4 * t else (sc - 4 * t) * 128
                        bi = 2 + (it % 2)
                        n = pe("matmul", out=Pap[bi][:, qlo:512],
                               lhsT=khat[:, kv * NTOK + sc * 128:kv * NTOK + (sc + 1) * 128],
                               rhs=qhat[:, h * 512 + qlo:(h + 1) * 512],
                               start=True, stop=True, waits=bufs["P"][bi].take())
                        wts = bufs["scrB"][sslot].take() + [(S.e_pe, n)]
                        csub = dve("tensor_tensor", out=scrB[:, sslot, qlo:512],
                                   in0=Pap[bi][:, qlo:512],
                                   in1=scrA[:, qlo:512], op=ALU.subtract, waits=wts)
                        bufs["scrA"].consumed(S.e_dve, csub)
                        bufs["P"][bi].consumed(S.e_dve, csub)
                        if sc >= 4 * t:
                            csub = dve("tensor_tensor", out=scrB[:, sslot, qlo:512],
                                       in0=scrB[:, sslot, qlo:512],
                                       in1=mkb.ap()[:, (sc - 4 * t) * 512 + qlo:(sc - 4 * t + 1) * 512],
                                       op=ALU.add, waits=[CONSTW])
                        cexp = act("activation", out=ptb[:, sslot, qlo:512],
                                   in_=scrB[:, sslot, qlo:512], func=AF.Exp,
                                   waits=bufs["ptb"][sslot].take() + [(S.e_dve, csub)])
                        bufs["scrB"][sslot].consumed(S.e_act, cexp)
                        wts = [(S.e_act, cexp), (S.e_dve, V_DONE)]
                        if sc == 0:
                            wts += bufs["P"][6].take()
                        n = pe("matmul", out=Pap[6][:, qlo:512],
                               lhsT=vsb[:, sc * 512 + kv * 128:sc * 512 + (kv + 1) * 128],
                               rhs=ptb[:, sslot, qlo:512], start=(sc == 0), stop=last,
                               waits=wts)
                        bufs["ptb"][sslot].consumed(S.e_pe, n)
                return f
            for it, (kind, sc) in enumerate(plan):
                units.append(bunit(kind, sc, it, it == len(plan) - 1))

            def evac():
                co = act("copy", out=ocat[:, h * 512:(h + 1) * 512], in_=Pap[6],
                         waits=[(S.e_pe, S.e_pe.n)])
                bufs["P"][6].consumed(S.e_act, co)
            units.append(evac)
            return units

        def zip_emit(A, Bu):
            na, nb = len(A), len(Bu)
            ia = ib = 0
            while ia < na or ib < nb:
                if ib < nb and (ia >= na or ib * na <= ia * nb):
                    Bu[ib](); ib += 1
                else:
                    A[ia](); ia += 1

        pend = None
        for h in range(NH):
            A = build_A(h)
            if pend is None:
                for f in A:
                    f()
            else:
                zip_emit(A, build_B(*pend))
            pend = (h, S.e_dve.n)
        for f in build_B(*pend):
            f()
        OCAT_DONE = S.e_act.n
        pr.phase = "wo"

        def res_load(mc):
            rslot = mc % 4
            nres = dma("sync", x0res[:, rslot, :],
                       xT[s][mc * 128:(mc + 1) * 128, t0:t0 + TT], RS["res"][rslot],
                       waits=bufs["x0res"][rslot].take())
            return rslot, nres
        res_info = {mc: res_load(mc) for mc in range(2)}
        ones_q = []

        def ones_mm(mc):
            (csq, qslot) = ones_q.pop(0)
            wts = [(S.e_dve, csq)]
            if mc == 0:
                wts += bufs["P"][4].take() + bufs["P"][5].take()
            pe("matmul", out=Pap[4][0:1, :], lhsT=onesbf.ap()[:],
               rhs=r1v[:, mc * 512:(mc + 1) * 512], start=(mc == 0),
               stop=(mc == DC - 1), waits=wts, mark=(mc == DC - 1))
            nss = pe("matmul", out=Pap[5][0:1, :], lhsT=onesbf.ap()[:],
                     rhs=sqb[:, qslot, :], start=(mc == 0), stop=(mc == DC - 1),
                     mark=True)
            bufs["sqb"][qslot].consumed(S.e_pe, nss)

        for mc in range(DC):
            slot, wql = load_wq("o", mc, first_tile)
            bi = mc % 2
            for cc in range(DC):
                wts = ()
                if cc == 0:
                    wts = [wql,
                           (S.e_act, OCAT_DONE)] + bufs["P"][bi].take()
                pe("matmul", out=Pap[bi], lhsT=wqb[:, slot, cc * 128:(cc + 1) * 128],
                   rhs=ocat[:, cc * 512:(cc + 1) * 512], start=(cc == 0),
                   stop=(cc == DC - 1), waits=wts, mark=(cc == DC - 1))
            MMN = S.e_pe.n
            bufs["wqb"][slot].consumed(S.e_pe, MMN)
            if mc + 2 < DC:
                res_info[mc + 2] = res_load(mc + 2)
            if mc >= 1:
                ones_mm(mc - 1)
            rslot, nres = res_info.pop(mc)
            cr = dve("tensor_tensor", out=r1v[:, mc * 512:(mc + 1) * 512], in0=Pap[bi],
                     in1=x0res[:, rslot, :], op=ALU.add,
                     waits=[(S.e_pe, MMN), (RS["res"][rslot], nres)])
            bufs["P"][bi].consumed(S.e_dve, cr)
            bufs["x0res"][rslot].consumed(S.e_dve, cr)
            qslot = mc % 2
            csq = dve("tensor_mul", out=sqb[:, qslot, :],
                      in0=r1v[:, mc * 512:(mc + 1) * 512],
                      in1=r1v[:, mc * 512:(mc + 1) * 512],
                      waits=bufs["sqb"][qslot].take())
            ones_q.append((csq, qslot))
        ones_mm(DC - 1)
        ln_apply(4, 5, r1v, x1bf, None, S.e_pe.n)
        if post_attn_hook is not None:
            post_attn_hook()

        pr.phase = "ffn"
        first = (s == 0 and t == 0)
        for ic in range(ICN):
            gslot = ic % 2
            if first:
                ng = dma("gpsimd", wgb[:, gslot, :], g_t[ic], RS["gg"][gslot],
                         waits=bufs["wgb"][gslot].take())
                cg = dve("tensor_scalar_add", out=wgb[:, gslot, :], in0=wgb[:, gslot, :],
                         scalar1=-128.0, waits=[(RS["gg"][gslot], ng)])
                nwb = dma("gpsimd", gs_s[ic], wgb[:, gslot, :], RS["wbg"][gslot],
                          waits=[(S.e_dve, cg)])
                bufs["wgb"][gslot].consumed(RS["wbg"][gslot], nwb)
                gw0 = (S.e_dve, cg)
                nu0 = dma("gpsimd", wub[:, gslot, :], u_t[ic], RS["ug"][gslot],
                          waits=bufs["wub"][gslot].take())
                cu = dve("tensor_scalar_add", out=wub[:, gslot, :], in0=wub[:, gslot, :],
                         scalar1=-128.0, waits=[(RS["ug"][gslot], nu0)])
                nwb = dma("gpsimd", us_s[ic], wub[:, gslot, :], RS["wbu"][gslot],
                          waits=[(S.e_dve, cu)])
                bufs["wub"][gslot].consumed(RS["wbu"][gslot], nwb)
                uw0 = (S.e_dve, cu)
            else:
                ng = dma("sync", wgb[:, gslot, :], gs_s[ic], RS["g"][gslot],
                         waits=bufs["wgb"][gslot].take())
                gw0 = (RS["g"][gslot], ng)
                nu = dma("sync", wub[:, gslot, :], us_s[ic], RS["u"][gslot],
                         waits=bufs["wub"][gslot].take())
                uw0 = (RS["u"][gslot], nu)
            bg, bu = gslot, 2 + gslot
            for hc in range(HCN):
                wts = ()
                if hc == 0:
                    wts = [gw0, (S.e_dve, X1_DONE[0])] + bufs["P"][bg].take()
                pe("matmul", out=Pap[bg], lhsT=wgb[:, gslot, hc * 128:(hc + 1) * 128],
                   rhs=x1bf[:, hc * 512:(hc + 1) * 512], start=(hc == 0),
                   stop=(hc == HCN - 1), waits=wts, mark=(hc == HCN - 1))
            bufs["wgb"][gslot].consumed(S.e_pe, S.e_pe.n)
            for hc in range(HCN):
                wts = ()
                if hc == 0:
                    wts = [uw0] + bufs["P"][bu].take()
                pe("matmul", out=Pap[bu], lhsT=wub[:, gslot, hc * 128:(hc + 1) * 128],
                   rhs=x1bf[:, hc * 512:(hc + 1) * 512], start=(hc == 0),
                   stop=(hc == HCN - 1), waits=wts, mark=(hc == HCN - 1))
            NU = S.e_pe.n
            bufs["wub"][gslot].consumed(S.e_pe, NU)
            sslot = ic % 2
            cs = act("activation", out=scrB[:, sslot, 0:512], in_=Pap[bg], func=AF.Silu,
                     scale=gsc.ap()[:, ic:ic + 1],
                     waits=bufs["scrB"][sslot].take() + [(S.e_pe, NU), CONSTW])
            bufs["P"][bg].consumed(S.e_act, cs)
            ch = dve("scalar_tensor_tensor", out=hact[:, ic * 512:(ic + 1) * 512],
                     in0=Pap[bu], scalar=usc.ap()[:, ic:ic + 1], in1=scrB[:, sslot, 0:512],
                     op0=ALU.mult, op1=ALU.mult, waits=[(S.e_act, cs)])
            bufs["P"][bu].consumed(S.e_dve, ch)
            bufs["scrB"][sslot].consumed(S.e_dve, ch)
        HACT_LAST = S.e_dve.n
        ones2_q = []

        def ones2_mm():
            (csq, qslot, hc2) = ones2_q.pop(0)
            wts = [(S.e_dve, csq)]
            if hc2 == 0:
                wts += bufs["P"][6].take() + bufs["P"][7].take()
            pe("matmul", out=Pap[6][0:1, :], lhsT=onesbf.ap()[:],
               rhs=r2v[:, hc2 * 512:(hc2 + 1) * 512], start=(hc2 == 0),
               stop=(hc2 == HCN - 1), waits=wts, mark=(hc2 == HCN - 1))
            pe("matmul", out=Pap[7][0:1, :], lhsT=onesbf.ap()[:], rhs=sqb[:, qslot, :],
               start=(hc2 == 0), stop=(hc2 == HCN - 1), mark=True)
            bufs["sqb"][qslot].consumed(S.e_pe, S.e_pe.n)

        for hc in range(HCN):
            bd = 4 + (hc % 2)
            for qtr in range(4):
                dslot = (hc * 4 + qtr) % 2
                w = bufs["wdb"][dslot].take()
                if first:
                    nd0 = dma("gpsimd", wdb[:, dslot, :],
                              d_t[hc][:, qtr * 2048:(qtr + 1) * 2048], RS["dg"][dslot],
                              waits=w)
                    cd = dve("tensor_scalar_add", out=wdb[:, dslot, :],
                             in0=wdb[:, dslot, :], scalar1=-128.0,
                             waits=[(RS["dg"][dslot], nd0)])
                    nwb = dma("gpsimd", ds_s[hc][:, qtr * 2048:(qtr + 1) * 2048],
                              wdb[:, dslot, :], RS["wbd"][dslot], waits=[(S.e_dve, cd)])
                    bufs["wdb"][dslot].consumed(RS["wbd"][dslot], nwb)
                    dw0 = (S.e_dve, cd)
                else:
                    nd = dma("sync", wdb[:, dslot, :],
                             ds_s[hc][:, qtr * 2048:(qtr + 1) * 2048], RS["d"][dslot],
                             waits=w)
                    dw0 = (RS["d"][dslot], nd)
                for ih in range(16):
                    ic = qtr * 16 + ih
                    wts = []
                    if ic == 0:
                        wts += [(S.e_dve, HACT_LAST)] + bufs["P"][bd].take()
                    if ih == 0:
                        wts += [dw0]
                    pe("matmul", out=Pap[bd], lhsT=wdb[:, dslot, ih * 128:(ih + 1) * 128],
                       rhs=hact[:, ic * 512:(ic + 1) * 512], start=(ic == 0),
                       stop=(ic == ICN - 1), waits=wts, mark=(ih == 15))
                bufs["wdb"][dslot].consumed(S.e_pe, S.e_pe.n)
            MMN = S.e_pe.n
            wts = [(S.e_pe, MMN), CONSTW]
            cr = dve("scalar_tensor_tensor", out=r2v[:, hc * 512:(hc + 1) * 512],
                     in0=Pap[bd], scalar=dsc.ap()[:, hc:hc + 1],
                     in1=x1bf[:, hc * 512:(hc + 1) * 512],
                     op0=ALU.mult, op1=ALU.add, waits=wts)
            bufs["P"][bd].consumed(S.e_dve, cr)
            qslot = hc % 2
            csq = dve("tensor_mul", out=sqb[:, qslot, :],
                      in0=r2v[:, hc * 512:(hc + 1) * 512],
                      in1=r2v[:, hc * 512:(hc + 1) * 512],
                      waits=bufs["sqb"][qslot].take())
            ones2_q.append((csq, qslot, hc))
            if hc >= 1:
                ones2_mm()
        ones2_mm()
        bufs["arena"].consumed(S.e_pe, S.e_pe.n)
        if nxt is not None:
            emit_cast_x0(*nxt)
        ln_apply(6, 7, r2v, None, (s, t0), S.e_pe.n)

    tiles = [(s, t) for s in range(n_seq) for t in range(n_tiles)]
    emit_cast_x0(*tiles[0], first_tile=True)
    for i, (s, t) in enumerate(tiles):
        nxt = tiles[i + 1] if i + 1 < len(tiles) else None
        emit_tile(s, t, nxt, post_attn_hook=emit_phasew if i == 0 else None)
    pr.rec("sync", "wait_ge", RS["out"][0].h, RS["out"][0].n)
    pr.rec("sync", "wait_ge", RS["out"][1].h, RS["out"][1].n)
    pr.replay()
    nc._prog = pr
    return nc



def _prep(inputs):
    f32 = np.float32
    Wq = np.asarray(inputs["L0_attn_Wq"], f32)
    Wk = np.asarray(inputs["L0_attn_Wk"], f32)
    Wv = np.asarray(inputs["L0_attn_Wv"], f32)
    Wo = np.asarray(inputs["L0_attn_Wo"], f32)
    gw = np.asarray(inputs["L0_ffn_gate_w"], np.int32)
    uw = np.asarray(inputs["L0_ffn_up_w"], np.int32)
    dw = np.asarray(inputs["L0_ffn_down_w"], np.int32)

    def wtile(W, n_out):
        r = W.reshape(n_out, 128, DC, 128)
        return np.ascontiguousarray(r.transpose(0, 3, 2, 1)).reshape(n_out, 128, D)

    d = {}
    d["wq_t"] = wtile(Wq, NH)
    d["wk_t"] = wtile(Wk, NKV)
    d["wo_t"] = wtile(Wo, DC)
    d["wv_t"] = np.ascontiguousarray(Wv.T.reshape(DC, 128, 512))
    d["g_t"] = np.ascontiguousarray(
        gw.reshape(ICN, 128, DC, 128).transpose(0, 3, 2, 1)).reshape(ICN, 128, D)
    d["u_t"] = np.ascontiguousarray(
        uw.reshape(ICN, 128, DC, 128).transpose(0, 3, 2, 1)).reshape(ICN, 128, D)
    d["d_t"] = np.ascontiguousarray(
        dw.reshape(HCN, 128, ICN, 128).transpose(0, 3, 2, 1)).reshape(HCN, 128, II)
    d["qs"] = np.ascontiguousarray(np.asarray(inputs["L0_attn_q_scale"], f32).reshape(128, 1))
    d["ks"] = np.ascontiguousarray(np.asarray(inputs["L0_attn_k_scale"], f32).reshape(128, 1))
    d["gs"] = np.ascontiguousarray(np.asarray(inputs["L0_ffn_gate_scale"], f32).reshape(ICN, 128).T)
    d["us"] = np.ascontiguousarray(np.asarray(inputs["L0_ffn_up_scale"], f32).reshape(ICN, 128).T)
    d["ds"] = np.ascontiguousarray(np.asarray(inputs["L0_ffn_down_scale"], f32).reshape(HCN, 128).T)
    i_idx = np.arange(128)[:, None]
    j_idx = np.arange(512)[None, :]
    d["mka"] = np.concatenate([np.where(j_idx <= qc * 128 + i_idx, 0.0, NEG)
                               for qc in range(4)], axis=1).astype(ml_dtypes.bfloat16)
    d["mkb"] = np.concatenate([np.where(p * 128 + i_idx <= j_idx, 0.0, NEG)
                               for p in range(4)], axis=1).astype(ml_dtypes.bfloat16)
    d["idn"] = np.eye(128, dtype=f32)
    return d


def kernel(**inputs):
    x = np.asarray(inputs["L0_attn_x"], np.float32)
    shared = _prep(inputs)
    nc = build_core_program()
    in_maps = []
    for c in range(NCORE):
        m = dict(shared)
        m["xT"] = np.ascontiguousarray(x[2 * c:2 * c + 2].transpose(0, 2, 1))
        in_maps.append(m)
    res = run_bass_kernel_spmd(nc, in_maps, core_ids=list(range(NCORE)))
    out = np.empty((B, T, D), np.float32)
    for c in range(NCORE):
        o = np.asarray(res.results[c]["outT"], np.float32)
        out[2 * c:2 * c + 2] = o.transpose(0, 2, 1)
    return out


# version 10
# speedup vs baseline: 1.1939x; 1.0713x over previous
import os
import sys
import numpy as np
import ml_dtypes

try:
    import concourse.bass as bass
except ImportError:
    sys.path.insert(0, "/opt/trn_rl_repo")
    import concourse.bass as bass
import concourse.mybir as mybir
from concourse.bass_utils import run_bass_kernel_spmd

F32 = mybir.dt.float32
BF16 = mybir.dt.bfloat16
I32 = mybir.dt.int32
AF = mybir.ActivationFunctionType
ALU = mybir.AluOpType
AX = mybir.AxisListType

B, T, D = 16, 2048, 2048
NH, NKV, HD = 16, 4, 128
II = 8192
TT = 512
DC = D // 128
ICN = II // 128
HCN = DC
NCORE = 8
SER = {"c4", "c6"}
NEG = -30000.0
EPS_QK = 1e-6
EPS_LN = 1e-5


class SemC:
    def __init__(self, nc, name):
        self.h = nc.alloc_semaphore(name)
        self.n = 0
        self.name = name


class Buf:
    def __init__(self):
        self.need = {}

    def consumed(self, sem, cnt):
        if sem is not None and (sem not in self.need or self.need[sem] < cnt):
            self.need[sem] = cnt

    def take(self):
        w = list(self.need.items())
        self.need = {}
        return w


class Prog:
    def __init__(self, nc, paranoid=False):
        self.nc = nc
        self.ops = []
        self.labels = []
        self.phase = ""
        self.label_of = {}
        if paranoid is True:
            paranoid = {"sync", "scalar", "vector", "tensor", "gpsimd"}
        self.paranoid = paranoid or set()
        if self.paranoid:
            self.pp = {e: SemC(nc, "pp_" + e) for e in self.paranoid}

    def rec(self, eng, fn, *args, waits=(), inc=None, **kwargs):
        ws = [(s, c) for (s, c) in waits if s is not None]
        if inc is not None:
            inc[0].n += inc[1]
        self.ops.append((eng, fn, args, kwargs, ws, inc, self.phase))
        self.labels.append(self.phase + "/" + fn)

    def replay(self):
        hw = {}
        dep = mybir.DependencyInfo(mybir.DependencyClass.NoSync,
                                   mybir.DependencyReason.DATA)

        def run_engine(engobj, engname):
            prev = None
            prev_done = None
            rn = {}
            for oi, (eng, fn, args, kwargs, ws, inc, phase) in enumerate(self.ops):
                if eng != engname:
                    continue
                lbl = self.labels[oi]
                par = engname in self.paranoid or (engname + ":" + phase) in self.paranoid
                if par and fn != "wait_ge" and prev_done is not None:
                    engobj.wait_ge(prev_done[0].h, prev_done[1])
                for (sem, cnt) in ws:
                    key = (engname, sem.name)
                    if hw.get(key, -1) >= cnt:
                        continue
                    hw[key] = cnt
                    engobj.wait_ge(sem.h, cnt)
                ins = getattr(engobj, fn)(*args, **kwargs)
                self.label_of[ins.ins.name] = lbl
                if inc is not None:
                    ins.then_inc(inc[0].h, inc[1])
                if par and fn != "wait_ge":
                    if inc is not None:
                        rn[inc[0].name] = rn.get(inc[0].name, 0) + inc[1]
                        prev_done = (inc[0], rn[inc[0].name])
                    else:
                        amt = 16 if fn == "dma_start" else 1
                        pp = self.pp[engname]
                        pp.n += amt
                        ins.then_inc(pp.h, amt)
                        prev_done = (pp, pp.n)
                if prev is not None:
                    try:
                        ins.ins.add_dependency(prev, dep)
                    except Exception:
                        pass
                prev = ins.ins.name

        with self.nc.Block() as block:
            block.sync(lambda e: run_engine(e, "sync"))
            block.scalar(lambda e: run_engine(e, "scalar"))
            block.vector(lambda e: run_engine(e, "vector"))
            block.tensor(lambda e: run_engine(e, "tensor"))
            block.gpsimd(lambda e: run_engine(e, "gpsimd"))


def build_core_program(n_seq=2, n_tiles=T // TT, paranoid=False):
    nc = bass.Bass("TRN2", target_bir_lowering=False)
    NTOK = n_tiles * TT

    xT = nc.dram_tensor("xT", [n_seq, D, NTOK], F32, kind="ExternalInput").ap()
    wq_t = nc.dram_tensor("wq_t", [NH, 128, D], F32, kind="ExternalInput").ap()
    wk_t = nc.dram_tensor("wk_t", [NKV, 128, D], F32, kind="ExternalInput").ap()
    wv_t = nc.dram_tensor("wv_t", [DC, 128, 512], F32, kind="ExternalInput").ap()
    wo_t = nc.dram_tensor("wo_t", [DC, 128, D], F32, kind="ExternalInput").ap()
    g_t = nc.dram_tensor("g_t", [ICN, 128, D], I32, kind="ExternalInput").ap()
    u_t = nc.dram_tensor("u_t", [ICN, 128, D], I32, kind="ExternalInput").ap()
    d_t = nc.dram_tensor("d_t", [HCN, 128, II], I32, kind="ExternalInput").ap()
    qs_d = nc.dram_tensor("qs", [128, 1], F32, kind="ExternalInput").ap()
    ks_d = nc.dram_tensor("ks", [128, 1], F32, kind="ExternalInput").ap()
    gs_d = nc.dram_tensor("gs", [128, ICN], F32, kind="ExternalInput").ap()
    us_d = nc.dram_tensor("us", [128, ICN], F32, kind="ExternalInput").ap()
    ds_d = nc.dram_tensor("ds", [128, HCN], F32, kind="ExternalInput").ap()
    mka_d = nc.dram_tensor("mka", [128, 4 * 512], BF16, kind="ExternalInput").ap()
    mkb_d = nc.dram_tensor("mkb", [128, 4 * 512], BF16, kind="ExternalInput").ap()
    idn_d = nc.dram_tensor("idn", [128, 128], F32, kind="ExternalInput").ap()
    outT = nc.dram_tensor("outT", [n_seq, D, NTOK], F32, kind="ExternalOutput").ap()
    gs_s = nc.dram_tensor("gs_s", [ICN, 128, D], BF16, kind="Internal").ap()
    us_s = nc.dram_tensor("us_s", [ICN, 128, D], BF16, kind="Internal").ap()
    ds_s = nc.dram_tensor("ds_s", [HCN, 128, II], BF16, kind="Internal").ap()
    wq_s = nc.dram_tensor("wq_s", [NH, 128, D], BF16, kind="Internal").ap()
    wk_s = nc.dram_tensor("wk_s", [NKV, 128, D], BF16, kind="Internal").ap()
    wv_s = nc.dram_tensor("wv_s", [DC, 128, 512], BF16, kind="Internal").ap()
    wo_s = nc.dram_tensor("wo_s", [DC, 128, D], BF16, kind="Internal").ap()
    xs_s = nc.dram_tensor("xs_s", [n_seq, D, NTOK], BF16, kind="Internal").ap()

    A = nc.alloc_sbuf_tensor
    identf = A("identf", [128, 128], F32)
    onesbf = A("onesbf", [128, 1], BF16)
    ones1f = A("ones1f", [1, 128], F32)
    ones1bf = A("ones1bf", [1, 128], BF16)
    qs = A("qs_s", [128, 1], F32)
    ks = A("ks_s", [128, 1], F32)
    gsc = A("gsc", [128, ICN], F32)
    usc = A("usc", [128, ICN], F32)
    dsc = A("dsc", [128, HCN], F32)
    mka = A("mka_s", [128, 4 * 512], BF16)
    mkb = A("mkb_s", [128, 4 * 512], BF16)
    khat_t = A("khat", [128, NKV * NTOK], BF16)
    vsb_t = A("vsb", [128, (NTOK // 128) * 512], BF16)
    arena_t = A("arena", [128, ICN * 512], BF16)
    khat = khat_t.ap()
    vsb = vsb_t.ap()
    arena = arena_t.ap()
    x0bf = arena[:, 0:DC * 512]
    qhat = arena[:, DC * 512:2 * DC * 512]
    ocat = arena[:, 2 * DC * 512:3 * DC * 512]
    r1v = arena[:, 3 * DC * 512:4 * DC * 512]
    hact = arena
    x1bf = A("x1bf", [128, DC * 512], BF16).ap()
    r2v = A("r2v", [128, DC * 512], BF16).ap()
    sqb = A("sqb", [128, 2, 512], BF16).ap()
    wqb = A("wqb", [128, 2, DC * 128], BF16).ap()
    wvb = A("wvb", [128, 2, 512], BF16).ap()
    wgb = A("wgb", [128, 2, HCN * 128], BF16).ap()
    wub = A("wub", [128, 2, HCN * 128], BF16).ap()
    wdb = A("wdb", [128, 2, 16 * 128], BF16).ap()
    scrA = A("scrA", [128, 512], F32).ap()
    scrB = A("scrB", [128, 2, 1024], F32).ap()
    dump = A("dump", [128, 1024], BF16).ap()
    ptb = A("ptb", [128, 2, 1024], BF16).ap()
    stt_t = A("stt", [128, 2, 8], F32).ap()
    sttl_t = A("sttl", [128, 2, 8], F32).ap()
    sttw_t = A("sttw", [128, 2, 4], F32).ap()
    sttlf_t = A("sttlf", [128, 2, 4], F32).ap()
    stt2_t = A("stt2", [128, 2, 4], F32).ap()
    st4 = A("st4", [4, 128], F32).ap()
    strow = A("strow", [1, 512], F32).ap()
    rows = A("rows", [1, 2, 512], F32).ap()
    rstdr = A("rstdr", [1, 2, 512], BF16).ap()
    x0res = A("x0res", [128, 4, 512], F32).ap()
    constt = A("constt", [128, 8], F32).ap()
    if os.environ.get("KSBUF"):
        print("sbuf remaining bytes/partition:", nc.sbuf_bytes_remaining)
    assert nc.sbuf_bytes_remaining >= 0, nc.sbuf_bytes_remaining

    PPAIR = [nc.alloc_psum_tensor(f"PP{i}", [128, 1024], F32) for i in range(4)]
    PAw, PBw, PCw, PDw = [p.ap() for p in PPAIR]
    P23w = PBw
    Pap = [PAw[:, 0:512], PAw[:, 512:1024], PBw[:, 0:512], PBw[:, 512:1024],
           PCw[:, 0:512], PCw[:, 512:1024], PDw[:, 0:512], PDw[:, 512:1024]]

    pr = Prog(nc, paranoid=paranoid)
    sems = {}
    for name in ("s_const", "s_stat", "g_x0", "e_pe", "e_dve", "e_act", "e_pool"):
        sems[name] = SemC(nc, name)
    for base in ("wq", "wv", "g", "u", "d", "out", "wbg", "wbu", "wbd",
                 "wqg", "wvg", "gg", "ug", "dg"):
        for j in range(2):
            sems[base + str(j)] = SemC(nc, base + str(j))
    for j in range(4):
        sems["res" + str(j)] = SemC(nc, "res" + str(j))
    sems["g_cv"] = SemC(nc, "g_cv")
    for j in range(4):
        sems["x0q" + str(j)] = SemC(nc, "x0q" + str(j))
    sems["gx0q"] = SemC(nc, "gx0q")
    sems["s_x0"] = SemC(nc, "s_x0")
    S = type("S", (), sems)
    RS = {b: (sems[b + "0"], sems[b + "1"]) for b in
          ("wq", "wv", "g", "u", "d", "out", "wbg", "wbu", "wbd", "wqg", "wvg",
           "gg", "ug", "dg")}
    RS["res"] = tuple(sems["res" + str(j)] for j in range(4))

    bufs = {
        "wqb": [Buf(), Buf()], "wvb": [Buf(), Buf()], "wgb": [Buf(), Buf()],
        "wub": [Buf(), Buf()], "wdb": [Buf(), Buf()], "x0res": [Buf(), Buf(), Buf(), Buf()],
        "sqb": [Buf(), Buf()], "scrB": [Buf(), Buf()],
        "ptb": [Buf(), Buf()],
        "arena": Buf(), "strow": Buf(), "scrA": Buf(), "st4": Buf(),
        "stt2": [Buf(), Buf()],
        "P": [Buf() for _ in range(8)],
    }

    def dve(fn, *a, waits=(), **k):
        pr.rec("vector", fn, *a, waits=waits, inc=(S.e_dve, 1), **k)
        return S.e_dve.n

    def act(fn, *a, waits=(), **k):
        pr.rec("scalar", fn, *a, waits=waits, inc=(S.e_act, 1), **k)
        return S.e_act.n

    def pool(fn, *a, waits=(), **k):
        pr.rec("gpsimd", fn, *a, waits=waits, inc=(S.e_pool, 1), **k)
        return S.e_pool.n

    def pe(fn, *a, waits=(), mark=True, **k):
        pr.rec("tensor", fn, *a, waits=waits, inc=(S.e_pe, 1) if mark else None, **k)
        return S.e_pe.n if mark else None

    def dma(queue, out, in_, sem, waits=()):
        pr.rec(queue, "dma_start", waits=waits, inc=(sem, 16), out=out, in_=in_)
        return sem.n

    for dst, src in ((qs, qs_d), (ks, ks_d), (gsc, gs_d), (usc, us_d), (dsc, ds_d),
                     (mka, mka_d), (mkb, mkb_d), (identf, idn_d)):
        dma("sync", dst.ap()[:], src, S.s_const)
    CONSTW = (S.s_const, S.s_const.n)
    dve("memset", onesbf.ap()[:], 1.0)
    dve("memset", ones1f.ap()[:], 1.0)
    dve("memset", ones1bf.ap()[:], 1.0)
    cvals = [EPS_QK, EPS_LN, -0.5 * float(np.log(128.0)), 1.0 / 128.0, -0.5, -1.0]
    for j, v in enumerate(cvals):
        dve("memset", constt[:, j:j + 1], v)
        nc.const_aps.register(F32, v, constt[:, j:j + 1])
    MEMSET = S.e_dve.n

    X1_DONE = [0]
    x0_info = [None]
    wq_i = [0]
    wv_i = [0]
    x0_i = [0]
    PW_TOTAL = 2 * ICN + HCN * 4

    CV_DONE = [0]

    def load_wq(kind, idx, first_tile):
        slot = wq_i[0] % 2
        wq_i[0] += 1
        w = bufs["wqb"][slot].take()
        if first_tile:
            srcap = {"k": wk_t, "q": wq_t, "o": wo_t}[kind][idx]
            cnt = dma("gpsimd", wqb[:, slot, :], srcap, RS["wqg"][slot], waits=w)
            return slot, (RS["wqg"][slot], cnt)
        else:
            srcap = {"k": wk_s, "q": wq_s, "o": wo_s}[kind][idx]
            w = w + [(S.g_cv, CV_DONE[0])]
            cnt = dma("sync", wqb[:, slot, :], srcap, RS["wq"][slot], waits=w)
        return slot, (RS["wq"][slot], cnt)

    def load_wv(dc, first_tile):
        slot = wv_i[0] % 2
        wv_i[0] += 1
        w = bufs["wvb"][slot].take()
        if first_tile:
            cnt = dma("gpsimd", wvb[:, slot, :], wv_t[dc], RS["wvg"][slot], waits=w)
            return slot, (RS["wvg"][slot], cnt)
        else:
            w = w + [(S.g_cv, CV_DONE[0])]
            cnt = dma("sync", wvb[:, slot, :], wv_s[dc], RS["wv"][slot], waits=w)
        return slot, (RS["wv"][slot], cnt)

    def emit_cast_x0(s, t, first_tile=False):
        t0 = t * TT
        w = bufs["arena"].take()
        x3d = x0bf.rearrange("p (dc t) -> p dc t", dc=DC)
        if first_tile:
            src3 = xT[s].rearrange("(dc p) t -> p dc t", p=128)[:, :, t0:t0 + TT]
            cnt = dma("gpsimd", x3d, src3, sems["gx0q"], waits=w)
            x0_info[0] = [(sems["gx0q"], cnt)] * 4
        else:
            src3 = xs_s[s].rearrange("(dc p) t -> p dc t", p=128)[:, :, t0:t0 + TT]
            w = w + [(S.g_cv, CV_DONE[0])]
            info = []
            for q4 in range(4):
                sq = sems["x0q" + str(q4)]
                cnt = dma("sync", x3d[:, q4 * 4:(q4 + 1) * 4, :],
                          src3[:, q4 * 4:(q4 + 1) * 4, :], sq, waits=w)
                w = []
                info.append((sq, cnt))
            x0_info[0] = info
        x0_i[0] += 1

    def emit_phasew():
        for oc in range(NKV):
            dma("gpsimd", wk_s[oc], wk_t[oc], S.g_cv)
        for oc in range(NH):
            dma("gpsimd", wq_s[oc], wq_t[oc], S.g_cv)
        for mc in range(DC):
            dma("gpsimd", wo_s[mc], wo_t[mc], S.g_cv)
        for dc in range(DC):
            dma("gpsimd", wv_s[dc], wv_t[dc], S.g_cv)
        for si in range(n_seq):
            for dc in range(DC):
                dma("gpsimd", xs_s[si][dc * 128:(dc + 1) * 128, :],
                    xT[si][dc * 128:(dc + 1) * 128, :], S.g_cv)
        CV_DONE[0] = S.g_cv.n

    def qknorm(bi, out_slice, scale_vec, is_q, rst_slot, mm_n):
        c_copy = dve("tensor_copy", out=scrA, in_=Pap[bi],
                     waits=[(S.e_pe, mm_n)] + bufs["scrA"].take())
        bufs["P"][bi].consumed(S.e_dve, c_copy)
        c_sq = dve("tensor_mul", out=sqb[:, 0, :], in0=scrA, in1=scrA,
                   waits=bufs["sqb"][0].take())
        n = pe("matmul", out=Pap[6][0:1, :], lhsT=onesbf.ap()[:], rhs=sqb[:, 0, :],
               start=True, stop=True, waits=[(S.e_dve, c_sq)] + bufs["P"][6].take())
        bufs["sqb"][0].consumed(S.e_pe, n)
        c1 = act("activation", out=rows[0:1, 1, :], in_=Pap[6][0:1, :], func=AF.Ln,
                 scale=1.0 / 128.0, bias=EPS_QK, waits=[(S.e_pe, n)])
        bufs["P"][6].consumed(S.e_act, c1)
        bias2 = -0.5 * float(np.log(128.0)) if is_q else 0.0
        c2 = act("activation", out=rstdr[0:1, rst_slot, :], in_=rows[0:1, 1, :],
                 func=AF.Exp, scale=-0.5, bias=bias2)
        n2 = pe("matmul", out=Pap[7], lhsT=ones1bf.ap()[:],
                rhs=rstdr[0:1, rst_slot, :], start=True, stop=True,
                waits=[(S.e_act, c2)] + bufs["P"][7].take())
        c3 = dve("scalar_tensor_tensor", out=out_slice, in0=scrA, scalar=scale_vec,
                 in1=Pap[7], op0=ALU.mult, op1=ALU.mult,
                 waits=[(S.e_pe, n2), CONSTW])
        bufs["P"][7].consumed(S.e_dve, c3)
        bufs["scrA"].consumed(S.e_dve, c3)
        return c3

    def ln_apply(bi_sum, bi_sq, rsrc, dst_bf, out_spec, pe_n):
        c1 = dve("tensor_scalar_mul", out=rows[0:1, 0, :], in0=Pap[bi_sum][0:1, :],
                 scalar1=1.0 / 2048.0, waits=[(S.e_pe, pe_n)])
        c2 = dve("tensor_scalar_mul", out=rows[0:1, 1, :], in0=Pap[bi_sq][0:1, :],
                 scalar1=1.0 / 2048.0)
        c3 = dve("scalar_tensor_tensor", out=Pap[bi_sum][0:1, :], in0=rows[0:1, 0, :],
                 scalar=1.0, in1=rows[0:1, 0, :], op0=ALU.mult, op1=ALU.mult)
        c4 = dve("tensor_tensor", out=rows[0:1, 1, :], in0=rows[0:1, 1, :],
                 in1=Pap[bi_sum][0:1, :], op=ALU.subtract)
        c5 = act("activation", out=rows[0:1, 1, :], in_=rows[0:1, 1, :], func=AF.Ln,
                 bias=EPS_LN, waits=[(S.e_dve, c4)])
        c6 = act("activation", out=rstdr[0:1, 0, :], in_=rows[0:1, 1, :], func=AF.Exp,
                 scale=-0.5)
        n1 = pe("matmul", out=Pap[bi_sum], lhsT=ones1f.ap()[:], rhs=rows[0:1, 0, :],
                start=True, stop=True, waits=[(S.e_dve, c4)] + bufs["P"][bi_sum].take())
        n2 = pe("matmul", out=Pap[bi_sq], lhsT=ones1bf.ap()[:], rhs=rstdr[0:1, 0, :],
                start=True, stop=True, waits=[(S.e_act, c6)] + bufs["P"][bi_sq].take())
        for mc in range(DC):
            wts = ([(S.e_pe, max(n1, n2))] if mc == 0 else []) + bufs["scrA"].take()
            dve("tensor_tensor", out=scrA, in0=rsrc[:, mc * 512:(mc + 1) * 512],
                in1=Pap[bi_sum], op=ALU.subtract, waits=wts)
            if dst_bf is not None:
                cx = dve("tensor_tensor", out=dst_bf[:, mc * 512:(mc + 1) * 512],
                         in0=scrA, in1=Pap[bi_sq], op=ALU.mult)
            else:
                (so, t0) = out_spec
                oslot = mc % 2
                cx = dve("tensor_tensor", out=scrB[:, oslot, 0:512], in0=scrA,
                         in1=Pap[bi_sq], op=ALU.mult, waits=bufs["scrB"][oslot].take())
                nout = dma("sync", outT[so][mc * 128:(mc + 1) * 128, t0:t0 + TT],
                           scrB[:, oslot, 0:512], RS["out"][oslot], waits=[(S.e_dve, cx)])
                bufs["scrB"][oslot].consumed(RS["out"][oslot], nout)
            bufs["scrA"].consumed(S.e_dve, cx)
        bufs["P"][bi_sum].consumed(S.e_dve, S.e_dve.n)
        bufs["P"][bi_sq].consumed(S.e_dve, S.e_dve.n)
        if dst_bf is not None:
            X1_DONE[0] = S.e_dve.n

    def emit_tile(s, t, nxt, post_attn_hook=None):
        t0 = t * TT

        pr.phase = "qkv"
        first_tile = (s == 0 and t == 0)
        XWALL = list(x0_info[0])
        for dc in range(DC):
            slot, wvl = load_wv(dc, first_tile)
            for sc in range(4):
                wts = []
                if dc == 0:
                    wts += bufs["P"][sc].take()
                if sc == 0:
                    wts += [wvl, XWALL[dc // 4]]
                pe("matmul", out=Pap[sc],
                   lhsT=x0bf[:, dc * 512 + sc * 128:dc * 512 + (sc + 1) * 128],
                   rhs=wvb[:, slot, :], start=(dc == 0), stop=(dc == DC - 1),
                   waits=wts, mark=(sc == 3))
            bufs["wvb"][slot].consumed(S.e_pe, S.e_pe.n)
        for sc in range(4):
            scg = (t0 // 128) + sc
            c = dve("tensor_copy", out=vsb[:, scg * 512:(scg + 1) * 512],
                    in_=Pap[sc], waits=[(S.e_pe, S.e_pe.n)])
            bufs["P"][sc].consumed(S.e_dve, c)
        V_DONE = S.e_dve.n
        qk_pend = []
        for oc in range(NKV):
            slot, wql = load_wq("k", oc, first_tile)
            bi = oc % 2
            for dc in range(DC):
                wts = ()
                if dc == 0:
                    wts = [wql, CONSTW,
                           (S.e_dve, MEMSET)] + XWALL + bufs["P"][bi].take()
                pe("matmul", out=Pap[bi], lhsT=wqb[:, slot, dc * 128:(dc + 1) * 128],
                   rhs=x0bf[:, dc * 512:dc * 512 + TT], start=(dc == 0),
                   stop=(dc == DC - 1), waits=wts, mark=(dc == DC - 1))
            bufs["wqb"][slot].consumed(S.e_pe, S.e_pe.n)
            if qk_pend:
                qknorm(*qk_pend.pop(0))
            qk_pend.append((bi, khat[:, oc * NTOK + t0: oc * NTOK + t0 + TT],
                            ks.ap()[:], False, oc % 2, S.e_pe.n))
        qknorm(*qk_pend.pop(0))
        KHAT_DONE = S.e_dve.n

        for oc in range(NH):
            slot, wql = load_wq("q", oc, first_tile)
            bi = oc % 2
            for dc in range(DC):
                wts = ()
                if dc == 0:
                    wts = [wql] + bufs["P"][bi].take()
                pe("matmul", out=Pap[bi], lhsT=wqb[:, slot, dc * 128:(dc + 1) * 128],
                   rhs=x0bf[:, dc * 512:dc * 512 + TT], start=(dc == 0),
                   stop=(dc == DC - 1), waits=wts, mark=(dc == DC - 1))
            bufs["wqb"][slot].consumed(S.e_pe, S.e_pe.n)
            if qk_pend:
                qknorm(*qk_pend.pop(0))
            qk_pend.append((bi, qhat[:, oc * 512:(oc + 1) * 512], qs.ap()[:],
                            True, oc % 2, S.e_pe.n))
        qknorm(*qk_pend.pop(0))
        QHAT_DONE = S.e_dve.n

        def take2(i):
            return bufs["P"][i].take() + bufs["P"][i + 1].take()

        def cons2(i, sem, cnt):
            bufs["P"][i].consumed(sem, cnt)
            bufs["P"][i + 1].consumed(sem, cnt)

        arot = [0]

        def build_A(h):
            par = h % 2
            stt = stt_t[:, par, :]
            sttl = sttl_t[:, par, :]
            sttw = sttw_t[:, par, :]
            sttlf = sttlf_t[:, par, :]
            stt2 = stt2_t[:, par, :]
            kv = h // 4
            units = []
            for qc in range(4):
                plan = []
                sti = 0
                while sti + 1 < t:
                    plan.append(("pair", sti))
                    sti += 2
                if sti < t:
                    plan.append(("one", sti))
                plan.append(("diag", t))
                nu = len(plan)

                def unit(kind, sti, u, qc=qc):
                    def f():
                        pr.phase = "attnA"
                        base = 0 if (arot[0] % 2 == 0) else 4
                        arot[0] += 1
                        wb = PAw if base == 0 else PCw
                        wts = take2(base) + [(S.e_dve, max(KHAT_DONE, QHAT_DONE))]
                        if kind == "pair":
                            pe("matmul", out=wb[:, 0:512],
                               lhsT=qhat[:, h * 512 + qc * 128:h * 512 + (qc + 1) * 128],
                               rhs=khat[:, kv * NTOK + sti * 512:kv * NTOK + (sti + 1) * 512],
                               start=True, stop=True, waits=wts)
                            n = pe("matmul", out=wb[:, 512:1024],
                                   lhsT=qhat[:, h * 512 + qc * 128:h * 512 + (qc + 1) * 128],
                                   rhs=khat[:, kv * NTOK + (sti + 1) * 512:kv * NTOK + (sti + 2) * 512],
                                   start=True, stop=True)
                            na = 1024
                        else:
                            na = 512 if kind == "one" else (qc + 1) * 128
                            n = pe("matmul", out=wb[:, 0:na],
                                   lhsT=qhat[:, h * 512 + qc * 128:h * 512 + (qc + 1) * 128],
                                   rhs=khat[:, kv * NTOK + sti * 512:kv * NTOK + sti * 512 + na],
                                   start=True, stop=True, waits=wts)
                        if kind == "diag":
                            cmask = dve("tensor_tensor", out=wb[:, 0:na], in0=wb[:, 0:na],
                                        in1=mka.ap()[:, qc * 512:qc * 512 + na], op=ALU.add,
                                        waits=[(S.e_pe, n), CONSTW])
                            wmax = [(S.e_dve, cmask)]
                        else:
                            wmax = [(S.e_pe, n)]
                        if u == 0:
                            wmax = wmax + [(S.e_act, S.e_act.n)]
                        cm = dve("tensor_reduce", out=stt[:, u:u + 1], in_=wb[:, 0:na],
                                 axis=AX.X, op=ALU.max, negate=True, waits=wmax)
                        ce = act("activation", out=dump[:, 0:na], in_=wb[:, 0:na],
                                 func=AF.Exp, bias=stt[:, u:u + 1],
                                 accum_out=sttl[:, u:u + 1],
                                 waits=[(S.e_dve, cm)])
                        cons2(base, S.e_act, ce)
                    return f
                for u, (kind, sti) in enumerate(plan):
                    units.append(unit(kind, sti, u))

                def combine(nu=nu, qc=qc):
                    def f():
                        pr.phase = "attnC"
                        w2 = bufs["stt2"][par].take() if qc == 0 else []
                        act("copy", out=sttlf[:, 0:nu], in_=sttl[:, 0:nu])
                        if nu == 1:
                            cl = act("activation", out=sttl[:, 5:6], in_=sttl[:, 0:1], func=AF.Ln)
                            dve("tensor_sub", out=stt2[:, qc:qc + 1], in0=sttl[:, 5:6],
                                in1=stt[:, 0:1], waits=[(S.e_act, cl)] + w2)
                        else:
                            c1 = dve("tensor_reduce", out=stt[:, 4:5], in_=stt[:, 0:nu],
                                     axis=AX.X, op=ALU.min,
                                     waits=[(S.e_act, S.e_act.n), (S.e_dve, S.e_dve.n)])
                            c2 = act("activation", out=sttw[:, 0:nu], in_=stt[:, 0:nu],
                                     func=AF.Exp, scale=-1.0, bias=stt[:, 4:5],
                                     waits=[(S.e_dve, c1)])
                            c3 = dve("tensor_mul", out=sttw[:, 0:nu], in0=sttw[:, 0:nu],
                                     in1=sttlf[:, 0:nu], waits=[(S.e_act, c2)])
                            dve("tensor_reduce", out=sttl[:, 4:5], in_=sttw[:, 0:nu],
                                axis=AX.X, op=ALU.add, waits=[(S.e_dve, S.e_dve.n)])
                            cl = act("activation", out=sttl[:, 5:6], in_=sttl[:, 4:5], func=AF.Ln,
                                     waits=[(S.e_dve, S.e_dve.n)])
                            dve("tensor_sub", out=stt2[:, qc:qc + 1], in0=sttl[:, 5:6],
                                in1=stt[:, 4:5],
                                waits=[(S.e_act, cl)] + w2 + [(S.e_dve, S.e_dve.n)])
                    return f
                units.append(combine())
            return units

        def build_B(h, STT2):
            par = h % 2
            stt2 = stt2_t[:, par, :]
            kv = h // 4
            units = []

            def stats():
                pr.phase = "attnB"
                n = pe("transpose", out=Pap[6][0:4, 0:128], in_=stt2[:, 0:4],
                       identity=identf.ap()[:],
                       waits=[(S.e_dve, STT2), CONSTW] + bufs["P"][6].take())
                bufs["stt2"][par].consumed(S.e_pe, n)
                c4 = dve("tensor_copy", out=st4, in_=Pap[6][0:4, 0:128],
                         waits=[(S.e_pe, n)] + bufs["st4"].take())
                bufs["P"][6].consumed(S.e_dve, c4)
                nrow = dma("sync", strow, st4, S.s_stat,
                           waits=bufs["strow"].take() + [(S.e_dve, c4)])
                bufs["st4"].consumed(S.s_stat, nrow)
                n = pe("matmul", out=Pap[7], lhsT=ones1f.ap()[:], rhs=strow,
                       start=True, stop=True,
                       waits=[(S.s_stat, nrow), (S.e_dve, MEMSET)] + bufs["P"][7].take())
                bufs["strow"].consumed(S.e_pe, n)
                c5 = dve("tensor_copy", out=scrA, in_=Pap[7],
                         waits=bufs["scrA"].take() + [(S.e_pe, n)])
                bufs["P"][7].consumed(S.e_dve, c5)
            units.append(stats)

            n_sc = 4 * (t + 1)
            plan = []
            sc = 0
            while sc < n_sc:
                if sc + 1 < 4 * t:
                    plan.append(("pair", sc)); sc += 2
                else:
                    plan.append(("one", sc)); sc += 1

            def bunit_mm(kind, sc, it):
                cell = {}

                def f():
                    pr.phase = "attnB"
                    sslot = it % 2
                    if kind == "pair":
                        w0 = take2(2)
                        pe("matmul", out=Pap[2],
                           lhsT=khat[:, kv * NTOK + sc * 128:kv * NTOK + (sc + 1) * 128],
                           rhs=qhat[:, h * 512:(h + 1) * 512],
                           start=True, stop=True, waits=w0)
                        n = pe("matmul", out=Pap[3],
                               lhsT=khat[:, kv * NTOK + (sc + 1) * 128:kv * NTOK + (sc + 2) * 128],
                               rhs=qhat[:, h * 512:(h + 1) * 512],
                               start=True, stop=True)
                        wts = bufs["scrB"][sslot].take() + [(S.e_pe, n)]
                        csub = dve("tensor_tensor",
                                   out=scrB[:, sslot, :].rearrange("p (a b) -> p a b", a=2),
                                   in0=PBw.rearrange("p (a b) -> p a b", a=2),
                                   in1=scrA.unsqueeze(1).broadcast_to([128, 2, 512]),
                                   op=ALU.subtract, waits=wts)
                        bufs["scrA"].consumed(S.e_dve, csub)
                        cons2(2, S.e_dve, csub)
                        cexp = act("activation", out=ptb[:, sslot, :], in_=scrB[:, sslot, :],
                                   func=AF.Exp,
                                   waits=bufs["ptb"][sslot].take() + [(S.e_dve, csub)])
                        bufs["scrB"][sslot].consumed(S.e_act, cexp)
                    else:
                        qlo = 0 if sc < 4 * t else (sc - 4 * t) * 128
                        bi = 2 + (it % 2)
                        n = pe("matmul", out=Pap[bi][:, qlo:512],
                               lhsT=khat[:, kv * NTOK + sc * 128:kv * NTOK + (sc + 1) * 128],
                               rhs=qhat[:, h * 512 + qlo:(h + 1) * 512],
                               start=True, stop=True, waits=bufs["P"][bi].take())
                        wts = bufs["scrB"][sslot].take() + [(S.e_pe, n)]
                        csub = dve("tensor_tensor", out=scrB[:, sslot, qlo:512],
                                   in0=Pap[bi][:, qlo:512],
                                   in1=scrA[:, qlo:512], op=ALU.subtract, waits=wts)
                        bufs["scrA"].consumed(S.e_dve, csub)
                        bufs["P"][bi].consumed(S.e_dve, csub)
                        if sc >= 4 * t:
                            csub = dve("tensor_tensor", out=scrB[:, sslot, qlo:512],
                                       in0=scrB[:, sslot, qlo:512],
                                       in1=mkb.ap()[:, (sc - 4 * t) * 512 + qlo:(sc - 4 * t + 1) * 512],
                                       op=ALU.add, waits=[CONSTW])
                        cexp = act("activation", out=ptb[:, sslot, qlo:512],
                                   in_=scrB[:, sslot, qlo:512], func=AF.Exp,
                                   waits=bufs["ptb"][sslot].take() + [(S.e_dve, csub)])
                        bufs["scrB"][sslot].consumed(S.e_act, cexp)
                    cell["exp"] = cexp
                return f, cell

            def bunit_av(kind, sc, it, last, cell):
                def f():
                    pr.phase = "attnB"
                    sslot = it % 2
                    cexp = cell["exp"]
                    if kind == "pair":
                        wts = [(S.e_act, cexp), (S.e_dve, V_DONE)]
                        if sc == 0:
                            wts += bufs["P"][6].take()
                        pe("matmul", out=Pap[6],
                           lhsT=vsb[:, sc * 512 + kv * 128:sc * 512 + (kv + 1) * 128],
                           rhs=ptb[:, sslot, 0:512], start=(sc == 0), stop=False,
                           waits=wts)
                        n = pe("matmul", out=Pap[6],
                               lhsT=vsb[:, (sc + 1) * 512 + kv * 128:(sc + 1) * 512 + (kv + 1) * 128],
                               rhs=ptb[:, sslot, 512:1024], start=False, stop=last)
                        bufs["ptb"][sslot].consumed(S.e_pe, n)
                    else:
                        qlo = 0 if sc < 4 * t else (sc - 4 * t) * 128
                        wts = [(S.e_act, cexp), (S.e_dve, V_DONE)]
                        if sc == 0:
                            wts += bufs["P"][6].take()
                        n = pe("matmul", out=Pap[6][:, qlo:512],
                               lhsT=vsb[:, sc * 512 + kv * 128:sc * 512 + (kv + 1) * 128],
                               rhs=ptb[:, sslot, qlo:512], start=(sc == 0), stop=last,
                               waits=wts)
                        bufs["ptb"][sslot].consumed(S.e_pe, n)
                return f
            for it, (kind, sc) in enumerate(plan):
                fmm, cell = bunit_mm(kind, sc, it)
                units.append(fmm)
                units.append(bunit_av(kind, sc, it, it == len(plan) - 1, cell))

            def evac():
                co = act("copy", out=ocat[:, h * 512:(h + 1) * 512], in_=Pap[6],
                         waits=[(S.e_pe, S.e_pe.n)])
                bufs["P"][6].consumed(S.e_act, co)
            units.append(evac)
            return units

        def zip_emit(A, Bu):
            na, nb = len(A), len(Bu)
            ia = ib = 0
            while ia < na or ib < nb:
                if ib < nb and (ia >= na or ib * na < ia * nb):
                    Bu[ib](); ib += 1
                else:
                    A[ia](); ia += 1

        pend = None
        for h in range(NH):
            A = build_A(h)
            if pend is None:
                for f in A:
                    f()
            else:
                zip_emit(A, build_B(*pend))
            pend = (h, S.e_dve.n)
        for f in build_B(*pend):
            f()
        OCAT_DONE = S.e_act.n
        pr.phase = "wo"

        def res_load(mc):
            rslot = mc % 4
            nres = dma("sync", x0res[:, rslot, :],
                       xT[s][mc * 128:(mc + 1) * 128, t0:t0 + TT], RS["res"][rslot],
                       waits=bufs["x0res"][rslot].take())
            return rslot, nres
        res_info = {mc: res_load(mc) for mc in range(2)}
        ones_q = []

        def ones_mm(mc):
            (csq, qslot) = ones_q.pop(0)
            wts = [(S.e_dve, csq)]
            if mc == 0:
                wts += bufs["P"][4].take() + bufs["P"][5].take()
            pe("matmul", out=Pap[4][0:1, :], lhsT=onesbf.ap()[:],
               rhs=r1v[:, mc * 512:(mc + 1) * 512], start=(mc == 0),
               stop=(mc == DC - 1), waits=wts, mark=(mc == DC - 1))
            nss = pe("matmul", out=Pap[5][0:1, :], lhsT=onesbf.ap()[:],
                     rhs=sqb[:, qslot, :], start=(mc == 0), stop=(mc == DC - 1),
                     mark=True)
            bufs["sqb"][qslot].consumed(S.e_pe, nss)

        for mc in range(DC):
            slot, wql = load_wq("o", mc, first_tile)
            bi = mc % 2
            for cc in range(DC):
                wts = ()
                if cc == 0:
                    wts = [wql,
                           (S.e_act, OCAT_DONE)] + bufs["P"][bi].take()
                pe("matmul", out=Pap[bi], lhsT=wqb[:, slot, cc * 128:(cc + 1) * 128],
                   rhs=ocat[:, cc * 512:(cc + 1) * 512], start=(cc == 0),
                   stop=(cc == DC - 1), waits=wts, mark=(cc == DC - 1))
            MMN = S.e_pe.n
            bufs["wqb"][slot].consumed(S.e_pe, MMN)
            if mc + 2 < DC:
                res_info[mc + 2] = res_load(mc + 2)
            if mc >= 1:
                ones_mm(mc - 1)
            rslot, nres = res_info.pop(mc)
            cr = dve("tensor_tensor", out=r1v[:, mc * 512:(mc + 1) * 512], in0=Pap[bi],
                     in1=x0res[:, rslot, :], op=ALU.add,
                     waits=[(S.e_pe, MMN), (RS["res"][rslot], nres)])
            bufs["P"][bi].consumed(S.e_dve, cr)
            bufs["x0res"][rslot].consumed(S.e_dve, cr)
            qslot = mc % 2
            csq = dve("tensor_mul", out=sqb[:, qslot, :],
                      in0=r1v[:, mc * 512:(mc + 1) * 512],
                      in1=r1v[:, mc * 512:(mc + 1) * 512],
                      waits=bufs["sqb"][qslot].take())
            ones_q.append((csq, qslot))
        ones_mm(DC - 1)
        ln_apply(4, 5, r1v, x1bf, None, S.e_pe.n)
        if post_attn_hook is not None:
            post_attn_hook()

        pr.phase = "ffn"
        first = (s == 0 and t == 0)
        for ic in range(ICN):
            gslot = ic % 2
            if first:
                ng = dma("gpsimd", wgb[:, gslot, :], g_t[ic], RS["gg"][gslot],
                         waits=bufs["wgb"][gslot].take())
                cg = dve("tensor_scalar_add", out=wgb[:, gslot, :], in0=wgb[:, gslot, :],
                         scalar1=-128.0, waits=[(RS["gg"][gslot], ng)])
                nwb = dma("gpsimd", gs_s[ic], wgb[:, gslot, :], RS["wbg"][gslot],
                          waits=[(S.e_dve, cg)])
                bufs["wgb"][gslot].consumed(RS["wbg"][gslot], nwb)
                gw0 = (S.e_dve, cg)
                nu0 = dma("gpsimd", wub[:, gslot, :], u_t[ic], RS["ug"][gslot],
                          waits=bufs["wub"][gslot].take())
                cu = dve("tensor_scalar_add", out=wub[:, gslot, :], in0=wub[:, gslot, :],
                         scalar1=-128.0, waits=[(RS["ug"][gslot], nu0)])
                nwb = dma("gpsimd", us_s[ic], wub[:, gslot, :], RS["wbu"][gslot],
                          waits=[(S.e_dve, cu)])
                bufs["wub"][gslot].consumed(RS["wbu"][gslot], nwb)
                uw0 = (S.e_dve, cu)
            else:
                ng = dma("sync", wgb[:, gslot, :], gs_s[ic], RS["g"][gslot],
                         waits=bufs["wgb"][gslot].take())
                gw0 = (RS["g"][gslot], ng)
                nu = dma("sync", wub[:, gslot, :], us_s[ic], RS["u"][gslot],
                         waits=bufs["wub"][gslot].take())
                uw0 = (RS["u"][gslot], nu)
            bg, bu = gslot, 2 + gslot
            for hc in range(HCN):
                wts = ()
                if hc == 0:
                    wts = [gw0, (S.e_dve, X1_DONE[0])] + bufs["P"][bg].take()
                pe("matmul", out=Pap[bg], lhsT=wgb[:, gslot, hc * 128:(hc + 1) * 128],
                   rhs=x1bf[:, hc * 512:(hc + 1) * 512], start=(hc == 0),
                   stop=(hc == HCN - 1), waits=wts, mark=(hc == HCN - 1))
            bufs["wgb"][gslot].consumed(S.e_pe, S.e_pe.n)
            for hc in range(HCN):
                wts = ()
                if hc == 0:
                    wts = [uw0] + bufs["P"][bu].take()
                pe("matmul", out=Pap[bu], lhsT=wub[:, gslot, hc * 128:(hc + 1) * 128],
                   rhs=x1bf[:, hc * 512:(hc + 1) * 512], start=(hc == 0),
                   stop=(hc == HCN - 1), waits=wts, mark=(hc == HCN - 1))
            NU = S.e_pe.n
            bufs["wub"][gslot].consumed(S.e_pe, NU)
            sslot = ic % 2
            cs = act("activation", out=scrB[:, sslot, 0:512], in_=Pap[bg], func=AF.Silu,
                     scale=gsc.ap()[:, ic:ic + 1],
                     waits=bufs["scrB"][sslot].take() + [(S.e_pe, NU), CONSTW])
            bufs["P"][bg].consumed(S.e_act, cs)
            ch = dve("scalar_tensor_tensor", out=hact[:, ic * 512:(ic + 1) * 512],
                     in0=Pap[bu], scalar=usc.ap()[:, ic:ic + 1], in1=scrB[:, sslot, 0:512],
                     op0=ALU.mult, op1=ALU.mult, waits=[(S.e_act, cs)])
            bufs["P"][bu].consumed(S.e_dve, ch)
            bufs["scrB"][sslot].consumed(S.e_dve, ch)
        HACT_LAST = S.e_dve.n
        ones2_q = []

        def ones2_mm():
            (csq, qslot, hc2) = ones2_q.pop(0)
            wts = [(S.e_dve, csq)]
            if hc2 == 0:
                wts += bufs["P"][6].take() + bufs["P"][7].take()
            pe("matmul", out=Pap[6][0:1, :], lhsT=onesbf.ap()[:],
               rhs=r2v[:, hc2 * 512:(hc2 + 1) * 512], start=(hc2 == 0),
               stop=(hc2 == HCN - 1), waits=wts, mark=(hc2 == HCN - 1))
            pe("matmul", out=Pap[7][0:1, :], lhsT=onesbf.ap()[:], rhs=sqb[:, qslot, :],
               start=(hc2 == 0), stop=(hc2 == HCN - 1), mark=True)
            bufs["sqb"][qslot].consumed(S.e_pe, S.e_pe.n)

        for hc in range(HCN):
            bd = 4 + (hc % 2)
            for qtr in range(4):
                dslot = (hc * 4 + qtr) % 2
                w = bufs["wdb"][dslot].take()
                if first:
                    nd0 = dma("gpsimd", wdb[:, dslot, :],
                              d_t[hc][:, qtr * 2048:(qtr + 1) * 2048], RS["dg"][dslot],
                              waits=w)
                    cd = dve("tensor_scalar_add", out=wdb[:, dslot, :],
                             in0=wdb[:, dslot, :], scalar1=-128.0,
                             waits=[(RS["dg"][dslot], nd0)])
                    nwb = dma("gpsimd", ds_s[hc][:, qtr * 2048:(qtr + 1) * 2048],
                              wdb[:, dslot, :], RS["wbd"][dslot], waits=[(S.e_dve, cd)])
                    bufs["wdb"][dslot].consumed(RS["wbd"][dslot], nwb)
                    dw0 = (S.e_dve, cd)
                else:
                    nd = dma("sync", wdb[:, dslot, :],
                             ds_s[hc][:, qtr * 2048:(qtr + 1) * 2048], RS["d"][dslot],
                             waits=w)
                    dw0 = (RS["d"][dslot], nd)
                for ih in range(16):
                    ic = qtr * 16 + ih
                    wts = []
                    if ic == 0:
                        wts += [(S.e_dve, HACT_LAST)] + bufs["P"][bd].take()
                    if ih == 0:
                        wts += [dw0]
                    pe("matmul", out=Pap[bd], lhsT=wdb[:, dslot, ih * 128:(ih + 1) * 128],
                       rhs=hact[:, ic * 512:(ic + 1) * 512], start=(ic == 0),
                       stop=(ic == ICN - 1), waits=wts, mark=(ih == 15))
                bufs["wdb"][dslot].consumed(S.e_pe, S.e_pe.n)
            MMN = S.e_pe.n
            wts = [(S.e_pe, MMN), CONSTW]
            cr = dve("scalar_tensor_tensor", out=r2v[:, hc * 512:(hc + 1) * 512],
                     in0=Pap[bd], scalar=dsc.ap()[:, hc:hc + 1],
                     in1=x1bf[:, hc * 512:(hc + 1) * 512],
                     op0=ALU.mult, op1=ALU.add, waits=wts)
            bufs["P"][bd].consumed(S.e_dve, cr)
            qslot = hc % 2
            csq = dve("tensor_mul", out=sqb[:, qslot, :],
                      in0=r2v[:, hc * 512:(hc + 1) * 512],
                      in1=r2v[:, hc * 512:(hc + 1) * 512],
                      waits=bufs["sqb"][qslot].take())
            ones2_q.append((csq, qslot, hc))
            if hc >= 1:
                ones2_mm()
        ones2_mm()
        bufs["arena"].consumed(S.e_pe, S.e_pe.n)
        if nxt is not None:
            emit_cast_x0(*nxt)
        ln_apply(6, 7, r2v, None, (s, t0), S.e_pe.n)

    tiles = [(s, t) for s in range(n_seq) for t in range(n_tiles)]
    emit_cast_x0(*tiles[0], first_tile=True)
    for i, (s, t) in enumerate(tiles):
        nxt = tiles[i + 1] if i + 1 < len(tiles) else None
        emit_tile(s, t, nxt, post_attn_hook=emit_phasew if i == 0 else None)
    pr.rec("sync", "wait_ge", RS["out"][0].h, RS["out"][0].n)
    pr.rec("sync", "wait_ge", RS["out"][1].h, RS["out"][1].n)
    pr.replay()
    nc._prog = pr
    return nc



def _prep(inputs):
    f32 = np.float32
    Wq = np.asarray(inputs["L0_attn_Wq"], f32)
    Wk = np.asarray(inputs["L0_attn_Wk"], f32)
    Wv = np.asarray(inputs["L0_attn_Wv"], f32)
    Wo = np.asarray(inputs["L0_attn_Wo"], f32)
    gw = np.asarray(inputs["L0_ffn_gate_w"], np.int32)
    uw = np.asarray(inputs["L0_ffn_up_w"], np.int32)
    dw = np.asarray(inputs["L0_ffn_down_w"], np.int32)

    def wtile(W, n_out):
        r = W.reshape(n_out, 128, DC, 128)
        return np.ascontiguousarray(r.transpose(0, 3, 2, 1)).reshape(n_out, 128, D)

    d = {}
    d["wq_t"] = wtile(Wq, NH)
    d["wk_t"] = wtile(Wk, NKV)
    d["wo_t"] = wtile(Wo, DC)
    d["wv_t"] = np.ascontiguousarray(Wv.T.reshape(DC, 128, 512))
    d["g_t"] = np.ascontiguousarray(
        gw.reshape(ICN, 128, DC, 128).transpose(0, 3, 2, 1)).reshape(ICN, 128, D)
    d["u_t"] = np.ascontiguousarray(
        uw.reshape(ICN, 128, DC, 128).transpose(0, 3, 2, 1)).reshape(ICN, 128, D)
    d["d_t"] = np.ascontiguousarray(
        dw.reshape(HCN, 128, ICN, 128).transpose(0, 3, 2, 1)).reshape(HCN, 128, II)
    d["qs"] = np.ascontiguousarray(np.asarray(inputs["L0_attn_q_scale"], f32).reshape(128, 1))
    d["ks"] = np.ascontiguousarray(np.asarray(inputs["L0_attn_k_scale"], f32).reshape(128, 1))
    d["gs"] = np.ascontiguousarray(np.asarray(inputs["L0_ffn_gate_scale"], f32).reshape(ICN, 128).T)
    d["us"] = np.ascontiguousarray(np.asarray(inputs["L0_ffn_up_scale"], f32).reshape(ICN, 128).T)
    d["ds"] = np.ascontiguousarray(np.asarray(inputs["L0_ffn_down_scale"], f32).reshape(HCN, 128).T)
    i_idx = np.arange(128)[:, None]
    j_idx = np.arange(512)[None, :]
    d["mka"] = np.concatenate([np.where(j_idx <= qc * 128 + i_idx, 0.0, NEG)
                               for qc in range(4)], axis=1).astype(ml_dtypes.bfloat16)
    d["mkb"] = np.concatenate([np.where(p * 128 + i_idx <= j_idx, 0.0, NEG)
                               for p in range(4)], axis=1).astype(ml_dtypes.bfloat16)
    d["idn"] = np.eye(128, dtype=f32)
    return d


def kernel(**inputs):
    x = np.asarray(inputs["L0_attn_x"], np.float32)
    shared = _prep(inputs)
    nc = build_core_program()
    in_maps = []
    for c in range(NCORE):
        m = dict(shared)
        m["xT"] = np.ascontiguousarray(x[2 * c:2 * c + 2].transpose(0, 2, 1))
        in_maps.append(m)
    res = run_bass_kernel_spmd(nc, in_maps, core_ids=list(range(NCORE)))
    out = np.empty((B, T, D), np.float32)
    for c in range(NCORE):
        o = np.asarray(res.results[c]["outT"], np.float32)
        out[2 * c:2 * c + 2] = o.transpose(0, 2, 1)
    return out
